# version 02
# baseline (speedup 1.0000x reference)
import contextlib
import numpy as np
import concourse.bass as bass
import concourse.mybir as mybir
from concourse.bass_utils import run_bass_kernel_spmd

F32 = mybir.dt.float32
BF16 = mybir.dt.bfloat16
FP8 = mybir.dt.float8e4
U16 = mybir.dt.uint16
AF = mybir.ActivationFunctionType
OP = mybir.AluOpType
AX = mybir.AxisListType.X

B, S, D = 2, 4096, 2048
NCORE = 8
TOK = 1024
NTT = 8
EPS = 1e-5
INV_D = float(1.0 / 2048.0)
GC1 = 0.0446777344
GC2 = 0.796875
SWAP32 = [i ^ 1 for i in range(32)]

ENGS = ("sync", "vector", "scalar", "tensor", "gpsimd")


class Sem:
    __slots__ = ("name", "count", "h")

    def __init__(self, name):
        self.name, self.count, self.h = name, 0, None


class Buf:
    __slots__ = ("name", "w", "readers")

    def __init__(self, name):
        self.name, self.w, self.readers = name, None, []


class Sched:
    def __init__(self):
        self.q = {e: [] for e in ENGS}
        self.esem = {e: Sem("E_" + e) for e in ENGS}
        self.dsem = {}
        self.xsem = []

    def all_sems(self):
        return list(self.esem.values()) + list(self.dsem.values()) + self.xsem

    def dma_sem(self, name):
        if name not in self.dsem:
            self.dsem[name] = Sem("D_" + name)
        return self.dsem[name]

    def new_sem(self, name):
        s = Sem(name)
        self.xsem.append(s)
        return s

    def emit(self, eng, build, reads=(), writes=(), dma=None, sem=None, waits=()):
        wmap = {}

        def addw(tok):
            if tok is None:
                return
            s, v = tok
            if wmap.get(s, -1) < v:
                wmap[s] = v

        for b in reads:
            addw(b.w)
        for b in writes:
            addw(b.w)
            for t in b.readers:
                addw(t)
        for t in waits:
            addw(t)
        if dma is not None:
            if dma == "auto":
                assert writes, "auto dma sem needs a write buffer"
                dma = "b_" + writes[0].name
            s = self.dma_sem(dma)
            s.count += 16
            tok = (s, s.count)
            amt = 16
        elif sem is not None:
            s = sem
            s.count += 1
            tok = (s, s.count)
            amt = 1
        else:
            s = self.esem[eng]
            s.count += 1
            tok = (s, s.count)
            amt = 1
        self.q[eng].append((build, tuple(wmap.items()), (s, amt)))
        for b in reads:
            b.readers.append(tok)
        for b in writes:
            b.w = tok
            b.readers = []
        return tok

    def wait_only(self, eng, toks):
        self.q[eng].append((None, tuple((s, v) for s, v in toks), None))

    def run(self, eng_name, engobj):
        waited = {}
        for build, waits, inc in self.q[eng_name]:
            for s, v in waits:
                if waited.get(s, 0) < v:
                    engobj.wait_ge(s.h, v)
                    waited[s] = v
            if build is None:
                continue
            ins = build(engobj)
            s, amt = inc
            ins.then_inc(s.h, amt)
            if waited.get(s, 0) < s.count:
                pass


def _rotary_tables(core):
    rot = D // 2
    inv_freq = (1.0 / (np.float32(10000.0) **
                       (np.arange(0, rot, 2, dtype=np.float32) / np.float32(rot)))).astype(np.float32)
    p0 = (core % 4) * TOK
    pos = np.arange(p0, p0 + TOK, dtype=np.float32)
    fr = (pos[:, None] * inv_freq[None, :]).astype(np.float32)
    fr = np.repeat(fr, 2, axis=1)
    cos = np.cos(fr).astype(np.float32)
    sin = np.sin(fr).astype(np.float32)
    sinS = sin.copy()
    sinS[:, 0::2] = -sin[:, 0::2]
    cos_fm = np.ascontiguousarray(cos.T.reshape(8, 128, TOK))
    sin_fm = np.ascontiguousarray(sinS.T.reshape(8, 128, TOK))
    return cos_fm, sin_fm


def build_nc():
    nc = bass.Bass("TRN2", target_bir_lowering=False, debug=False)

    def din(name, shape):
        return nc.dram_tensor(name, list(shape), F32, kind="ExternalInput").ap()

    x = din("x", (TOK, D))
    y = nc.dram_tensor("y", [TOK, D], F32, kind="ExternalOutput").ap()
    w_in_st = din("w_in_st", (24, 16, 128, 256))
    wg_st = din("wg_st", (16, 16, 128, 1024))
    wl_st = din("wl_st", (2, 64, 128, 1024))
    wout_st = din("wout_st", (16, 16, 128, 128))
    b_in_fm = din("b_in_fm", (128, 48))
    b_out_fm = din("b_out_fm", (128, 16))
    gam_rep = din("gam_rep", (128, D))
    bet_rep = din("bet_rep", (128, D))
    cos_st = din("cos_st", (8, 128, TOK))
    sin_st = din("sin_st", (8, 128, TOK))
    ident = din("ident", (128, 128))
    ccin = nc.dram_tensor("ccin", [128, 16], F32)
    ccout = nc.dram_tensor("ccout", [128, 16], F32)
    proj_tm = nc.dram_tensor("proj_tm", [TOK, D], BF16)

    es = contextlib.ExitStack()
    sch = Sched()
    with es:
        T = lambda n, shp, dt: es.enter_context(nc.sbuf_tensor(n, list(shp), dt))
        xslot = T("xslot", (128, D), F32)
        xnt = T("xnt", (128, D), F32)
        sc1 = T("sc1", (128, D), F32)
        ring = T("ring", (128, D), F32)
        tab = T("tab", (128, D), F32)
        stats = T("stats", (128, 64), F32)
        winslab = T("winslab", (128, 4096), BF16)
        XNT = T("XNT", (128, 16384), BF16)
        QTB = T("QTB", (128, 16384), BF16)
        gpool = T("gpool", (128, 3072), BF16)
        vpb = T("vpb", (128, 1024), BF16)
        stg = T("stg", (128, 128), BF16)
        identf = T("identf", (128, 128), F32)
        onesb = T("onesb", (128, 1), F32)
        onesbf = T("onesbf", (128, 1), BF16)
        ones1 = T("ones1", (1, 128), F32)
        binf = T("binf", (128, 48), F32)
        boutf = T("boutf", (128, 16), F32)
        QP1 = T("QP1", (128, 16384), FP8)
        QP2 = T("QP2", (128, 16384), FP8)
        xnqT = T("xnqT", (128, 16384), FP8)
        actT = T("actT", (128, 32768), FP8)
        wlslab = T("wlslab", (128, 2, 1024), FP8)
        woutsl = T("woutsl", (128, 2048), BF16)
        PS = [es.enter_context(nc.psum_tensor(f"ps{i}", [128, 1024], F32)) for i in range(4)]

        bufs = {}

        def Bf(n):
            bufs[n] = Buf(n)
            return bufs[n]

        b_x, b_xn, b_sc1, b_ring, b_tab = Bf("x"), Bf("xn"), Bf("sc1"), Bf("ring"), Bf("tab")
        b_small, b_stats = Bf("small"), Bf("stats")
        b_winslab, b_XNT, b_QTB, b_gpool = Bf("winslab"), Bf("XNT"), Bf("QTB"), Bf("gpool")
        b_vpb, b_stg = Bf("vpb"), Bf("stg")
        b_identf, b_onesb, b_ones1, b_binf, b_boutf = Bf("identf"), Bf("onesb"), Bf("ones1"), Bf("binf"), Bf("boutf")
        b_onesbf = Bf("onesbf")
        b_QP1, b_QP2, b_xnqT, b_actT = Bf("QP1"), Bf("QP2"), Bf("xnqT"), Bf("actT")
        b_wls = [Bf("wls0"), Bf("wls1")]
        b_wos = Bf("wos")
        b_ps = [Bf(f"ps{i}") for i in range(4)]
        b_ccin, b_ccout, b_projtm = Bf("ccin"), Bf("ccout"), Bf("projtm")

        E = sch.emit

        E("sync", lambda e: e.dma_start(out=identf[:, :], in_=ident[:, :]), writes=[b_identf], dma="auto")
        E("sync", lambda e: e.dma_start(out=binf[:, :], in_=b_in_fm[:, :]), writes=[b_binf], dma="auto")
        E("sync", lambda e: e.dma_start(out=boutf[:, :], in_=b_out_fm[:, :]), writes=[b_boutf], dma="auto")
        E("sync", lambda e: e.dma_start(out=tab[:, :], in_=gam_rep[:, :]), writes=[b_tab], dma="auto")
        E("sync", lambda e: e.dma_start(out=ring[:, :], in_=bet_rep[:, :]), writes=[b_ring], dma="auto")
        E("vector", lambda e: e.memset(onesb[:, :], 1.0), writes=[b_onesb])
        E("vector", lambda e: e.memset(onesbf[:, :], 1.0), writes=[b_onesbf])
        E("vector", lambda e: e.memset(ones1[:, :], 1.0), writes=[b_ones1])

        mS = lambda tt: stats[:, tt:tt + 1]
        rS = lambda tt: stats[:, 8 + tt:9 + tt]
        t0 = stats[:, 16:17]; t1 = stats[:, 17:18]; t2 = stats[:, 18:19]
        t3 = stats[:, 19:20]; t4 = stats[:, 20:21]; t5 = stats[:, 21:22]
        pA_ap = stats[:, 32:48]
        A_ap = stats[:, 48:64]
        b_pA, b_A = Bf("pA"), Bf("A")

        def rsqrt_into(src_ap, dst_ap, rb, wb):
            E("scalar", lambda e: e.activation(t3, src_ap, AF.Sqrt), reads=[rb], writes=[b_stats])
            E("vector", lambda e: e.reciprocal(t4, t3), reads=[b_stats], writes=[b_stats])
            E("vector", lambda e: e.tensor_mul(t5, t4, t4), reads=[b_stats], writes=[b_stats])
            E("vector", lambda e: e.tensor_mul(t5, src_ap, t5), reads=[rb, b_stats], writes=[b_stats])
            E("vector", lambda e: e.tensor_scalar(t5, t5, -0.5, 1.5, OP.mult, OP.add),
              reads=[b_stats], writes=[b_stats])
            E("vector", lambda e: e.tensor_mul(dst_ap, t4, t5), reads=[b_stats], writes=[wb])

        for tt in range(NTT):
            E("sync", lambda e, tt=tt: e.dma_start(out=xslot[:, :], in_=x[tt * 128:(tt + 1) * 128, :]),
              writes=[b_x], dma="auto")
            E("vector", lambda e: e.reduce_sum(t0, xslot[:, :], axis=AX), reads=[b_x], writes=[b_stats])
            E("scalar", lambda e, tt=tt: e.activation(mS(tt), t0, AF.Identity, scale=INV_D),
              reads=[b_stats], writes=[b_stats])
            E("vector", lambda e, tt=tt: e.tensor_scalar(sc1[:, :], xslot[:, :], mS(tt), None, OP.subtract),
              reads=[b_x, b_stats], writes=[b_sc1])
            E("vector", lambda e: e.scalar_tensor_tensor(xnt[:, :], sc1[:, :], 1.0, sc1[:, :],
                                                         OP.mult, OP.mult, accum_out=t1),
              reads=[b_sc1], writes=[b_xn, b_stats])
            E("vector", lambda e: e.tensor_scalar(t2, t1, INV_D, float(EPS), OP.mult, OP.add),
              reads=[b_stats], writes=[b_stats])
            rsqrt_into(t2, rS(tt), b_stats, b_stats)
            E("vector", lambda e, tt=tt: e.tensor_scalar(xnt[:, :], xslot[:, :], mS(tt), rS(tt),
                                                         OP.subtract, OP.mult),
              reads=[b_x, b_stats], writes=[b_xn])
            E("scalar", lambda e, tt=tt: e.copy(QP1[:, tt * 2048:(tt + 1) * 2048], xnt[:, :]),
              reads=[b_xn], writes=[b_QP1])
            E("vector", lambda e: e.reduce_sum(t0, xnt[:, :], axis=AX), reads=[b_xn], writes=[b_stats])
            E("scalar", lambda e: e.activation(t1, t0, AF.Identity, scale=INV_D),
              reads=[b_stats], writes=[b_stats])
            E("vector", lambda e: e.tensor_scalar(sc1[:, :], xnt[:, :], t1, None, OP.subtract),
              reads=[b_xn, b_stats], writes=[b_sc1])
            E("vector", lambda e: e.scalar_tensor_tensor(xslot[:, :], sc1[:, :], 1.0, sc1[:, :],
                                                         OP.mult, OP.mult, accum_out=t2),
              reads=[b_sc1], writes=[b_x, b_stats])
            E("vector", lambda e: e.tensor_scalar(t2, t2, INV_D, float(EPS), OP.mult, OP.add),
              reads=[b_stats], writes=[b_stats])
            rsqrt_into(t2, t0, b_stats, b_stats)
            E("vector", lambda e: e.scalar_tensor_tensor(xslot[:, :], sc1[:, :], t0, tab[:, :],
                                                         OP.mult, OP.mult),
              reads=[b_sc1, b_stats, b_tab], writes=[b_x])
            E("vector", lambda e: e.tensor_tensor(sc1[:, :], xslot[:, :], ring[:, :], OP.add),
              reads=[b_x, b_ring], writes=[b_sc1])
            for half in range(2):
                ps, bp = PS[2 + half], b_ps[2 + half]
                for k in range(8):
                    kc = half * 8 + k
                    E("tensor", lambda e, kc=kc, k=k, ps=ps: e.transpose(
                        ps[:, k * 128:(k + 1) * 128], sc1[:, kc * 128:(kc + 1) * 128], identf[:, :]),
                      reads=[b_sc1, b_identf], writes=[bp])
                for k in range(8):
                    kc = half * 8 + k
                    E("scalar", lambda e, kc=kc, k=k, tt=tt, ps=ps: e.copy(
                        XNT[:, kc * 1024 + tt * 128: kc * 1024 + (tt + 1) * 128],
                        ps[:, k * 128:(k + 1) * 128]),
                      reads=[bp], writes=[b_XNT])

        def load_winslab(g):
            for kc in range(16):
                E("gpsimd", lambda e, g=g, kc=kc: e.dma_start(
                    out=winslab[:, kc * 256:(kc + 1) * 256], in_=w_in_st[g, kc, :, :]),
                  writes=[b_winslab], dma="auto")

        def qkv_mms(oc01, ps, bp):
            for kc in range(16):
                for nh in range(2):
                    last = (kc == 15 and nh == 1)
                    E("tensor", lambda e, oc=oc01, kc=kc, nh=nh, ps=ps: e.matmul(
                        ps[:, nh * 512:(nh + 1) * 512],
                        winslab[:, kc * 256 + oc * 128: kc * 256 + (oc + 1) * 128],
                        XNT[:, kc * 1024 + nh * 512: kc * 1024 + (nh + 1) * 512],
                        start=(kc == 0), stop=last),
                      reads=[b_winslab, b_XNT],
                      writes=[bp] if (kc == 0 and nh == 0) or last else [])

        nv2 = xslot[0:1, 0:1024]
        nk2 = xslot[0:1, 1024:2048]
        rv_ = xslot[0:1, 0:1024]
        rk_ = xslot[0:1, 1024:2048]
        s3_ = tab[0:1, 0:1024]
        s4_ = ring[0:1, 0:1024]
        w1_ = ring[0:1, 1024:2048]
        E("vector", lambda e: e.memset(xslot[0:1, 0:2048], 0.0), writes=[b_x])

        r0ap = xnt[:, 0:1024]
        r1ap = xnt[:, 1024:2048]
        m1ap = ring[:, 0:1024]
        m2ap = ring[:, 1024:2048]
        k1ap = sc1[:, 0:1024]

        def norm_accum(dst_row, src_ap, b_src):
            for nh in range(2):
                E("tensor", lambda e, nh=nh, src_ap=src_ap: e.matmul(
                    PS[2][0:1, nh * 512:(nh + 1) * 512], onesbf[:, :],
                    src_ap[:, nh * 512:(nh + 1) * 512], start=True, stop=True),
                  reads=[b_src, b_onesbf], writes=[b_ps[2]])
            E("vector", lambda e, dst_row=dst_row: e.tensor_tensor(dst_row, dst_row, PS[2][0:1, :], OP.add),
              reads=[b_ps[2], b_x], writes=[b_x])

        pending_norms = []

        def flush_norms():
            if pending_norms:
                pending_norms.pop()
                norm_accum(nv2, woutsl[:, 0:1024], b_wos)
                norm_accum(nk2, woutsl[:, 1024:2048], b_wos)

        for g in range(24):
            load_winslab(g)
            qkv_mms(0, PS[0], b_ps[0])
            qkv_mms(1, PS[1], b_ps[1])
            flush_norms()
            if g < 16:
                kc = g
                if kc < 8:
                    E("sync", lambda e, kc=kc: e.dma_start(out=tab[:, 0:1024], in_=cos_st[kc, :, :]),
                      writes=[b_tab], dma="auto")
                    E("sync", lambda e, kc=kc: e.dma_start(out=tab[:, 1024:2048], in_=sin_st[kc, :, :]),
                      writes=[b_tab], dma="auto")
                    E("scalar", lambda e, kc=kc: e.activation(r0ap, PS[0][:, :], AF.Identity,
                                                              bias=binf[:, 32 + kc:33 + kc]),
                      reads=[b_ps[0], b_binf], writes=[b_xn])
                    E("vector", lambda e: e.stream_shuffle(r1ap, r0ap, SWAP32),
                      reads=[b_xn], writes=[b_xn])
                    E("vector", lambda e: e.tensor_tensor(m1ap, r0ap, tab[:, 0:1024], OP.mult),
                      reads=[b_xn, b_tab], writes=[b_ring])
                    E("gpsimd", lambda e: e.tensor_tensor(m2ap, r1ap, tab[:, 1024:2048], OP.mult),
                      reads=[b_xn, b_tab], writes=[b_ring])
                    E("vector", lambda e: e.tensor_tensor(vpb[:, :], m1ap, m2ap, OP.add),
                      reads=[b_ring], writes=[b_vpb])
                    E("scalar", lambda e, kc=kc: e.activation(r0ap, PS[1][:, :], AF.Identity,
                                                              bias=binf[:, 16 + kc:17 + kc]),
                      reads=[b_ps[1], b_binf], writes=[b_xn])
                    E("vector", lambda e: e.stream_shuffle(r1ap, r0ap, SWAP32),
                      reads=[b_xn], writes=[b_xn])
                    E("vector", lambda e: e.tensor_tensor(k1ap, r0ap, tab[:, 0:1024], OP.mult),
                      reads=[b_xn, b_tab], writes=[b_sc1])
                    E("gpsimd", lambda e: e.tensor_tensor(m2ap, r1ap, tab[:, 1024:2048], OP.mult),
                      reads=[b_xn, b_tab], writes=[b_ring])
                    E("vector", lambda e: e.tensor_tensor(m1ap, k1ap, m2ap, OP.add),
                      reads=[b_sc1, b_ring], writes=[b_ring])
                else:
                    E("scalar", lambda e, kc=kc: e.activation(vpb[:, :], PS[0][:, :], AF.Identity,
                                                              bias=binf[:, 32 + kc:33 + kc]),
                      reads=[b_ps[0], b_binf], writes=[b_vpb])
                    E("scalar", lambda e, kc=kc: e.activation(m1ap, PS[1][:, :], AF.Identity,
                                                              bias=binf[:, 16 + kc:17 + kc]),
                      reads=[b_ps[1], b_binf], writes=[b_ring])
                E("vector", lambda e, kc=kc: e.tensor_tensor(QP2[:, kc * 1024:(kc + 1) * 1024],
                                                             m1ap, vpb[:, :], OP.mult),
                  reads=[b_ring, b_vpb], writes=[b_QP2])
                E("vector", lambda e: e.tensor_tensor(woutsl[:, 0:1024], vpb[:, :], vpb[:, :], OP.mult),
                  reads=[b_vpb], writes=[b_wos])
                E("vector", lambda e: e.tensor_tensor(woutsl[:, 1024:2048], m1ap, m1ap, OP.mult),
                  reads=[b_ring], writes=[b_wos])
                pending_norms.append(True)
            else:
                q = g - 16
                for j in range(2):
                    oc = 2 * q + j
                    ps, bp = PS[j], b_ps[j]
                    E("scalar", lambda e, oc=oc, ps=ps: e.activation(
                        QTB[:, oc * 1024:(oc + 1) * 1024], ps[:, :], AF.Identity,
                        bias=binf[:, oc:oc + 1]),
                      reads=[bp, b_binf], writes=[b_QTB])

        flush_norms()
        def rsqrt_row(src, dst):
            E("scalar", lambda e: e.activation(s3_, src, AF.Sqrt), reads=[b_x], writes=[b_tab])
            E("vector", lambda e: e.reciprocal(s4_, s3_), reads=[b_tab], writes=[b_ring])
            E("vector", lambda e: e.tensor_mul(s3_, s4_, s4_), reads=[b_ring], writes=[b_tab])
            E("vector", lambda e: e.tensor_mul(s3_, src, s3_), reads=[b_x, b_tab], writes=[b_tab])
            E("vector", lambda e: e.tensor_scalar(s3_, s3_, -0.5, 1.5, OP.mult, OP.add),
              reads=[b_tab], writes=[b_tab])
            E("vector", lambda e: e.tensor_mul(dst, s4_, s3_), reads=[b_ring, b_tab], writes=[b_x])

        rsqrt_row(nv2, rv_)
        rsqrt_row(nk2, rk_)
        E("vector", lambda e: e.tensor_mul(w1_, rk_, rv_), reads=[b_x], writes=[b_ring])
        for nh in range(2):
            E("tensor", lambda e, nh=nh: e.matmul(PS[2][:, nh * 512:(nh + 1) * 512], ones1[:, :],
                                                  w1_[:, nh * 512:(nh + 1) * 512], start=True, stop=True),
              reads=[b_ring, b_ones1], writes=[b_ps[2]])
        wrep = tab[:, 0:1024]
        E("vector", lambda e: e.tensor_copy(wrep, PS[2][:, :]), reads=[b_ps[2]], writes=[b_tab])
        for kc in range(16):
            E("vector", lambda e, kc=kc: e.scalar_tensor_tensor(
                ring[:, 0:1024], QP2[:, kc * 1024:(kc + 1) * 1024], 1.0, wrep,
                OP.mult, OP.mult, accum_out=pA_ap[:, kc:kc + 1]),
              reads=[b_QP2, b_tab], writes=[b_ring, b_pA])
        E("gpsimd", lambda e: e.dma_start(out=ccin.ap()[:, :], in_=pA_ap),
          reads=[b_pA], writes=[b_ccin], dma="auto")
        cc_sem = sch.new_sem("ccdone")
        E("gpsimd", lambda e: e.collective_compute(
            "AllReduce", OP.add, replica_groups=[[0, 1, 2, 3], [4, 5, 6, 7]],
            ins=[ccin.ap().opt()], outs=[ccout.ap().opt()]),
          reads=[b_ccin], writes=[b_ccout], sem=cc_sem)
        E("gpsimd", lambda e: e.dma_start(out=A_ap, in_=ccout.ap()[:, :]),
          reads=[b_ccout], writes=[b_A], dma="auto")
        E("vector", lambda e: e.tensor_scalar(A_ap, A_ap, 0.5, 0.5, OP.mult, OP.add),
          reads=[b_A], writes=[b_A])

        for tt in range(NTT):
            E("vector", lambda e, tt=tt: e.tensor_copy(sc1[:, :], QP1[:, tt * 2048:(tt + 1) * 2048]),
              reads=[b_QP1], writes=[b_sc1])
            for half in range(2):
                ps, bp = PS[2 + half], b_ps[2 + half]
                for k in range(8):
                    kc = half * 8 + k
                    E("tensor", lambda e, kc=kc, k=k, ps=ps: e.transpose(
                        ps[:, k * 128:(k + 1) * 128], sc1[:, kc * 128:(kc + 1) * 128], identf[:, :]),
                      reads=[b_sc1, b_identf], writes=[bp])
                for k in range(8):
                    kc = half * 8 + k
                    E("scalar", lambda e, kc=kc, k=k, tt=tt, ps=ps: e.copy(
                        xnqT[:, kc * 1024 + tt * 128: kc * 1024 + (tt + 1) * 128],
                        ps[:, k * 128:(k + 1) * 128]),
                      reads=[bp], writes=[b_xnqT])

        gB = gpool[:, 0:512]
        fB = gpool[:, 512:1024]
        tA = gpool[:, 1024:1536]
        tB = gpool[:, 1536:2048]
        fbB = gpool[:, 2048:3072]

        pend_tp = []
        blkpar = [0]

        def flush_tp():
            while pend_tp:
                jb0, tl0, par0 = pend_tp.pop(0)
                pst, bpt = PS[2 + par0], b_ps[2 + par0]
                for i in range(4):
                    E("tensor", lambda e, i=i, par0=par0, pst=pst: e.transpose(
                        pst[:, i * 128:(i + 1) * 128],
                        ring[:, par0 * 512 + i * 128: par0 * 512 + (i + 1) * 128], identf[:, :]),
                      reads=[b_ring, b_identf], writes=[bpt])
                for i in range(4):
                    E("scalar", lambda e, i=i, jb0=jb0, tl0=tl0, pst=pst: e.copy(
                        actT[:, (4 * jb0 + i) * 512 + tl0 * 128: (4 * jb0 + i) * 512 + (tl0 + 1) * 128],
                        pst[:, i * 128:(i + 1) * 128]),
                      reads=[bpt], writes=[b_actT])

        for h in range(2):
            for jb in range(16):
                slab = (QP1, QP2)[jb % 2]
                b_slab = (b_QP1, b_QP2)[jb % 2]
                for kc in range(16):
                    E("gpsimd", lambda e, jb=jb, kc=kc, slab=slab: e.dma_start(
                        out=slab[:, kc * 1024:(kc + 1) * 1024], in_=wg_st[jb, kc, :, :]),
                      writes=[b_slab], dma="auto")
                for tl in range(4):
                    tt = 4 * h + tl
                    ps, bp = PS[tl % 2], b_ps[tl % 2]
                    for kc in range(16):
                        for nh in range(2):
                            last = (kc == 15 and nh == 1)
                            E("tensor", lambda e, kc=kc, nh=nh, tt=tt, ps=ps, slab=slab: e.matmul(
                                ps[:, nh * 512:(nh + 1) * 512],
                                xnqT[:, kc * 1024 + tt * 128: kc * 1024 + (tt + 1) * 128],
                                slab[:, kc * 1024 + nh * 512: kc * 1024 + (nh + 1) * 512],
                                start=(kc == 0), stop=last),
                              reads=[b_xnqT, b_slab],
                              writes=[bp] if (kc == 0 and nh == 0) or last else [])
                    E("scalar", lambda e, ps=ps: e.copy(gB, ps[:, 0:512]), reads=[bp], writes=[b_gpool])
                    E("scalar", lambda e, ps=ps: e.copy(fB, ps[:, 512:1024]), reads=[bp], writes=[b_gpool])
                    E("vector", lambda e: e.tensor_mul(tA, gB, gB), reads=[b_gpool], writes=[b_gpool])
                    E("vector", lambda e: e.tensor_mul(tB, tA, gB), reads=[b_gpool], writes=[b_gpool])
                    E("scalar", lambda e: e.activation(tA, tB, AF.Copy, scale=GC1),
                      reads=[b_gpool], writes=[b_gpool])
                    E("vector", lambda e: e.tensor_tensor(tB, gB, tA, OP.add),
                      reads=[b_gpool], writes=[b_gpool])
                    E("scalar", lambda e: e.activation(tA, tB, AF.Copy, scale=GC2),
                      reads=[b_gpool], writes=[b_gpool])
                    E("scalar", lambda e: e.activation(tB, tA, AF.Tanh),
                      reads=[b_gpool], writes=[b_gpool])
                    E("vector", lambda e: e.tensor_scalar_add(tA, tB, 1.0),
                      reads=[b_gpool], writes=[b_gpool])
                    E("scalar", lambda e: e.activation(tB, tA, AF.Copy, scale=0.5),
                      reads=[b_gpool], writes=[b_gpool])
                    E("vector", lambda e: e.tensor_mul(tA, gB, tB), reads=[b_gpool], writes=[b_gpool])
                    par = blkpar[0]
                    blkpar[0] ^= 1
                    E("vector", lambda e, par=par: e.tensor_mul(winslab[:, par * 512:(par + 1) * 512], tA, fB),
                      reads=[b_gpool], writes=[b_winslab])
                    E("vector", lambda e, par=par: e.tensor_copy(ring[:, par * 512:(par + 1) * 512],
                                                                 winslab[:, par * 512:(par + 1) * 512]),
                      reads=[b_winslab], writes=[b_ring])
                    flush_tp()
                    pend_tp.append((jb, tl, par))
            flush_tp()
            for dh in range(2):
                for kc in range(64):
                    sl = kc % 2
                    E("gpsimd", lambda e, dh=dh, kc=kc, sl=sl: e.dma_start(
                        out=wlslab[:, sl, :], in_=wl_st[dh, kc, :, :]),
                      writes=[b_wls[sl]], dma="auto")
                    for tl in range(4):
                        last = (kc == 63)
                        E("tensor", lambda e, kc=kc, tl=tl, sl=sl: e.matmul(
                            PS[tl][:, 0:512],
                            actT[:, kc * 512 + tl * 128: kc * 512 + (tl + 1) * 128],
                            wlslab[:, sl, 0:512], start=(kc == 0), stop=last),
                          reads=[b_actT, b_wls[sl]],
                          writes=[b_ps[tl]] if kc == 0 else [])
                        E("tensor", lambda e, kc=kc, tl=tl, sl=sl: e.matmul(
                            PS[tl][:, 512:1024],
                            actT[:, kc * 512 + tl * 128: kc * 512 + (tl + 1) * 128],
                            wlslab[:, sl, 512:1024], start=(kc == 0), stop=last),
                          reads=[b_actT, b_wls[sl]],
                          writes=[b_ps[tl]] if last else [])
                for tl in range(4):
                    tt = 4 * h + tl
                    E("scalar", lambda e, tl=tl: e.copy(fbB, PS[tl][:, :]),
                      reads=[b_ps[tl]], writes=[b_gpool])
                    E("scalar", lambda e, tt=tt, dh=dh: e.dma_start(
                        out=XNT[:, tt * 2048 + dh * 1024: tt * 2048 + (dh + 1) * 1024].bitcast(U16),
                        in_=fbB.bitcast(U16)),
                      reads=[b_gpool], writes=[b_XNT], dma="auto")

        for kc in range(16):
            E("scalar", lambda e, kc=kc: e.copy(gpool[:, 1024:2048],
                                                QTB[:, kc * 1024:(kc + 1) * 1024]),
              reads=[b_QTB], writes=[b_gpool])
            E("scalar", lambda e, kc=kc: e.activation(QTB[:, kc * 1024:(kc + 1) * 1024],
                                                      gpool[:, 1024:2048], AF.Copy,
                                                      scale=A_ap[:, kc:kc + 1]),
              reads=[b_gpool, b_A], writes=[b_QTB])
        for db in range(16):
            for kc in range(16):
                E("gpsimd", lambda e, db=db, kc=kc: e.dma_start(
                    out=woutsl[:, kc * 128:(kc + 1) * 128], in_=wout_st[db, kc, :, :]),
                  writes=[b_wos], dma="auto")
            ps, bp = PS[db % 2], b_ps[db % 2]
            for kc in range(16):
                for nh in range(2):
                    last = (kc == 15 and nh == 1)
                    E("tensor", lambda e, kc=kc, nh=nh, ps=ps: e.matmul(
                        ps[:, nh * 512:(nh + 1) * 512],
                        woutsl[:, kc * 128:(kc + 1) * 128],
                        QTB[:, kc * 1024 + nh * 512: kc * 1024 + (nh + 1) * 512],
                        start=(kc == 0), stop=last),
                      reads=[b_wos, b_QTB],
                      writes=[bp] if (kc == 0 and nh == 0) or last else [])
            E("scalar", lambda e, db=db, ps=ps: e.activation(fbB, ps[:, :], AF.Identity,
                                                             bias=boutf[:, db:db + 1]),
              reads=[bp, b_boutf], writes=[b_gpool])
            E("vector", lambda e: e.tensor_copy(ring[:, 0:1024], fbB), reads=[b_gpool], writes=[b_ring])
            pst, bpt = PS[2 + (db % 2)], b_ps[2 + (db % 2)]
            for i in range(8):
                E("tensor", lambda e, i=i, pst=pst: e.transpose(
                    pst[:, i * 128:(i + 1) * 128], ring[:, i * 128:(i + 1) * 128], identf[:, :]),
                  reads=[b_ring, b_identf], writes=[bpt])
            for i in range(8):
                E("scalar", lambda e, i=i, pst=pst: e.copy(stg[:, :], pst[:, i * 128:(i + 1) * 128]),
                  reads=[bpt], writes=[b_stg])
                E("scalar", lambda e, i=i, db=db: e.dma_start(
                    out=proj_tm.ap()[i * 128:(i + 1) * 128, db * 128:(db + 1) * 128], in_=stg[:, :]),
                  reads=[b_stg], writes=[b_projtm], dma="auto")

        projslot = gpool[:, 0:2048]
        for tt in range(NTT):
            E("sync", lambda e, tt=tt: e.dma_start(out=xslot[:, :], in_=x[tt * 128:(tt + 1) * 128, :]),
              writes=[b_x], dma="auto")
            E("sync", lambda e, tt=tt: e.dma_start(out=projslot,
                                                   in_=proj_tm.ap()[tt * 128:(tt + 1) * 128, :]),
              reads=[b_projtm], writes=[b_gpool], dma="auto")
            E("vector", lambda e, tt=tt: e.tensor_scalar(sc1[:, :], xslot[:, :], mS(tt), rS(tt),
                                                         OP.subtract, OP.mult),
              reads=[b_x, b_stats], writes=[b_sc1])
            E("vector", lambda e: e.tensor_tensor(ring[:, :], sc1[:, :], xslot[:, :], OP.add),
              reads=[b_sc1, b_x], writes=[b_ring])
            E("vector", lambda e: e.tensor_tensor(sc1[:, :], ring[:, :], projslot, OP.add),
              reads=[b_ring, b_gpool], writes=[b_sc1])
            E("vector", lambda e, tt=tt: e.tensor_tensor(ring[:, :], sc1[:, :],
                                                         XNT[:, tt * 2048:(tt + 1) * 2048], OP.add),
              reads=[b_sc1, b_XNT], writes=[b_ring])
            E("sync", lambda e, tt=tt: e.dma_start(out=y[tt * 128:(tt + 1) * 128, :], in_=ring[:, :]),
              reads=[b_ring], dma="o")
        so = sch.dma_sem("o")
        sch.wait_only("gpsimd", [(so, so.count)])

        sems = sch.all_sems()
        with contextlib.ExitStack() as es2:
            for s in sems:
                s.h = es2.enter_context(nc.semaphore(s.name))
            with nc.Block() as block:
                @block.sync
                def _(e):
                    sch.run("sync", e)

                @block.vector
                def _(e):
                    sch.run("vector", e)

                @block.scalar
                def _(e):
                    sch.run("scalar", e)

                @block.tensor
                def _(e):
                    sch.run("tensor", e)

                @block.gpsimd
                def _(e):
                    sch.run("gpsimd", e)
    return nc


_NC_CACHE = [None]


def kernel(L0_attn_X, L0_attn_W_in, L0_attn_b_in, L0_attn_W_out, L0_attn_b_out,
           L0_attn_gamma, L0_attn_beta, L0_ffn_w_gating, L0_ffn_w_linear):
    X = np.ascontiguousarray(np.asarray(L0_attn_X, dtype=np.float32))
    W_in = np.asarray(L0_attn_W_in, dtype=np.float32)
    b_in = np.asarray(L0_attn_b_in, dtype=np.float32)
    W_out = np.asarray(L0_attn_W_out, dtype=np.float32)
    b_out = np.asarray(L0_attn_b_out, dtype=np.float32)
    gamma = np.asarray(L0_attn_gamma, dtype=np.float32)
    beta = np.asarray(L0_attn_beta, dtype=np.float32)
    w_g = np.asarray(L0_ffn_w_gating, dtype=np.float32)
    w_l = np.asarray(L0_ffn_w_linear, dtype=np.float32)

    cols = []
    for kc in range(16):
        cols += list(range(4096 + kc * 128, 4096 + (kc + 1) * 128))
        cols += list(range(2048 + kc * 128, 2048 + (kc + 1) * 128))
    cols += list(range(0, 2048))
    w_in_st = np.ascontiguousarray(
        W_in[:, cols].reshape(16, 128, 24, 256).transpose(2, 0, 1, 3))
    a0 = w_g[0].reshape(16, 128, 16, 512)
    a1 = w_g[1].reshape(16, 128, 16, 512)
    wg_st = np.ascontiguousarray(
        np.concatenate([a0, a1], axis=3).transpose(2, 0, 1, 3))
    wl_st = np.ascontiguousarray(w_l.reshape(64, 128, 2, 1024).transpose(2, 0, 1, 3))
    wout_st = np.ascontiguousarray(W_out.reshape(16, 128, 16, 128).transpose(2, 0, 1, 3))
    b_in_fm = np.ascontiguousarray(b_in.reshape(48, 128).T)
    b_out_fm = np.ascontiguousarray(b_out.reshape(16, 128).T)
    gam_rep = np.ascontiguousarray(np.broadcast_to(gamma, (128, D)))
    bet_rep = np.ascontiguousarray(np.broadcast_to(beta, (128, D)))
    ident = np.eye(128, dtype=np.float32)

    shared = dict(w_in_st=w_in_st, wg_st=wg_st, wl_st=wl_st, wout_st=wout_st,
                  b_in_fm=b_in_fm, b_out_fm=b_out_fm, gam_rep=gam_rep,
                  bet_rep=bet_rep, ident=ident)
    in_maps = []
    for c in range(NCORE):
        b = c // 4
        r0 = (c % 4) * TOK
        cos_fm, sin_fm = _rotary_tables(c)
        m = dict(shared)
        m["x"] = np.ascontiguousarray(X[b, r0:r0 + TOK, :])
        m["cos_st"] = cos_fm
        m["sin_st"] = sin_fm
        in_maps.append(m)

    if _NC_CACHE[0] is None:
        _NC_CACHE[0] = build_nc()
    nc = _NC_CACHE[0]
    res = run_bass_kernel_spmd(nc, in_maps, core_ids=list(range(NCORE)))
    out = np.empty((B, S, D), dtype=np.float32)
    for c in range(NCORE):
        b = c // 4
        r0 = (c % 4) * TOK
        out[b, r0:r0 + TOK, :] = res.results[c]["y"]
    return out


# version 03
# speedup vs baseline: 1.4469x; 1.4469x over previous
import contextlib
import numpy as np
import concourse.bass as bass
import concourse.mybir as mybir
from concourse.bass_utils import run_bass_kernel_spmd

F32 = mybir.dt.float32
BF16 = mybir.dt.bfloat16
FP8 = mybir.dt.float8e4
U16 = mybir.dt.uint16
AF = mybir.ActivationFunctionType
OP = mybir.AluOpType
AX = mybir.AxisListType.X

B, S, D = 2, 4096, 2048
NCORE = 8
TOK = 1024
NTT = 8
EPS = 1e-5
INV_D = float(1.0 / 2048.0)
GC1 = 0.0446777344
GC2 = 0.796875
SWAP32 = [i ^ 1 for i in range(32)]

ENGS = ("sync", "vector", "scalar", "tensor", "gpsimd")


class Sem:
    __slots__ = ("name", "count", "h")

    def __init__(self, name):
        self.name, self.count, self.h = name, 0, None


class Buf:
    __slots__ = ("name", "w", "readers")

    def __init__(self, name):
        self.name, self.w, self.readers = name, None, []


class Sched:
    def __init__(self):
        self.q = {e: [] for e in ENGS}
        self.esem = {e: Sem("E_" + e) for e in ENGS}
        self.dsem = {}
        self.xsem = []

    def all_sems(self):
        return list(self.esem.values()) + list(self.dsem.values()) + self.xsem

    def dma_sem(self, name):
        if name not in self.dsem:
            self.dsem[name] = Sem("D_" + name)
        return self.dsem[name]

    def new_sem(self, name):
        s = Sem(name)
        self.xsem.append(s)
        return s

    def emit(self, eng, build, reads=(), writes=(), dma=None, sem=None, waits=()):
        wmap = {}

        def addw(tok):
            if tok is None:
                return
            s, v = tok
            if wmap.get(s, -1) < v:
                wmap[s] = v

        for b in reads:
            addw(b.w)
        for b in writes:
            addw(b.w)
            for t in b.readers:
                addw(t)
        for t in waits:
            addw(t)
        if dma is not None:
            if dma == "auto":
                assert writes, "auto dma sem needs a write buffer"
                dma = "b_" + writes[0].name
            s = self.dma_sem(dma)
            s.count += 16
            tok = (s, s.count)
            amt = 16
        elif sem is not None:
            s = sem
            s.count += 1
            tok = (s, s.count)
            amt = 1
        else:
            s = self.esem[eng]
            s.count += 1
            tok = (s, s.count)
            amt = 1
        self.q[eng].append((build, tuple(wmap.items()), (s, amt)))
        for b in reads:
            b.readers.append(tok)
        for b in writes:
            b.w = tok
            b.readers = []
        return tok

    def wait_only(self, eng, toks):
        self.q[eng].append((None, tuple((s, v) for s, v in toks), None))

    def run(self, eng_name, engobj):
        waited = {}
        for build, waits, inc in self.q[eng_name]:
            for s, v in waits:
                if waited.get(s, 0) < v:
                    engobj.wait_ge(s.h, v)
                    waited[s] = v
            if build is None:
                continue
            ins = build(engobj)
            s, amt = inc
            ins.then_inc(s.h, amt)
            if waited.get(s, 0) < s.count:
                pass


def _rotary_tables(core):
    rot = D // 2
    inv_freq = (1.0 / (np.float32(10000.0) **
                       (np.arange(0, rot, 2, dtype=np.float32) / np.float32(rot)))).astype(np.float32)
    p0 = (core % 4) * TOK
    pos = np.arange(p0, p0 + TOK, dtype=np.float32)
    fr = (pos[:, None] * inv_freq[None, :]).astype(np.float32)
    fr = np.repeat(fr, 2, axis=1)
    cos = np.cos(fr).astype(np.float32)
    sin = np.sin(fr).astype(np.float32)
    sinS = sin.copy()
    sinS[:, 0::2] = -sin[:, 0::2]
    cos_fm = np.ascontiguousarray(cos.T.reshape(8, 128, TOK))
    sin_fm = np.ascontiguousarray(sinS.T.reshape(8, 128, TOK))
    return cos_fm, sin_fm


def build_nc():
    nc = bass.Bass("TRN2", target_bir_lowering=False, debug=False)

    def din(name, shape):
        return nc.dram_tensor(name, list(shape), F32, kind="ExternalInput").ap()

    x = din("x", (TOK, D))
    y = nc.dram_tensor("y", [TOK, D], F32, kind="ExternalOutput").ap()
    w_in_st = din("w_in_st", (24, 128, 4096))
    wg_st = din("wg_st", (16, 128, 16384))
    wl_st = din("wl_st", (2, 8, 128, 8192))
    wout_st = din("wout_st", (16, 128, 2048))
    b_in_fm = din("b_in_fm", (128, 48))
    b_out_fm = din("b_out_fm", (128, 16))
    gam_rep = din("gam_rep", (128, D))
    bet_rep = din("bet_rep", (128, D))
    tabs_st = din("tabs_st", (8, 128, 2048))
    ident = din("ident", (128, 128))
    ccin = nc.dram_tensor("ccin", [128, 16], F32)
    ccout = nc.dram_tensor("ccout", [128, 16], F32)
    proj_tm = nc.dram_tensor("proj_tm", [TOK, D], BF16)

    es = contextlib.ExitStack()
    sch = Sched()
    with es:
        T = lambda n, shp, dt: es.enter_context(nc.sbuf_tensor(n, list(shp), dt))
        xslot = T("xslot", (128, D), F32)
        xnt = T("xnt", (128, D), F32)
        sc1 = T("sc1", (128, D), F32)
        ring = T("ring", (128, D), F32)
        tab = T("tab", (128, D), F32)
        stats = T("stats", (128, 64), F32)
        winslab = T("winslab", (128, 4096), BF16)
        XNT = T("XNT", (128, 16384), BF16)
        QTB = T("QTB", (128, 16384), BF16)
        gpool = T("gpool", (128, 3072), BF16)
        vpb = T("vpb", (128, 1024), BF16)
        stg = T("stg", (128, 1024), BF16)
        identf = T("identf", (128, 128), F32)
        onesb = T("onesb", (128, 1), F32)
        onesbf = T("onesbf", (128, 1), BF16)
        ones1 = T("ones1", (1, 128), F32)
        binf = T("binf", (128, 48), F32)
        boutf = T("boutf", (128, 16), F32)
        QP1 = T("QP1", (128, 16384), FP8)
        QP2 = T("QP2", (128, 16384), FP8)
        xnqT = T("xnqT", (128, 16384), FP8)
        actT = T("actT", (128, 32768), FP8)
        woutsl = T("woutsl", (128, 2048), BF16)
        PS = [es.enter_context(nc.psum_tensor(f"ps{i}", [128, 1024], F32)) for i in range(4)]

        bufs = {}

        def Bf(n):
            bufs[n] = Buf(n)
            return bufs[n]

        b_x, b_xn, b_sc1, b_ring, b_tab = Bf("x"), Bf("xn"), Bf("sc1"), Bf("ring"), Bf("tab")
        b_small, b_stats = Bf("small"), Bf("stats")
        b_winslab, b_XNT, b_QTB, b_gpool = Bf("winslab"), Bf("XNT"), Bf("QTB"), Bf("gpool")
        b_vpb, b_stg = Bf("vpb"), Bf("stg")
        b_identf, b_onesb, b_ones1, b_binf, b_boutf = Bf("identf"), Bf("onesb"), Bf("ones1"), Bf("binf"), Bf("boutf")
        b_onesbf = Bf("onesbf")
        b_QP1, b_QP2, b_xnqT, b_actT = Bf("QP1"), Bf("QP2"), Bf("xnqT"), Bf("actT")
        b_wls = [Bf("wls0"), Bf("wls1")]
        b_wos = Bf("wos")
        b_ps = [Bf(f"ps{i}") for i in range(4)]
        b_ccin, b_ccout, b_projtm = Bf("ccin"), Bf("ccout"), Bf("projtm")

        E = sch.emit

        def toks(buf):
            t = list(buf.readers)
            if buf.w:
                t.append(buf.w)
            return t

        E("sync", lambda e: e.dma_start(out=identf[:, :], in_=ident[:, :]), writes=[b_identf], dma="auto")
        E("sync", lambda e: e.dma_start(out=binf[:, :], in_=b_in_fm[:, :]), writes=[b_binf], dma="auto")
        E("sync", lambda e: e.dma_start(out=boutf[:, :], in_=b_out_fm[:, :]), writes=[b_boutf], dma="auto")
        E("sync", lambda e: e.dma_start(out=tab[:, :], in_=gam_rep[:, :]), writes=[b_tab], dma="auto")
        E("sync", lambda e: e.dma_start(out=ring[:, :], in_=bet_rep[:, :]), writes=[b_ring], dma="auto")
        E("vector", lambda e: e.memset(onesb[:, :], 1.0), writes=[b_onesb])
        E("vector", lambda e: e.memset(onesbf[:, :], 1.0), writes=[b_onesbf])
        E("vector", lambda e: e.memset(ones1[:, :], 1.0), writes=[b_ones1])

        mS = lambda tt: stats[:, tt:tt + 1]
        rS = lambda tt: stats[:, 8 + tt:9 + tt]
        t0 = stats[:, 16:17]; t1 = stats[:, 17:18]; t2 = stats[:, 18:19]
        t3 = stats[:, 19:20]; t4 = stats[:, 20:21]; t5 = stats[:, 21:22]
        pA_ap = stats[:, 32:48]
        A_ap = stats[:, 48:64]
        b_pA, b_A = Bf("pA"), Bf("A")

        def rsqrt_into(src_ap, dst_ap, rb, wb):
            E("scalar", lambda e: e.activation(t3, src_ap, AF.Sqrt), reads=[rb], writes=[b_stats])
            E("vector", lambda e: e.reciprocal(t4, t3), reads=[b_stats], writes=[b_stats])
            E("vector", lambda e: e.tensor_mul(t5, t4, t4), reads=[b_stats], writes=[b_stats])
            E("vector", lambda e: e.tensor_mul(t5, src_ap, t5), reads=[rb, b_stats], writes=[b_stats])
            E("vector", lambda e: e.tensor_scalar(t5, t5, -0.5, 1.5, OP.mult, OP.add),
              reads=[b_stats], writes=[b_stats])
            E("vector", lambda e: e.tensor_mul(dst_ap, t4, t5), reads=[b_stats], writes=[wb])

        for tt in range(NTT):
            E("sync", lambda e, tt=tt: e.dma_start(out=xslot[:, :], in_=x[tt * 128:(tt + 1) * 128, :]),
              writes=[b_x], dma="auto")
            E("vector", lambda e: e.reduce_sum(t0, xslot[:, :], axis=AX), reads=[b_x], writes=[b_stats])
            E("scalar", lambda e, tt=tt: e.activation(mS(tt), t0, AF.Identity, scale=INV_D),
              reads=[b_stats], writes=[b_stats])
            E("vector", lambda e, tt=tt: e.tensor_scalar(sc1[:, :], xslot[:, :], mS(tt), None, OP.subtract),
              reads=[b_x, b_stats], writes=[b_sc1])
            E("vector", lambda e: e.scalar_tensor_tensor(xnt[:, :], sc1[:, :], 1.0, sc1[:, :],
                                                         OP.mult, OP.mult, accum_out=t1),
              reads=[b_sc1], writes=[b_xn, b_stats])
            E("vector", lambda e: e.tensor_scalar(t2, t1, INV_D, float(EPS), OP.mult, OP.add),
              reads=[b_stats], writes=[b_stats])
            rsqrt_into(t2, rS(tt), b_stats, b_stats)
            E("vector", lambda e, tt=tt: e.tensor_scalar(xnt[:, :], xslot[:, :], mS(tt), rS(tt),
                                                         OP.subtract, OP.mult),
              reads=[b_x, b_stats], writes=[b_xn])
            E("scalar", lambda e, tt=tt: e.copy(QP1[:, tt * 2048:(tt + 1) * 2048], xnt[:, :]),
              reads=[b_xn], writes=[b_QP1])
            E("vector", lambda e: e.reduce_sum(t0, xnt[:, :], axis=AX), reads=[b_xn], writes=[b_stats])
            E("scalar", lambda e: e.activation(t1, t0, AF.Identity, scale=INV_D),
              reads=[b_stats], writes=[b_stats])
            E("vector", lambda e: e.tensor_scalar(sc1[:, :], xnt[:, :], t1, None, OP.subtract),
              reads=[b_xn, b_stats], writes=[b_sc1])
            E("vector", lambda e: e.scalar_tensor_tensor(xslot[:, :], sc1[:, :], 1.0, sc1[:, :],
                                                         OP.mult, OP.mult, accum_out=t2),
              reads=[b_sc1], writes=[b_x, b_stats])
            E("vector", lambda e: e.tensor_scalar(t2, t2, INV_D, float(EPS), OP.mult, OP.add),
              reads=[b_stats], writes=[b_stats])
            rsqrt_into(t2, t0, b_stats, b_stats)
            E("vector", lambda e: e.scalar_tensor_tensor(xslot[:, :], sc1[:, :], t0, tab[:, :],
                                                         OP.mult, OP.mult),
              reads=[b_sc1, b_stats, b_tab], writes=[b_x])
            E("vector", lambda e: e.tensor_tensor(sc1[:, :], xslot[:, :], ring[:, :], OP.add),
              reads=[b_x, b_ring], writes=[b_sc1])
            for half in range(2):
                ps, bp = PS[2 + half], b_ps[2 + half]
                for k in range(8):
                    kc = half * 8 + k
                    E("tensor", lambda e, kc=kc, k=k, ps=ps: e.transpose(
                        ps[:, k * 128:(k + 1) * 128], sc1[:, kc * 128:(kc + 1) * 128], identf[:, :]),
                      reads=[b_sc1, b_identf], writes=[bp])
                for k in range(8):
                    kc = half * 8 + k
                    E("scalar", lambda e, kc=kc, k=k, tt=tt, ps=ps: e.copy(
                        XNT[:, kc * 1024 + tt * 128: kc * 1024 + (tt + 1) * 128],
                        ps[:, k * 128:(k + 1) * 128]),
                      reads=[bp], writes=[b_XNT])

        def load_winslab(g):
            E("gpsimd", lambda e, g=g: e.dma_start(out=winslab[:, :], in_=w_in_st[g, :, :]),
              writes=[b_winslab], dma="auto")

        def qkv_mms(oc01, ps, bp):
            for kc in range(16):
                for nh in range(2):
                    last = (kc == 15 and nh == 1)
                    E("tensor", lambda e, oc=oc01, kc=kc, nh=nh, ps=ps: e.matmul(
                        ps[:, nh * 512:(nh + 1) * 512],
                        winslab[:, kc * 256 + oc * 128: kc * 256 + (oc + 1) * 128],
                        XNT[:, kc * 1024 + nh * 512: kc * 1024 + (nh + 1) * 512],
                        start=(kc == 0), stop=last),
                      reads=[b_winslab, b_XNT],
                      writes=[bp] if (kc == 0 and nh == 0) or last else [])

        nv2 = xslot[0:1, 0:1024]
        nk2 = xslot[0:1, 1024:2048]
        rv_ = xslot[0:1, 0:1024]
        rk_ = xslot[0:1, 1024:2048]
        s3_ = tab[0:1, 0:1024]
        s4_ = ring[0:1, 0:1024]
        w1_ = ring[0:1, 1024:2048]
        E("vector", lambda e: e.memset(xslot[0:1, 0:2048], 0.0), writes=[b_x])

        r0ap = xnt[:, 0:1024]
        r1ap = xnt[:, 1024:2048]
        m1ap = ring[:, 0:1024]
        m2ap = ring[:, 1024:2048]
        k1ap = sc1[:, 0:1024]

        def norm_accum(dst_row, src_ap, b_src):
            for nh in range(2):
                E("tensor", lambda e, nh=nh, src_ap=src_ap: e.matmul(
                    PS[2][0:1, nh * 512:(nh + 1) * 512], onesbf[:, :],
                    src_ap[:, nh * 512:(nh + 1) * 512], start=True, stop=True),
                  reads=[b_src, b_onesbf], writes=[b_ps[2]])
            E("vector", lambda e, dst_row=dst_row: e.tensor_tensor(dst_row, dst_row, PS[2][0:1, :], OP.add),
              reads=[b_ps[2], b_x], writes=[b_x])

        pending_norms = []

        def flush_norms():
            if pending_norms:
                pending_norms.pop()
                norm_accum(nv2, woutsl[:, 0:1024], b_wos)
                norm_accum(nk2, woutsl[:, 1024:2048], b_wos)

        for g in range(24):
            load_winslab(g)
            qkv_mms(0, PS[0], b_ps[0])
            qkv_mms(1, PS[1], b_ps[1])
            flush_norms()
            if g < 16:
                kc = g
                if kc < 8:
                    E("sync", lambda e, kc=kc: e.dma_start(out=tab[:, :], in_=tabs_st[kc, :, :]),
                      writes=[b_tab], dma="auto")
                    E("scalar", lambda e, kc=kc: e.activation(r0ap, PS[0][:, :], AF.Identity,
                                                              bias=binf[:, 32 + kc:33 + kc]),
                      reads=[b_ps[0], b_binf], writes=[b_xn])
                    E("vector", lambda e: e.stream_shuffle(r1ap, r0ap, SWAP32),
                      reads=[b_xn], writes=[b_xn])
                    E("vector", lambda e: e.tensor_tensor(m1ap, r0ap, tab[:, 0:1024], OP.mult),
                      reads=[b_xn, b_tab], writes=[b_ring])
                    E("vector", lambda e: e.tensor_tensor(m2ap, r1ap, tab[:, 1024:2048], OP.mult),
                      reads=[b_xn, b_tab], writes=[b_ring])
                    E("vector", lambda e: e.tensor_tensor(vpb[:, :], m1ap, m2ap, OP.add),
                      reads=[b_ring], writes=[b_vpb])
                    E("scalar", lambda e, kc=kc: e.activation(r0ap, PS[1][:, :], AF.Identity,
                                                              bias=binf[:, 16 + kc:17 + kc]),
                      reads=[b_ps[1], b_binf], writes=[b_xn])
                    E("vector", lambda e: e.stream_shuffle(r1ap, r0ap, SWAP32),
                      reads=[b_xn], writes=[b_xn])
                    E("vector", lambda e: e.tensor_tensor(k1ap, r0ap, tab[:, 0:1024], OP.mult),
                      reads=[b_xn, b_tab], writes=[b_sc1])
                    E("vector", lambda e: e.tensor_tensor(m2ap, r1ap, tab[:, 1024:2048], OP.mult),
                      reads=[b_xn, b_tab], writes=[b_ring])
                    E("vector", lambda e: e.tensor_tensor(m1ap, k1ap, m2ap, OP.add),
                      reads=[b_sc1, b_ring], writes=[b_ring])
                else:
                    E("scalar", lambda e, kc=kc: e.activation(vpb[:, :], PS[0][:, :], AF.Identity,
                                                              bias=binf[:, 32 + kc:33 + kc]),
                      reads=[b_ps[0], b_binf], writes=[b_vpb])
                    E("scalar", lambda e, kc=kc: e.activation(m1ap, PS[1][:, :], AF.Identity,
                                                              bias=binf[:, 16 + kc:17 + kc]),
                      reads=[b_ps[1], b_binf], writes=[b_ring])
                E("vector", lambda e, kc=kc: e.tensor_tensor(QP2[:, kc * 1024:(kc + 1) * 1024],
                                                             m1ap, vpb[:, :], OP.mult),
                  reads=[b_ring, b_vpb], writes=[b_QP2])
                E("vector", lambda e: e.tensor_tensor(woutsl[:, 0:1024], vpb[:, :], vpb[:, :], OP.mult),
                  reads=[b_vpb], writes=[b_wos])
                E("vector", lambda e: e.tensor_tensor(woutsl[:, 1024:2048], m1ap, m1ap, OP.mult),
                  reads=[b_ring], writes=[b_wos])
                pending_norms.append(True)
            else:
                q = g - 16
                for j in range(2):
                    oc = 2 * q + j
                    ps, bp = PS[j], b_ps[j]
                    E("scalar", lambda e, oc=oc, ps=ps: e.activation(
                        QTB[:, oc * 1024:(oc + 1) * 1024], ps[:, :], AF.Identity,
                        bias=binf[:, oc:oc + 1]),
                      reads=[bp, b_binf], writes=[b_QTB])

        flush_norms()
        def rsqrt_row(src, dst):
            E("scalar", lambda e: e.activation(s3_, src, AF.Sqrt), reads=[b_x], writes=[b_tab])
            E("vector", lambda e: e.reciprocal(s4_, s3_), reads=[b_tab], writes=[b_ring])
            E("vector", lambda e: e.tensor_mul(s3_, s4_, s4_), reads=[b_ring], writes=[b_tab])
            E("vector", lambda e: e.tensor_mul(s3_, src, s3_), reads=[b_x, b_tab], writes=[b_tab])
            E("vector", lambda e: e.tensor_scalar(s3_, s3_, -0.5, 1.5, OP.mult, OP.add),
              reads=[b_tab], writes=[b_tab])
            E("vector", lambda e: e.tensor_mul(dst, s4_, s3_), reads=[b_ring, b_tab], writes=[b_x])

        rsqrt_row(nv2, rv_)
        rsqrt_row(nk2, rk_)
        E("vector", lambda e: e.tensor_mul(w1_, rk_, rv_), reads=[b_x], writes=[b_ring])
        for nh in range(2):
            E("tensor", lambda e, nh=nh: e.matmul(PS[2][:, nh * 512:(nh + 1) * 512], ones1[:, :],
                                                  w1_[:, nh * 512:(nh + 1) * 512], start=True, stop=True),
              reads=[b_ring, b_ones1], writes=[b_ps[2]])
        wrep = tab[:, 0:1024]
        E("vector", lambda e: e.tensor_copy(wrep, PS[2][:, :]), reads=[b_ps[2]], writes=[b_tab])
        for kc in range(16):
            E("vector", lambda e, kc=kc: e.scalar_tensor_tensor(
                ring[:, 0:1024], QP2[:, kc * 1024:(kc + 1) * 1024], 1.0, wrep,
                OP.mult, OP.mult, accum_out=pA_ap[:, kc:kc + 1]),
              reads=[b_QP2, b_tab], writes=[b_ring, b_pA])
        E("gpsimd", lambda e: e.dma_start(out=ccin.ap()[:, :], in_=pA_ap),
          reads=[b_pA], writes=[b_ccin], dma="auto")
        cc_sem = sch.new_sem("ccdone")
        E("gpsimd", lambda e: e.collective_compute(
            "AllReduce", OP.add, replica_groups=[[0, 1, 2, 3], [4, 5, 6, 7]],
            ins=[ccin.ap().opt()], outs=[ccout.ap().opt()]),
          reads=[b_ccin], writes=[b_ccout], sem=cc_sem)
        E("gpsimd", lambda e: e.dma_start(out=A_ap, in_=ccout.ap()[:, :]),
          reads=[b_ccout], writes=[b_A], dma="auto")
        E("vector", lambda e: e.tensor_scalar(A_ap, A_ap, 0.5, 0.5, OP.mult, OP.add),
          reads=[b_A], writes=[b_A])

        for tt in range(NTT):
            E("vector", lambda e, tt=tt: e.tensor_copy(sc1[:, :], QP1[:, tt * 2048:(tt + 1) * 2048]),
              reads=[b_QP1], writes=[b_sc1])
            for half in range(2):
                ps, bp = PS[2 + half], b_ps[2 + half]
                for k in range(8):
                    kc = half * 8 + k
                    E("tensor", lambda e, kc=kc, k=k, ps=ps: e.transpose(
                        ps[:, k * 128:(k + 1) * 128], sc1[:, kc * 128:(kc + 1) * 128], identf[:, :]),
                      reads=[b_sc1, b_identf], writes=[bp])
                for k in range(8):
                    kc = half * 8 + k
                    E("scalar", lambda e, kc=kc, k=k, tt=tt, ps=ps: e.copy(
                        xnqT[:, kc * 1024 + tt * 128: kc * 1024 + (tt + 1) * 128],
                        ps[:, k * 128:(k + 1) * 128]),
                      reads=[bp], writes=[b_xnqT])

        gB = gpool[:, 0:512]
        fB = gpool[:, 512:1024]
        tA = gpool[:, 1024:1536]
        tB = gpool[:, 1536:2048]
        fbB = gpool[:, 2048:3072]

        pend_tp = []
        blkpar = [0]

        def flush_tp():
            while pend_tp:
                jb0, tl0, par0 = pend_tp.pop(0)
                pst, bpt = PS[2 + par0], b_ps[2 + par0]
                for i in range(4):
                    E("tensor", lambda e, i=i, par0=par0, pst=pst: e.transpose(
                        pst[:, i * 128:(i + 1) * 128],
                        ring[:, par0 * 512 + i * 128: par0 * 512 + (i + 1) * 128], identf[:, :]),
                      reads=[b_ring, b_identf], writes=[bpt])
                for i in range(4):
                    E("scalar", lambda e, i=i, jb0=jb0, tl0=tl0, pst=pst: e.copy(
                        actT[:, (4 * jb0 + i) * 512 + tl0 * 128: (4 * jb0 + i) * 512 + (tl0 + 1) * 128],
                        pst[:, i * 128:(i + 1) * 128]),
                      reads=[bpt], writes=[b_actT])

        for h in range(2):
            wl_toks = toks(b_wls[0]) + toks(b_wls[1])
            for jb in range(16):
                slab = (QP1, QP2)[jb % 2]
                b_slab = (b_QP1, b_QP2)[jb % 2]
                ew = tuple(wl_toks) if (jb < 2 and slab is QP1) else ()
                E("gpsimd", lambda e, jb=jb, slab=slab: e.dma_start(
                    out=slab[:, :], in_=wg_st[jb, :, :]),
                  writes=[b_slab], dma="auto", waits=ew)
                for tl in range(4):
                    tt = 4 * h + tl
                    ps, bp = PS[tl % 2], b_ps[tl % 2]
                    for kc in range(16):
                        for nh in range(2):
                            last = (kc == 15 and nh == 1)
                            E("tensor", lambda e, kc=kc, nh=nh, tt=tt, ps=ps, slab=slab: e.matmul(
                                ps[:, nh * 512:(nh + 1) * 512],
                                xnqT[:, kc * 1024 + tt * 128: kc * 1024 + (tt + 1) * 128],
                                slab[:, kc * 1024 + nh * 512: kc * 1024 + (nh + 1) * 512],
                                start=(kc == 0), stop=last),
                              reads=[b_xnqT, b_slab],
                              writes=[bp] if (kc == 0 and nh == 0) or last else [])
                    E("scalar", lambda e, ps=ps: e.copy(gB, ps[:, 0:512]), reads=[bp], writes=[b_gpool])
                    E("scalar", lambda e, ps=ps: e.copy(fB, ps[:, 512:1024]), reads=[bp], writes=[b_gpool])
                    E("vector", lambda e: e.tensor_mul(tA, gB, gB), reads=[b_gpool], writes=[b_gpool])
                    E("vector", lambda e: e.tensor_mul(tB, tA, gB), reads=[b_gpool], writes=[b_gpool])
                    E("scalar", lambda e: e.activation(tA, tB, AF.Copy, scale=GC1),
                      reads=[b_gpool], writes=[b_gpool])
                    E("vector", lambda e: e.tensor_tensor(tB, gB, tA, OP.add),
                      reads=[b_gpool], writes=[b_gpool])
                    E("scalar", lambda e: e.activation(tA, tB, AF.Copy, scale=GC2),
                      reads=[b_gpool], writes=[b_gpool])
                    E("scalar", lambda e: e.activation(tB, tA, AF.Tanh),
                      reads=[b_gpool], writes=[b_gpool])
                    E("vector", lambda e: e.tensor_scalar_add(tA, tB, 1.0),
                      reads=[b_gpool], writes=[b_gpool])
                    E("scalar", lambda e: e.activation(tB, tA, AF.Copy, scale=0.5),
                      reads=[b_gpool], writes=[b_gpool])
                    E("vector", lambda e: e.tensor_mul(tA, gB, tB), reads=[b_gpool], writes=[b_gpool])
                    par = blkpar[0]
                    blkpar[0] ^= 1
                    E("vector", lambda e, par=par: e.tensor_mul(winslab[:, par * 512:(par + 1) * 512], tA, fB),
                      reads=[b_gpool], writes=[b_winslab])
                    E("vector", lambda e, par=par: e.tensor_copy(ring[:, par * 512:(par + 1) * 512],
                                                                 winslab[:, par * 512:(par + 1) * 512]),
                      reads=[b_winslab], writes=[b_ring])
                    flush_tp()
                    pend_tp.append((jb, tl, par))
            flush_tp()
            qp1_toks = toks(b_QP1)
            first_wl = [True, True]
            for dh in range(2):
                for grp in range(8):
                    sl = grp % 2
                    ew = tuple(qp1_toks) if first_wl[sl] else ()
                    first_wl[sl] = False
                    E("gpsimd", lambda e, dh=dh, grp=grp, sl=sl: e.dma_start(
                        out=QP1[:, sl * 8192:(sl + 1) * 8192], in_=wl_st[dh, grp, :, :]),
                      writes=[b_wls[sl]], dma="auto", waits=ew)
                    for k8 in range(8):
                        kc = grp * 8 + k8
                        for tl in range(4):
                            last = (kc == 63)
                            E("tensor", lambda e, kc=kc, k8=k8, tl=tl, sl=sl: e.matmul(
                                PS[tl][:, 0:512],
                                actT[:, kc * 512 + tl * 128: kc * 512 + (tl + 1) * 128],
                                QP1[:, sl * 8192 + k8 * 1024: sl * 8192 + k8 * 1024 + 512],
                                start=(kc == 0), stop=last),
                              reads=[b_actT, b_wls[sl]],
                              writes=[b_ps[tl]] if kc == 0 else [])
                            E("tensor", lambda e, kc=kc, k8=k8, tl=tl, sl=sl: e.matmul(
                                PS[tl][:, 512:1024],
                                actT[:, kc * 512 + tl * 128: kc * 512 + (tl + 1) * 128],
                                QP1[:, sl * 8192 + k8 * 1024 + 512: sl * 8192 + (k8 + 1) * 1024],
                                start=(kc == 0), stop=last),
                              reads=[b_actT, b_wls[sl]],
                              writes=[b_ps[tl]] if last else [])
                for tl in range(4):
                    tt = 4 * h + tl
                    E("scalar", lambda e, tl=tl: e.copy(fbB, PS[tl][:, :]),
                      reads=[b_ps[tl]], writes=[b_gpool])
                    E("scalar", lambda e, tt=tt, dh=dh: e.dma_start(
                        out=XNT[:, tt * 2048 + dh * 1024: tt * 2048 + (dh + 1) * 1024].bitcast(U16),
                        in_=fbB.bitcast(U16)),
                      reads=[b_gpool], writes=[b_XNT], dma="auto")

        for kc in range(16):
            E("scalar", lambda e, kc=kc: e.copy(gpool[:, 1024:2048],
                                                QTB[:, kc * 1024:(kc + 1) * 1024]),
              reads=[b_QTB], writes=[b_gpool])
            E("scalar", lambda e, kc=kc: e.activation(QTB[:, kc * 1024:(kc + 1) * 1024],
                                                      gpool[:, 1024:2048], AF.Copy,
                                                      scale=A_ap[:, kc:kc + 1]),
              reads=[b_gpool, b_A], writes=[b_QTB])
        for db in range(16):
            E("gpsimd", lambda e, db=db: e.dma_start(out=woutsl[:, :], in_=wout_st[db, :, :]),
              writes=[b_wos], dma="auto")
            ps, bp = PS[db % 2], b_ps[db % 2]
            for kc in range(16):
                for nh in range(2):
                    last = (kc == 15 and nh == 1)
                    E("tensor", lambda e, kc=kc, nh=nh, ps=ps: e.matmul(
                        ps[:, nh * 512:(nh + 1) * 512],
                        woutsl[:, kc * 128:(kc + 1) * 128],
                        QTB[:, kc * 1024 + nh * 512: kc * 1024 + (nh + 1) * 512],
                        start=(kc == 0), stop=last),
                      reads=[b_wos, b_QTB],
                      writes=[bp] if (kc == 0 and nh == 0) or last else [])
            E("scalar", lambda e, db=db, ps=ps: e.activation(fbB, ps[:, :], AF.Identity,
                                                             bias=boutf[:, db:db + 1]),
              reads=[bp, b_boutf], writes=[b_gpool])
            E("vector", lambda e: e.tensor_copy(ring[:, 0:1024], fbB), reads=[b_gpool], writes=[b_ring])
            pst, bpt = PS[2 + (db % 2)], b_ps[2 + (db % 2)]
            for i in range(8):
                E("tensor", lambda e, i=i, pst=pst: e.transpose(
                    pst[:, i * 128:(i + 1) * 128], ring[:, i * 128:(i + 1) * 128], identf[:, :]),
                  reads=[b_ring, b_identf], writes=[bpt])
            E("scalar", lambda e, pst=pst: e.copy(stg[:, :], pst[:, :]),
              reads=[bpt], writes=[b_stg])
            E("scalar", lambda e, db=db: e.dma_start(
                out=proj_tm.reshape([8, 128, 2048]).ap()[:, :, db * 128:(db + 1) * 128].transpose([1, 0, 2]),
                in_=stg.reshape([128, 8, 128]).ap()[:, :, :]),
              reads=[b_stg], writes=[b_projtm], dma="auto")

        projslot = gpool[:, 0:2048]
        for tt in range(NTT):
            E("sync", lambda e, tt=tt: e.dma_start(out=xslot[:, :], in_=x[tt * 128:(tt + 1) * 128, :]),
              writes=[b_x], dma="auto")
            E("sync", lambda e, tt=tt: e.dma_start(out=projslot,
                                                   in_=proj_tm.ap()[tt * 128:(tt + 1) * 128, :]),
              reads=[b_projtm], writes=[b_gpool], dma="auto")
            E("vector", lambda e, tt=tt: e.tensor_scalar(sc1[:, :], xslot[:, :], mS(tt), rS(tt),
                                                         OP.subtract, OP.mult),
              reads=[b_x, b_stats], writes=[b_sc1])
            E("vector", lambda e: e.tensor_tensor(ring[:, :], sc1[:, :], xslot[:, :], OP.add),
              reads=[b_sc1, b_x], writes=[b_ring])
            E("vector", lambda e: e.tensor_tensor(sc1[:, :], ring[:, :], projslot, OP.add),
              reads=[b_ring, b_gpool], writes=[b_sc1])
            E("vector", lambda e, tt=tt: e.tensor_tensor(ring[:, :], sc1[:, :],
                                                         XNT[:, tt * 2048:(tt + 1) * 2048], OP.add),
              reads=[b_sc1, b_XNT], writes=[b_ring])
            E("sync", lambda e, tt=tt: e.dma_start(out=y[tt * 128:(tt + 1) * 128, :], in_=ring[:, :]),
              reads=[b_ring], dma="o")
        so = sch.dma_sem("o")
        sch.wait_only("gpsimd", [(so, so.count)])

        sems = sch.all_sems()
        with contextlib.ExitStack() as es2:
            for s in sems:
                s.h = es2.enter_context(nc.semaphore(s.name))
            with nc.Block() as block:
                @block.sync
                def _(e):
                    sch.run("sync", e)

                @block.vector
                def _(e):
                    sch.run("vector", e)

                @block.scalar
                def _(e):
                    sch.run("scalar", e)

                @block.tensor
                def _(e):
                    sch.run("tensor", e)

                @block.gpsimd
                def _(e):
                    sch.run("gpsimd", e)
    return nc


_NC_CACHE = [None]


def kernel(L0_attn_X, L0_attn_W_in, L0_attn_b_in, L0_attn_W_out, L0_attn_b_out,
           L0_attn_gamma, L0_attn_beta, L0_ffn_w_gating, L0_ffn_w_linear):
    X = np.ascontiguousarray(np.asarray(L0_attn_X, dtype=np.float32))
    W_in = np.asarray(L0_attn_W_in, dtype=np.float32)
    b_in = np.asarray(L0_attn_b_in, dtype=np.float32)
    W_out = np.asarray(L0_attn_W_out, dtype=np.float32)
    b_out = np.asarray(L0_attn_b_out, dtype=np.float32)
    gamma = np.asarray(L0_attn_gamma, dtype=np.float32)
    beta = np.asarray(L0_attn_beta, dtype=np.float32)
    w_g = np.asarray(L0_ffn_w_gating, dtype=np.float32)
    w_l = np.asarray(L0_ffn_w_linear, dtype=np.float32)

    cols = []
    for kc in range(16):
        cols += list(range(4096 + kc * 128, 4096 + (kc + 1) * 128))
        cols += list(range(2048 + kc * 128, 2048 + (kc + 1) * 128))
    cols += list(range(0, 2048))
    w_in_st = np.ascontiguousarray(
        W_in[:, cols].reshape(16, 128, 24, 256).transpose(2, 1, 0, 3).reshape(24, 128, 4096))
    a0 = w_g[0].reshape(16, 128, 16, 512)
    a1 = w_g[1].reshape(16, 128, 16, 512)
    wg_st = np.ascontiguousarray(
        np.concatenate([a0, a1], axis=3).transpose(2, 1, 0, 3).reshape(16, 128, 16384))
    wl_st = np.ascontiguousarray(
        w_l.reshape(8, 8, 128, 2, 1024).transpose(3, 0, 2, 1, 4).reshape(2, 8, 128, 8192))
    wout_st = np.ascontiguousarray(
        W_out.reshape(16, 128, 16, 128).transpose(2, 1, 0, 3).reshape(16, 128, 2048))
    b_in_fm = np.ascontiguousarray(b_in.reshape(48, 128).T)
    b_out_fm = np.ascontiguousarray(b_out.reshape(16, 128).T)
    gam_rep = np.ascontiguousarray(np.broadcast_to(gamma, (128, D)))
    bet_rep = np.ascontiguousarray(np.broadcast_to(beta, (128, D)))
    ident = np.eye(128, dtype=np.float32)

    shared = dict(w_in_st=w_in_st, wg_st=wg_st, wl_st=wl_st, wout_st=wout_st,
                  b_in_fm=b_in_fm, b_out_fm=b_out_fm, gam_rep=gam_rep,
                  bet_rep=bet_rep, ident=ident)
    in_maps = []
    for c in range(NCORE):
        b = c // 4
        r0 = (c % 4) * TOK
        cos_fm, sin_fm = _rotary_tables(c)
        m = dict(shared)
        m["x"] = np.ascontiguousarray(X[b, r0:r0 + TOK, :])
        m["tabs_st"] = np.ascontiguousarray(np.concatenate([cos_fm, sin_fm], axis=2))
        in_maps.append(m)

    if _NC_CACHE[0] is None:
        _NC_CACHE[0] = build_nc()
    nc = _NC_CACHE[0]
    res = run_bass_kernel_spmd(nc, in_maps, core_ids=list(range(NCORE)))
    out = np.empty((B, S, D), dtype=np.float32)
    for c in range(NCORE):
        b = c // 4
        r0 = (c % 4) * TOK
        out[b, r0:r0 + TOK, :] = res.results[c]["y"]
    return out


# version 04
# speedup vs baseline: 1.4597x; 1.0089x over previous
import contextlib
import numpy as np
import concourse.bass as bass
import concourse.mybir as mybir
from concourse.bass_utils import run_bass_kernel_spmd

F32 = mybir.dt.float32
BF16 = mybir.dt.bfloat16
FP8 = mybir.dt.float8e4
U16 = mybir.dt.uint16
AF = mybir.ActivationFunctionType
OP = mybir.AluOpType
AX = mybir.AxisListType.X

B, S, D = 2, 4096, 2048
NCORE = 8
TOK = 1024
NTT = 8
EPS = 1e-5
INV_D = float(1.0 / 2048.0)
GC1 = 0.0446777344
GC2 = 0.796875
SWAP32 = [i ^ 1 for i in range(32)]

ENGS = ("sync", "vector", "scalar", "tensor", "gpsimd")


class Sem:
    __slots__ = ("name", "count", "h")

    def __init__(self, name):
        self.name, self.count, self.h = name, 0, None


class Buf:
    __slots__ = ("name", "w", "readers")

    def __init__(self, name):
        self.name, self.w, self.readers = name, None, []


class Sched:
    def __init__(self):
        self.q = {e: [] for e in ENGS}
        self.esem = {e: Sem("E_" + e) for e in ENGS}
        self.dsem = {}
        self.xsem = []

    def all_sems(self):
        return list(self.esem.values()) + list(self.dsem.values()) + self.xsem

    def dma_sem(self, name):
        if name not in self.dsem:
            self.dsem[name] = Sem("D_" + name)
        return self.dsem[name]

    def new_sem(self, name):
        s = Sem(name)
        self.xsem.append(s)
        return s

    def emit(self, eng, build, reads=(), writes=(), dma=None, sem=None, waits=()):
        wmap = {}

        def addw(tok):
            if tok is None:
                return
            s, v = tok
            if wmap.get(s, -1) < v:
                wmap[s] = v

        for b in reads:
            addw(b.w)
        for b in writes:
            addw(b.w)
            for t in b.readers:
                addw(t)
        for t in waits:
            addw(t)
        if dma is not None:
            if dma == "auto":
                assert writes, "auto dma sem needs a write buffer"
                dma = "b_" + writes[0].name
            s = self.dma_sem(dma)
            s.count += 16
            tok = (s, s.count)
            amt = 16
        elif sem is not None:
            s = sem
            s.count += 1
            tok = (s, s.count)
            amt = 1
        else:
            s = self.esem[eng]
            s.count += 1
            tok = (s, s.count)
            amt = 1
        self.q[eng].append((build, tuple(wmap.items()), (s, amt)))
        for b in reads:
            b.readers.append(tok)
        for b in writes:
            b.w = tok
            b.readers = []
        return tok

    def wait_only(self, eng, toks):
        self.q[eng].append((None, tuple((s, v) for s, v in toks), None))

    def run(self, eng_name, engobj):
        waited = {}
        for build, waits, inc in self.q[eng_name]:
            for s, v in waits:
                if waited.get(s, 0) < v:
                    engobj.wait_ge(s.h, v)
                    waited[s] = v
            if build is None:
                continue
            ins = build(engobj)
            s, amt = inc
            ins.then_inc(s.h, amt)
            if waited.get(s, 0) < s.count:
                pass


def _rotary_tables(core):
    rot = D // 2
    inv_freq = (1.0 / (np.float32(10000.0) **
                       (np.arange(0, rot, 2, dtype=np.float32) / np.float32(rot)))).astype(np.float32)
    p0 = (core % 4) * TOK
    pos = np.arange(p0, p0 + TOK, dtype=np.float32)
    fr = (pos[:, None] * inv_freq[None, :]).astype(np.float32)
    fr = np.repeat(fr, 2, axis=1)
    cos = np.cos(fr).astype(np.float32)
    sin = np.sin(fr).astype(np.float32)
    sinS = sin.copy()
    sinS[:, 0::2] = -sin[:, 0::2]
    cos_fm = np.ascontiguousarray(cos.T.reshape(8, 128, TOK))
    sin_fm = np.ascontiguousarray(sinS.T.reshape(8, 128, TOK))
    return cos_fm, sin_fm


def build_nc():
    nc = bass.Bass("TRN2", target_bir_lowering=False, debug=False)

    def din(name, shape):
        return nc.dram_tensor(name, list(shape), F32, kind="ExternalInput").ap()

    x = din("x", (TOK, D))
    y = nc.dram_tensor("y", [TOK, D], F32, kind="ExternalOutput").ap()
    w_in_st = din("w_in_st", (24, 128, 4096))
    wg_st = din("wg_st", (16, 128, 16384))
    wl_st = din("wl_st", (2, 8, 128, 8192))
    wout_st = din("wout_st", (16, 128, 2048))
    b_in_fm = din("b_in_fm", (128, 48))
    b_out_fm = din("b_out_fm", (128, 16))
    gam_rep = din("gam_rep", (128, D))
    bet_rep = din("bet_rep", (128, D))
    tabs_st = din("tabs_st", (8, 128, 2048))
    ident = din("ident", (128, 128))
    ccin = nc.dram_tensor("ccin", [128, 16], F32)
    ccout = nc.dram_tensor("ccout", [128, 16], F32)
    proj_tm = nc.dram_tensor("proj_tm", [TOK, D], BF16)

    es = contextlib.ExitStack()
    sch = Sched()
    with es:
        T = lambda n, shp, dt: es.enter_context(nc.sbuf_tensor(n, list(shp), dt))
        xslot = T("xslot", (128, D), F32)
        xnt = T("xnt", (128, D), F32)
        sc1 = T("sc1", (128, D), F32)
        ring = T("ring", (128, D), F32)
        tab = T("tab", (128, D), F32)
        stats = T("stats", (128, 64), F32)
        winslab = T("winslab", (128, 4096), BF16)
        XNT = T("XNT", (128, 16384), BF16)
        QTB = T("QTB", (128, 16384), BF16)
        gpool = T("gpool", (128, 3072), BF16)
        vpb = T("vpb", (128, 1024), BF16)
        stg = T("stg", (128, 1024), BF16)
        identf = T("identf", (128, 128), F32)
        onesb = T("onesb", (128, 1), F32)
        onesbf = T("onesbf", (128, 1), BF16)
        ones1 = T("ones1", (1, 128), F32)
        binf = T("binf", (128, 48), F32)
        boutf = T("boutf", (128, 16), F32)
        QP1 = T("QP1", (128, 16384), FP8)
        QP2 = T("QP2", (128, 16384), FP8)
        xnqT = T("xnqT", (128, 16384), FP8)
        actT = T("actT", (128, 32768), FP8)
        woutsl = T("woutsl", (128, 2048), BF16)
        PS = [es.enter_context(nc.psum_tensor(f"ps{i}", [128, 1024], F32)) for i in range(4)]

        bufs = {}

        def Bf(n):
            bufs[n] = Buf(n)
            return bufs[n]

        b_x, b_xn, b_sc1, b_ring, b_tab = Bf("x"), Bf("xn"), Bf("sc1"), Bf("ring"), Bf("tab")
        b_small, b_stats = Bf("small"), Bf("stats")
        b_winslab, b_QTB, b_gpool = Bf("winslab"), Bf("QTB"), Bf("gpool")
        b_XNTh = [Bf("XNTh0"), Bf("XNTh1")]
        b_vpb, b_stg = Bf("vpb"), Bf("stg")
        b_identf, b_onesb, b_ones1, b_binf, b_boutf = Bf("identf"), Bf("onesb"), Bf("ones1"), Bf("binf"), Bf("boutf")
        b_onesbf = Bf("onesbf")
        b_QP1, b_QP2, b_xnqT, b_actT = Bf("QP1"), Bf("QP2"), Bf("xnqT"), Bf("actT")
        b_wls = [Bf("wls0"), Bf("wls1")]
        b_wos = Bf("wos")
        b_ps = [Bf(f"ps{i}") for i in range(4)]
        b_ccin, b_ccout, b_projtm = Bf("ccin"), Bf("ccout"), Bf("projtm")

        E = sch.emit

        def toks(buf):
            t = list(buf.readers)
            if buf.w:
                t.append(buf.w)
            return t

        E("sync", lambda e: e.dma_start(out=identf[:, :], in_=ident[:, :]), writes=[b_identf], dma="auto")
        E("sync", lambda e: e.dma_start(out=binf[:, :], in_=b_in_fm[:, :]), writes=[b_binf], dma="auto")
        E("sync", lambda e: e.dma_start(out=boutf[:, :], in_=b_out_fm[:, :]), writes=[b_boutf], dma="auto")
        E("sync", lambda e: e.dma_start(out=tab[:, :], in_=gam_rep[:, :]), writes=[b_tab], dma="auto")
        E("sync", lambda e: e.dma_start(out=ring[:, :], in_=bet_rep[:, :]), writes=[b_ring], dma="auto")
        E("vector", lambda e: e.memset(onesb[:, :], 1.0), writes=[b_onesb])
        E("vector", lambda e: e.memset(onesbf[:, :], 1.0), writes=[b_onesbf])
        E("vector", lambda e: e.memset(ones1[:, :], 1.0), writes=[b_ones1])

        mS = lambda tt: stats[:, tt:tt + 1]
        rS = lambda tt: stats[:, 8 + tt:9 + tt]
        t0 = stats[:, 16:17]; t1 = stats[:, 17:18]; t2 = stats[:, 18:19]
        t3 = stats[:, 19:20]; t4 = stats[:, 20:21]; t5 = stats[:, 21:22]
        pA_ap = stats[:, 32:48]
        A_ap = stats[:, 48:64]
        b_pA, b_A = Bf("pA"), Bf("A")

        def rsqrt_into(src_ap, dst_ap, rb, wb):
            E("scalar", lambda e: e.activation(t3, src_ap, AF.Sqrt), reads=[rb], writes=[b_stats])
            E("vector", lambda e: e.reciprocal(t4, t3), reads=[b_stats], writes=[b_stats])
            E("vector", lambda e: e.tensor_mul(t5, t4, t4), reads=[b_stats], writes=[b_stats])
            E("vector", lambda e: e.tensor_mul(t5, src_ap, t5), reads=[rb, b_stats], writes=[b_stats])
            E("vector", lambda e: e.tensor_scalar(t5, t5, -0.5, 1.5, OP.mult, OP.add),
              reads=[b_stats], writes=[b_stats])
            E("vector", lambda e: e.tensor_mul(dst_ap, t4, t5), reads=[b_stats], writes=[wb])

        for tt in range(NTT):
            E("sync", lambda e, tt=tt: e.dma_start(out=xslot[:, :], in_=x[tt * 128:(tt + 1) * 128, :]),
              writes=[b_x], dma="auto")
            E("vector", lambda e: e.reduce_sum(t0, xslot[:, :], axis=AX), reads=[b_x], writes=[b_stats])
            E("scalar", lambda e, tt=tt: e.activation(mS(tt), t0, AF.Identity, scale=INV_D),
              reads=[b_stats], writes=[b_stats])
            E("vector", lambda e, tt=tt: e.tensor_scalar(sc1[:, :], xslot[:, :], mS(tt), None, OP.subtract),
              reads=[b_x, b_stats], writes=[b_sc1])
            E("vector", lambda e: e.scalar_tensor_tensor(xnt[:, :], sc1[:, :], 1.0, sc1[:, :],
                                                         OP.mult, OP.mult, accum_out=t1),
              reads=[b_sc1], writes=[b_xn, b_stats])
            E("vector", lambda e: e.tensor_scalar(t2, t1, INV_D, float(EPS), OP.mult, OP.add),
              reads=[b_stats], writes=[b_stats])
            rsqrt_into(t2, rS(tt), b_stats, b_stats)
            E("vector", lambda e, tt=tt: e.tensor_scalar(xnt[:, :], xslot[:, :], mS(tt), rS(tt),
                                                         OP.subtract, OP.mult),
              reads=[b_x, b_stats], writes=[b_xn])
            E("scalar", lambda e, tt=tt: e.copy(QP1[:, tt * 2048:(tt + 1) * 2048], xnt[:, :]),
              reads=[b_xn], writes=[b_QP1])
            E("vector", lambda e: e.reduce_sum(t0, xnt[:, :], axis=AX), reads=[b_xn], writes=[b_stats])
            E("scalar", lambda e: e.activation(t1, t0, AF.Identity, scale=INV_D),
              reads=[b_stats], writes=[b_stats])
            E("vector", lambda e: e.tensor_scalar(sc1[:, :], xnt[:, :], t1, None, OP.subtract),
              reads=[b_xn, b_stats], writes=[b_sc1])
            E("vector", lambda e: e.scalar_tensor_tensor(xslot[:, :], sc1[:, :], 1.0, sc1[:, :],
                                                         OP.mult, OP.mult, accum_out=t2),
              reads=[b_sc1], writes=[b_x, b_stats])
            E("vector", lambda e: e.tensor_scalar(t2, t2, INV_D, float(EPS), OP.mult, OP.add),
              reads=[b_stats], writes=[b_stats])
            rsqrt_into(t2, t0, b_stats, b_stats)
            E("vector", lambda e: e.scalar_tensor_tensor(xslot[:, :], sc1[:, :], t0, tab[:, :],
                                                         OP.mult, OP.mult),
              reads=[b_sc1, b_stats, b_tab], writes=[b_x])
            E("vector", lambda e: e.tensor_tensor(sc1[:, :], xslot[:, :], ring[:, :], OP.add),
              reads=[b_x, b_ring], writes=[b_sc1])
            for half in range(2):
                ps, bp = PS[2 + half], b_ps[2 + half]
                for k in range(8):
                    kc = half * 8 + k
                    E("tensor", lambda e, kc=kc, k=k, ps=ps: e.transpose(
                        ps[:, k * 128:(k + 1) * 128], sc1[:, kc * 128:(kc + 1) * 128], identf[:, :]),
                      reads=[b_sc1, b_identf], writes=[bp])
                for k in range(8):
                    kc = half * 8 + k
                    E("scalar", lambda e, kc=kc, k=k, tt=tt, ps=ps: e.copy(
                        XNT[:, kc * 1024 + tt * 128: kc * 1024 + (tt + 1) * 128],
                        ps[:, k * 128:(k + 1) * 128]),
                      reads=[bp], writes=[b_XNTh[tt // 4]])

        def load_winslab(g):
            E("gpsimd", lambda e, g=g: e.dma_start(out=winslab[:, :], in_=w_in_st[g, :, :]),
              writes=[b_winslab], dma="auto")

        def qkv_mms(oc01, ps, bp):
            for nh in range(2):
                for kc in range(16):
                    last = (kc == 15 and nh == 1)
                    E("tensor", lambda e, oc=oc01, kc=kc, nh=nh, ps=ps: e.matmul(
                        ps[:, nh * 512:(nh + 1) * 512],
                        winslab[:, kc * 256 + oc * 128: kc * 256 + (oc + 1) * 128],
                        XNT[:, kc * 1024 + nh * 512: kc * 1024 + (nh + 1) * 512],
                        start=(kc == 0), stop=last),
                      reads=[b_winslab, b_XNTh[nh]],
                      writes=[bp] if (kc == 0 and nh == 0) or last else [])

        nv2 = xslot[0:1, 0:1024]
        nk2 = xslot[0:1, 1024:2048]
        rv_ = xslot[0:1, 0:1024]
        rk_ = xslot[0:1, 1024:2048]
        s3_ = tab[0:1, 0:1024]
        s4_ = ring[0:1, 0:1024]
        w1_ = ring[0:1, 1024:2048]
        E("vector", lambda e: e.memset(xslot[0:1, 0:2048], 0.0), writes=[b_x])

        r0ap = xnt[:, 0:1024]
        r1ap = xnt[:, 1024:2048]
        m1ap = ring[:, 0:1024]
        m2ap = ring[:, 1024:2048]
        k1ap = sc1[:, 0:1024]

        def norm_accum(dst_row, src_ap, b_src):
            for nh in range(2):
                E("tensor", lambda e, nh=nh, src_ap=src_ap: e.matmul(
                    PS[2][0:1, nh * 512:(nh + 1) * 512], onesbf[:, :],
                    src_ap[:, nh * 512:(nh + 1) * 512], start=True, stop=True),
                  reads=[b_src, b_onesbf], writes=[b_ps[2]])
            E("vector", lambda e, dst_row=dst_row: e.tensor_tensor(dst_row, dst_row, PS[2][0:1, :], OP.add),
              reads=[b_ps[2], b_x], writes=[b_x])

        pending_norms = []

        def flush_norms():
            if pending_norms:
                pending_norms.pop()
                norm_accum(nv2, woutsl[:, 0:1024], b_wos)
                norm_accum(nk2, woutsl[:, 1024:2048], b_wos)

        for g in range(24):
            load_winslab(g)
            qkv_mms(0, PS[0], b_ps[0])
            qkv_mms(1, PS[1], b_ps[1])
            flush_norms()
            if g < 16:
                kc = g
                if kc < 8:
                    E("sync", lambda e, kc=kc: e.dma_start(out=tab[:, :], in_=tabs_st[kc, :, :]),
                      writes=[b_tab], dma="auto")
                    E("scalar", lambda e, kc=kc: e.activation(r0ap, PS[0][:, :], AF.Identity,
                                                              bias=binf[:, 32 + kc:33 + kc]),
                      reads=[b_ps[0], b_binf], writes=[b_xn])
                    E("vector", lambda e: e.stream_shuffle(r1ap, r0ap, SWAP32),
                      reads=[b_xn], writes=[b_xn])
                    E("vector", lambda e: e.tensor_tensor(m1ap, r0ap, tab[:, 0:1024], OP.mult),
                      reads=[b_xn, b_tab], writes=[b_ring])
                    E("vector", lambda e: e.tensor_tensor(m2ap, r1ap, tab[:, 1024:2048], OP.mult),
                      reads=[b_xn, b_tab], writes=[b_ring])
                    E("vector", lambda e: e.tensor_tensor(vpb[:, :], m1ap, m2ap, OP.add),
                      reads=[b_ring], writes=[b_vpb])
                    E("scalar", lambda e, kc=kc: e.activation(r0ap, PS[1][:, :], AF.Identity,
                                                              bias=binf[:, 16 + kc:17 + kc]),
                      reads=[b_ps[1], b_binf], writes=[b_xn])
                    E("vector", lambda e: e.stream_shuffle(r1ap, r0ap, SWAP32),
                      reads=[b_xn], writes=[b_xn])
                    E("vector", lambda e: e.tensor_tensor(k1ap, r0ap, tab[:, 0:1024], OP.mult),
                      reads=[b_xn, b_tab], writes=[b_sc1])
                    E("vector", lambda e: e.tensor_tensor(m2ap, r1ap, tab[:, 1024:2048], OP.mult),
                      reads=[b_xn, b_tab], writes=[b_ring])
                    E("vector", lambda e: e.tensor_tensor(m1ap, k1ap, m2ap, OP.add),
                      reads=[b_sc1, b_ring], writes=[b_ring])
                else:
                    E("scalar", lambda e, kc=kc: e.activation(vpb[:, :], PS[0][:, :], AF.Identity,
                                                              bias=binf[:, 32 + kc:33 + kc]),
                      reads=[b_ps[0], b_binf], writes=[b_vpb])
                    E("scalar", lambda e, kc=kc: e.activation(m1ap, PS[1][:, :], AF.Identity,
                                                              bias=binf[:, 16 + kc:17 + kc]),
                      reads=[b_ps[1], b_binf], writes=[b_ring])
                E("vector", lambda e, kc=kc: e.tensor_tensor(QP2[:, kc * 1024:(kc + 1) * 1024],
                                                             m1ap, vpb[:, :], OP.mult),
                  reads=[b_ring, b_vpb], writes=[b_QP2])
                E("vector", lambda e: e.tensor_tensor(woutsl[:, 0:1024], vpb[:, :], vpb[:, :], OP.mult),
                  reads=[b_vpb], writes=[b_wos])
                E("vector", lambda e: e.tensor_tensor(woutsl[:, 1024:2048], m1ap, m1ap, OP.mult),
                  reads=[b_ring], writes=[b_wos])
                pending_norms.append(True)
            else:
                q = g - 16
                for j in range(2):
                    oc = 2 * q + j
                    ps, bp = PS[j], b_ps[j]
                    E("scalar", lambda e, oc=oc, ps=ps: e.activation(
                        QTB[:, oc * 1024:(oc + 1) * 1024], ps[:, :], AF.Identity,
                        bias=binf[:, oc:oc + 1]),
                      reads=[bp, b_binf], writes=[b_QTB])

        flush_norms()
        def rsqrt_row(src, dst):
            E("scalar", lambda e: e.activation(s3_, src, AF.Sqrt), reads=[b_x], writes=[b_tab])
            E("vector", lambda e: e.reciprocal(s4_, s3_), reads=[b_tab], writes=[b_ring])
            E("vector", lambda e: e.tensor_mul(s3_, s4_, s4_), reads=[b_ring], writes=[b_tab])
            E("vector", lambda e: e.tensor_mul(s3_, src, s3_), reads=[b_x, b_tab], writes=[b_tab])
            E("vector", lambda e: e.tensor_scalar(s3_, s3_, -0.5, 1.5, OP.mult, OP.add),
              reads=[b_tab], writes=[b_tab])
            E("vector", lambda e: e.tensor_mul(dst, s4_, s3_), reads=[b_ring, b_tab], writes=[b_x])

        rsqrt_row(nv2, rv_)
        rsqrt_row(nk2, rk_)
        E("vector", lambda e: e.tensor_mul(w1_, rk_, rv_), reads=[b_x], writes=[b_ring])
        for nh in range(2):
            E("tensor", lambda e, nh=nh: e.matmul(PS[2][:, nh * 512:(nh + 1) * 512], ones1[:, :],
                                                  w1_[:, nh * 512:(nh + 1) * 512], start=True, stop=True),
              reads=[b_ring, b_ones1], writes=[b_ps[2]])
        wrep = tab[:, 0:1024]
        E("vector", lambda e: e.tensor_copy(wrep, PS[2][:, :]), reads=[b_ps[2]], writes=[b_tab])
        for kc in range(16):
            E("vector", lambda e, kc=kc: e.scalar_tensor_tensor(
                ring[:, 0:1024], QP2[:, kc * 1024:(kc + 1) * 1024], 1.0, wrep,
                OP.mult, OP.mult, accum_out=pA_ap[:, kc:kc + 1]),
              reads=[b_QP2, b_tab], writes=[b_ring, b_pA])
        E("gpsimd", lambda e: e.dma_start(out=ccin.ap()[:, :], in_=pA_ap),
          reads=[b_pA], writes=[b_ccin], dma="auto")
        cc_sem = sch.new_sem("ccdone")
        E("gpsimd", lambda e: e.collective_compute(
            "AllReduce", OP.add, replica_groups=[[0, 1, 2, 3], [4, 5, 6, 7]],
            ins=[ccin.ap().opt()], outs=[ccout.ap().opt()]),
          reads=[b_ccin], writes=[b_ccout], sem=cc_sem)

        for tt in range(NTT):
            E("vector", lambda e, tt=tt: e.tensor_copy(sc1[:, :], QP1[:, tt * 2048:(tt + 1) * 2048]),
              reads=[b_QP1], writes=[b_sc1])
            for half in range(2):
                ps, bp = PS[2 + half], b_ps[2 + half]
                for k in range(8):
                    kc = half * 8 + k
                    E("tensor", lambda e, kc=kc, k=k, ps=ps: e.transpose(
                        ps[:, k * 128:(k + 1) * 128], sc1[:, kc * 128:(kc + 1) * 128], identf[:, :]),
                      reads=[b_sc1, b_identf], writes=[bp])
                for k in range(8):
                    kc = half * 8 + k
                    E("scalar", lambda e, kc=kc, k=k, tt=tt, ps=ps: e.copy(
                        xnqT[:, kc * 1024 + tt * 128: kc * 1024 + (tt + 1) * 128],
                        ps[:, k * 128:(k + 1) * 128]),
                      reads=[bp], writes=[b_xnqT])

        E("gpsimd", lambda e: e.dma_start(out=A_ap, in_=ccout.ap()[:, :]),
          reads=[b_ccout], writes=[b_A], dma="auto")
        E("vector", lambda e: e.tensor_scalar(A_ap, A_ap, 0.5, 0.5, OP.mult, OP.add),
          reads=[b_A], writes=[b_A])

        gB = gpool[:, 0:512]
        fB = gpool[:, 512:1024]
        tA = gpool[:, 1024:1536]
        tB = gpool[:, 1536:2048]
        fbB = gpool[:, 2048:3072]

        pend_tp = []
        blkpar = [0]

        def flush_tp():
            while pend_tp:
                jb0, tl0, par0 = pend_tp.pop(0)
                pst, bpt = PS[2 + par0], b_ps[2 + par0]
                for i in range(4):
                    E("tensor", lambda e, i=i, par0=par0, pst=pst: e.transpose(
                        pst[:, i * 128:(i + 1) * 128],
                        ring[:, par0 * 512 + i * 128: par0 * 512 + (i + 1) * 128], identf[:, :]),
                      reads=[b_ring, b_identf], writes=[bpt])
                for i in range(4):
                    E("scalar", lambda e, i=i, jb0=jb0, tl0=tl0, pst=pst: e.copy(
                        actT[:, (4 * jb0 + i) * 512 + tl0 * 128: (4 * jb0 + i) * 512 + (tl0 + 1) * 128],
                        pst[:, i * 128:(i + 1) * 128]),
                      reads=[bpt], writes=[b_actT])

        for h in range(2):
            wl_toks = toks(b_wls[0]) + toks(b_wls[1])
            for jb in range(16):
                slab = (QP1, QP2)[jb % 2]
                b_slab = (b_QP1, b_QP2)[jb % 2]
                ew = tuple(wl_toks) if (jb < 2 and slab is QP1) else ()
                E("gpsimd", lambda e, jb=jb, slab=slab: e.dma_start(
                    out=slab[:, :], in_=wg_st[jb, :, :]),
                  writes=[b_slab], dma="auto", waits=ew)
                for tl in range(4):
                    tt = 4 * h + tl
                    ps, bp = PS[tl % 2], b_ps[tl % 2]
                    for kc in range(16):
                        for nh in range(2):
                            last = (kc == 15 and nh == 1)
                            E("tensor", lambda e, kc=kc, nh=nh, tt=tt, ps=ps, slab=slab: e.matmul(
                                ps[:, nh * 512:(nh + 1) * 512],
                                xnqT[:, kc * 1024 + tt * 128: kc * 1024 + (tt + 1) * 128],
                                slab[:, kc * 1024 + nh * 512: kc * 1024 + (nh + 1) * 512],
                                start=(kc == 0), stop=last),
                              reads=[b_xnqT, b_slab],
                              writes=[bp] if (kc == 0 and nh == 0) or last else [])
                    E("scalar", lambda e, ps=ps: e.copy(gB, ps[:, 0:512]), reads=[bp], writes=[b_gpool])
                    E("scalar", lambda e, ps=ps: e.copy(fB, ps[:, 512:1024]), reads=[bp], writes=[b_gpool])
                    E("vector", lambda e: e.tensor_mul(tA, gB, gB), reads=[b_gpool], writes=[b_gpool])
                    E("vector", lambda e: e.tensor_mul(tB, tA, gB), reads=[b_gpool], writes=[b_gpool])
                    E("scalar", lambda e: e.activation(tA, tB, AF.Copy, scale=GC1),
                      reads=[b_gpool], writes=[b_gpool])
                    E("vector", lambda e: e.tensor_tensor(tB, gB, tA, OP.add),
                      reads=[b_gpool], writes=[b_gpool])
                    E("scalar", lambda e: e.activation(tA, tB, AF.Copy, scale=GC2),
                      reads=[b_gpool], writes=[b_gpool])
                    E("scalar", lambda e: e.activation(tB, tA, AF.Tanh),
                      reads=[b_gpool], writes=[b_gpool])
                    E("vector", lambda e: e.tensor_scalar_add(tA, tB, 1.0),
                      reads=[b_gpool], writes=[b_gpool])
                    E("scalar", lambda e: e.activation(tB, tA, AF.Copy, scale=0.5),
                      reads=[b_gpool], writes=[b_gpool])
                    E("vector", lambda e: e.tensor_mul(tA, gB, tB), reads=[b_gpool], writes=[b_gpool])
                    par = blkpar[0]
                    blkpar[0] ^= 1
                    E("vector", lambda e, par=par: e.tensor_mul(winslab[:, par * 512:(par + 1) * 512], tA, fB),
                      reads=[b_gpool], writes=[b_winslab])
                    E("vector", lambda e, par=par: e.tensor_copy(ring[:, par * 512:(par + 1) * 512],
                                                                 winslab[:, par * 512:(par + 1) * 512]),
                      reads=[b_winslab], writes=[b_ring])
                    flush_tp()
                    pend_tp.append((jb, tl, par))
            flush_tp()
            if h == 1:
                for kc in range(16):
                    E("scalar", lambda e, kc=kc: e.copy(vpb[:, :],
                                                        QTB[:, kc * 1024: kc * 1024 + 1024]),
                      reads=[b_QTB], writes=[b_vpb])
                    E("scalar", lambda e, kc=kc: e.activation(QTB[:, kc * 1024:(kc + 1) * 1024],
                                                              vpb[:, :], AF.Copy,
                                                              scale=A_ap[:, kc:kc + 1]),
                      reads=[b_vpb, b_A], writes=[b_QTB])
            qp1_toks = toks(b_QP1)
            first_wl = [True, True]
            for dh in range(2):
                for grp in range(8):
                    sl = grp % 2
                    ew = tuple(qp1_toks) if first_wl[sl] else ()
                    first_wl[sl] = False
                    E("gpsimd", lambda e, dh=dh, grp=grp, sl=sl: e.dma_start(
                        out=QP1[:, sl * 8192:(sl + 1) * 8192], in_=wl_st[dh, grp, :, :]),
                      writes=[b_wls[sl]], dma="auto", waits=ew)
                    for k8 in range(8):
                        kc = grp * 8 + k8
                        for tl in range(4):
                            last = (kc == 63)
                            E("tensor", lambda e, kc=kc, k8=k8, tl=tl, sl=sl: e.matmul(
                                PS[tl][:, 0:512],
                                actT[:, kc * 512 + tl * 128: kc * 512 + (tl + 1) * 128],
                                QP1[:, sl * 8192 + k8 * 1024: sl * 8192 + k8 * 1024 + 512],
                                start=(kc == 0), stop=last),
                              reads=[b_actT, b_wls[sl]],
                              writes=[b_ps[tl]] if kc == 0 else [])
                            E("tensor", lambda e, kc=kc, k8=k8, tl=tl, sl=sl: e.matmul(
                                PS[tl][:, 512:1024],
                                actT[:, kc * 512 + tl * 128: kc * 512 + (tl + 1) * 128],
                                QP1[:, sl * 8192 + k8 * 1024 + 512: sl * 8192 + (k8 + 1) * 1024],
                                start=(kc == 0), stop=last),
                              reads=[b_actT, b_wls[sl]],
                              writes=[b_ps[tl]] if last else [])
                for tl in range(4):
                    tt = 4 * h + tl
                    E("scalar", lambda e, tl=tl: e.copy(fbB, PS[tl][:, :]),
                      reads=[b_ps[tl]], writes=[b_gpool])
                    E("scalar", lambda e, tt=tt, dh=dh: e.dma_start(
                        out=XNT[:, tt * 2048 + dh * 1024: tt * 2048 + (dh + 1) * 1024].bitcast(U16),
                        in_=fbB.bitcast(U16)),
                      reads=[b_gpool], writes=[b_XNTh[tt // 4]], dma="auto")

        for db in range(16):
            E("gpsimd", lambda e, db=db: e.dma_start(out=woutsl[:, :], in_=wout_st[db, :, :]),
              writes=[b_wos], dma="auto")
            ps, bp = PS[db % 2], b_ps[db % 2]
            for kc in range(16):
                for nh in range(2):
                    last = (kc == 15 and nh == 1)
                    E("tensor", lambda e, kc=kc, nh=nh, ps=ps: e.matmul(
                        ps[:, nh * 512:(nh + 1) * 512],
                        woutsl[:, kc * 128:(kc + 1) * 128],
                        QTB[:, kc * 1024 + nh * 512: kc * 1024 + (nh + 1) * 512],
                        start=(kc == 0), stop=last),
                      reads=[b_wos, b_QTB],
                      writes=[bp] if (kc == 0 and nh == 0) or last else [])
            E("scalar", lambda e, db=db, ps=ps: e.activation(fbB, ps[:, :], AF.Identity,
                                                             bias=boutf[:, db:db + 1]),
              reads=[bp, b_boutf], writes=[b_gpool])
            E("vector", lambda e: e.tensor_copy(ring[:, 0:1024], fbB), reads=[b_gpool], writes=[b_ring])
            pst, bpt = PS[2 + (db % 2)], b_ps[2 + (db % 2)]
            for i in range(8):
                E("tensor", lambda e, i=i, pst=pst: e.transpose(
                    pst[:, i * 128:(i + 1) * 128], ring[:, i * 128:(i + 1) * 128], identf[:, :]),
                  reads=[b_ring, b_identf], writes=[bpt])
            E("scalar", lambda e, pst=pst: e.copy(stg[:, :], pst[:, :]),
              reads=[bpt], writes=[b_stg])
            E("scalar", lambda e, db=db: e.dma_start(
                out=proj_tm.reshape([8, 128, 2048]).ap()[:, :, db * 128:(db + 1) * 128].transpose([1, 0, 2]),
                in_=stg.reshape([128, 8, 128]).ap()[:, :, :]),
              reads=[b_stg], writes=[b_projtm], dma="auto")

        projslot = gpool[:, 0:2048]
        for tt in range(NTT):
            E("sync", lambda e, tt=tt: e.dma_start(out=xslot[:, :], in_=x[tt * 128:(tt + 1) * 128, :]),
              writes=[b_x], dma="auto")
            E("sync", lambda e, tt=tt: e.dma_start(out=projslot,
                                                   in_=proj_tm.ap()[tt * 128:(tt + 1) * 128, :]),
              reads=[b_projtm], writes=[b_gpool], dma="auto")
            E("vector", lambda e, tt=tt: e.tensor_scalar(sc1[:, :], xslot[:, :], mS(tt), rS(tt),
                                                         OP.subtract, OP.mult),
              reads=[b_x, b_stats], writes=[b_sc1])
            E("vector", lambda e: e.tensor_tensor(ring[:, :], sc1[:, :], xslot[:, :], OP.add),
              reads=[b_sc1, b_x], writes=[b_ring])
            E("vector", lambda e: e.tensor_tensor(sc1[:, :], ring[:, :], projslot, OP.add),
              reads=[b_ring, b_gpool], writes=[b_sc1])
            E("vector", lambda e, tt=tt: e.tensor_tensor(ring[:, :], sc1[:, :],
                                                         XNT[:, tt * 2048:(tt + 1) * 2048], OP.add),
              reads=[b_sc1, b_XNTh[tt // 4]], writes=[b_ring])
            E("sync", lambda e, tt=tt: e.dma_start(out=y[tt * 128:(tt + 1) * 128, :], in_=ring[:, :]),
              reads=[b_ring], dma="o")
        so = sch.dma_sem("o")
        sch.wait_only("gpsimd", [(so, so.count)])

        sems = sch.all_sems()
        with contextlib.ExitStack() as es2:
            for s in sems:
                s.h = es2.enter_context(nc.semaphore(s.name))
            with nc.Block() as block:
                @block.sync
                def _(e):
                    sch.run("sync", e)

                @block.vector
                def _(e):
                    sch.run("vector", e)

                @block.scalar
                def _(e):
                    sch.run("scalar", e)

                @block.tensor
                def _(e):
                    sch.run("tensor", e)

                @block.gpsimd
                def _(e):
                    sch.run("gpsimd", e)
    return nc


_NC_CACHE = [None]


def kernel(L0_attn_X, L0_attn_W_in, L0_attn_b_in, L0_attn_W_out, L0_attn_b_out,
           L0_attn_gamma, L0_attn_beta, L0_ffn_w_gating, L0_ffn_w_linear):
    X = np.ascontiguousarray(np.asarray(L0_attn_X, dtype=np.float32))
    W_in = np.asarray(L0_attn_W_in, dtype=np.float32)
    b_in = np.asarray(L0_attn_b_in, dtype=np.float32)
    W_out = np.asarray(L0_attn_W_out, dtype=np.float32)
    b_out = np.asarray(L0_attn_b_out, dtype=np.float32)
    gamma = np.asarray(L0_attn_gamma, dtype=np.float32)
    beta = np.asarray(L0_attn_beta, dtype=np.float32)
    w_g = np.asarray(L0_ffn_w_gating, dtype=np.float32)
    w_l = np.asarray(L0_ffn_w_linear, dtype=np.float32)

    cols = []
    for kc in range(16):
        cols += list(range(4096 + kc * 128, 4096 + (kc + 1) * 128))
        cols += list(range(2048 + kc * 128, 2048 + (kc + 1) * 128))
    cols += list(range(0, 2048))
    w_in_st = np.ascontiguousarray(
        W_in[:, cols].reshape(16, 128, 24, 256).transpose(2, 1, 0, 3).reshape(24, 128, 4096))
    a0 = w_g[0].reshape(16, 128, 16, 512)
    a1 = w_g[1].reshape(16, 128, 16, 512)
    wg_st = np.ascontiguousarray(
        np.concatenate([a0, a1], axis=3).transpose(2, 1, 0, 3).reshape(16, 128, 16384))
    wl_st = np.ascontiguousarray(
        w_l.reshape(8, 8, 128, 2, 1024).transpose(3, 0, 2, 1, 4).reshape(2, 8, 128, 8192))
    wout_st = np.ascontiguousarray(
        W_out.reshape(16, 128, 16, 128).transpose(2, 1, 0, 3).reshape(16, 128, 2048))
    b_in_fm = np.ascontiguousarray(b_in.reshape(48, 128).T)
    b_out_fm = np.ascontiguousarray(b_out.reshape(16, 128).T)
    gam_rep = np.ascontiguousarray(np.broadcast_to(gamma, (128, D)))
    bet_rep = np.ascontiguousarray(np.broadcast_to(beta, (128, D)))
    ident = np.eye(128, dtype=np.float32)

    shared = dict(w_in_st=w_in_st, wg_st=wg_st, wl_st=wl_st, wout_st=wout_st,
                  b_in_fm=b_in_fm, b_out_fm=b_out_fm, gam_rep=gam_rep,
                  bet_rep=bet_rep, ident=ident)
    in_maps = []
    for c in range(NCORE):
        b = c // 4
        r0 = (c % 4) * TOK
        cos_fm, sin_fm = _rotary_tables(c)
        m = dict(shared)
        m["x"] = np.ascontiguousarray(X[b, r0:r0 + TOK, :])
        m["tabs_st"] = np.ascontiguousarray(np.concatenate([cos_fm, sin_fm], axis=2))
        in_maps.append(m)

    if _NC_CACHE[0] is None:
        _NC_CACHE[0] = build_nc()
    nc = _NC_CACHE[0]
    res = run_bass_kernel_spmd(nc, in_maps, core_ids=list(range(NCORE)))
    out = np.empty((B, S, D), dtype=np.float32)
    for c in range(NCORE):
        b = c // 4
        r0 = (c % 4) * TOK
        out[b, r0:r0 + TOK, :] = res.results[c]["y"]
    return out


# version 06
# speedup vs baseline: 2.0162x; 1.3812x over previous
import contextlib
import numpy as np
import concourse.bass as bass
import concourse.mybir as mybir
from concourse.bass_utils import run_bass_kernel_spmd

F32 = mybir.dt.float32
BF16 = mybir.dt.bfloat16
FP8 = mybir.dt.float8e4
U16 = mybir.dt.uint16
AF = mybir.ActivationFunctionType
OP = mybir.AluOpType
AX = mybir.AxisListType.X

B, S, D = 2, 4096, 2048
NCORE = 8
TOK = 1024
NTT = 8
EPS = 1e-5
INV_D = float(1.0 / 2048.0)
GC1 = 0.0446777344
GC2 = 0.796875
SWAP32 = [i ^ 1 for i in range(32)]

ENGS = ("sync", "vector", "scalar", "tensor", "gpsimd")


class Sem:
    __slots__ = ("name", "count", "h")

    def __init__(self, name):
        self.name, self.count, self.h = name, 0, None


class Buf:
    __slots__ = ("name", "w", "readers")

    def __init__(self, name):
        self.name, self.w, self.readers = name, None, []


class Sched:
    def __init__(self):
        self.q = {e: [] for e in ENGS}
        self.esem = {e: Sem("E_" + e) for e in ENGS}
        self.dsem = {}
        self.xsem = []

    def all_sems(self):
        return list(self.esem.values()) + list(self.dsem.values()) + self.xsem

    def dma_sem(self, name):
        if name not in self.dsem:
            self.dsem[name] = Sem("D_" + name)
        return self.dsem[name]

    def new_sem(self, name):
        s = Sem(name)
        self.xsem.append(s)
        return s

    def emit(self, eng, build, reads=(), writes=(), dma=None, sem=None, waits=()):
        wmap = {}

        def addw(tok):
            if tok is None:
                return
            s, v = tok
            if wmap.get(s, -1) < v:
                wmap[s] = v

        for b in reads:
            addw(b.w)
        for b in writes:
            addw(b.w)
            for t in b.readers:
                addw(t)
        for t in waits:
            addw(t)
        if dma is not None:
            if dma == "auto":
                assert writes, "auto dma sem needs a write buffer"
                dma = "b_" + writes[0].name
            s = self.dma_sem(dma)
            s.count += 16
            tok = (s, s.count)
            amt = 16
        elif sem is not None:
            s = sem
            s.count += 1
            tok = (s, s.count)
            amt = 1
        else:
            s = self.esem[eng]
            s.count += 1
            tok = (s, s.count)
            amt = 1
        self.q[eng].append((build, tuple(wmap.items()), (s, amt)))
        for b in reads:
            b.readers.append(tok)
        for b in writes:
            b.w = tok
            b.readers = []
        return tok

    def wait_only(self, eng, toks):
        self.q[eng].append((None, tuple((s, v) for s, v in toks), None))

    def run(self, eng_name, engobj):
        waited = {}
        for build, waits, inc in self.q[eng_name]:
            for s, v in waits:
                if waited.get(s, 0) < v:
                    engobj.wait_ge(s.h, v)
                    waited[s] = v
            if build is None:
                continue
            ins = build(engobj)
            s, amt = inc
            ins.then_inc(s.h, amt)
            if waited.get(s, 0) < s.count:
                pass


def _rotary_tables(core):
    rot = D // 2
    inv_freq = (1.0 / (np.float32(10000.0) **
                       (np.arange(0, rot, 2, dtype=np.float32) / np.float32(rot)))).astype(np.float32)
    p0 = (core % 4) * TOK
    pos = np.arange(p0, p0 + TOK, dtype=np.float32)
    fr = (pos[:, None] * inv_freq[None, :]).astype(np.float32)
    fr = np.repeat(fr, 2, axis=1)
    cos = np.cos(fr).astype(np.float32)
    sin = np.sin(fr).astype(np.float32)
    sinS = sin.copy()
    sinS[:, 0::2] = -sin[:, 0::2]
    cos_fm = np.ascontiguousarray(cos.T.reshape(8, 128, TOK))
    sin_fm = np.ascontiguousarray(sinS.T.reshape(8, 128, TOK))
    return cos_fm, sin_fm


def build_nc():
    nc = bass.Bass("TRN2", target_bir_lowering=False, debug=False)

    def din(name, shape):
        return nc.dram_tensor(name, list(shape), F32, kind="ExternalInput").ap()

    x = din("x", (TOK, D))
    y = nc.dram_tensor("y", [TOK, D], F32, kind="ExternalOutput").ap()
    w_in_st = din("w_in_st", (24, 128, 4096))
    wg_st = din("wg_st", (16, 128, 16384))
    wl_st = din("wl_st", (2, 8, 128, 8192))
    wout_st = din("wout_st", (16, 128, 2048))
    b_in_fm = din("b_in_fm", (128, 48))
    b_out_fm = din("b_out_fm", (128, 16))
    gam_rep = din("gam_rep", (128, D))
    bet_rep = din("bet_rep", (128, D))
    tabs_st = din("tabs_st", (8, 128, 2048))
    ident = din("ident", (128, 128))
    ccin = nc.dram_tensor("ccin", [128, 16], F32)
    ccout = nc.dram_tensor("ccout", [128, 16], F32)
    proj_tm = nc.dram_tensor("proj_tm", [TOK, D], BF16)

    es = contextlib.ExitStack()
    sch = Sched()
    with es:
        T = lambda n, shp, dt: es.enter_context(nc.sbuf_tensor(n, list(shp), dt))
        xslot = T("xslot", (128, D), F32)
        xnt = T("xnt", (128, D), F32)
        sc1 = T("sc1", (128, D), F32)
        ring = T("ring", (128, D), F32)
        tab = T("tab", (128, D), F32)
        stats = T("stats", (128, 64), F32)
        winslab = T("winslab", (128, 4096), BF16)
        XNT = T("XNT", (128, 16384), BF16)
        QTB = T("QTB", (128, 16384), BF16)
        gpool = T("gpool", (128, 3072), BF16)
        vpb = T("vpb", (128, 1024), BF16)
        stg = T("stg", (128, 1024), BF16)
        identf = T("identf", (128, 128), F32)
        onesb = T("onesb", (128, 1), F32)
        onesbf = T("onesbf", (128, 1), BF16)
        ones1 = T("ones1", (1, 128), F32)
        binf = T("binf", (128, 48), F32)
        boutf = T("boutf", (128, 16), F32)
        QP1 = T("QP1", (128, 16384), FP8)
        QP2 = T("QP2", (128, 16384), FP8)
        xnqT = T("xnqT", (128, 16384), FP8)
        actT = T("actT", (128, 32768), FP8)
        woutsl = T("woutsl", (128, 2048), BF16)
        PS = [es.enter_context(nc.psum_tensor(f"ps{i}", [128, 1024], F32)) for i in range(4)]

        bufs = {}

        def Bf(n):
            bufs[n] = Buf(n)
            return bufs[n]

        b_x, b_xn, b_sc1, b_ring, b_tab = Bf("x"), Bf("xn"), Bf("sc1"), Bf("ring"), Bf("tab")
        b_small, b_stats = Bf("small"), Bf("stats")
        b_winslab, b_QTB, b_gpool = Bf("winslab"), Bf("QTB"), Bf("gpool")
        b_XNTh = [Bf("XNTh0"), Bf("XNTh1")]
        b_vpb, b_stg = Bf("vpb"), Bf("stg")
        b_rh = [Bf("rh0"), Bf("rh1")]
        b_gel = [Bf("gel0"), Bf("gel1")]
        b_identf, b_onesb, b_ones1, b_binf, b_boutf = Bf("identf"), Bf("onesb"), Bf("ones1"), Bf("binf"), Bf("boutf")
        b_onesbf = Bf("onesbf")
        b_QP1, b_QP2, b_xnqT, b_actT = Bf("QP1"), Bf("QP2"), Bf("xnqT"), Bf("actT")
        b_wls = [Bf("wls0"), Bf("wls1")]
        b_wos = Bf("wos")
        b_ps = [Bf(f"ps{i}") for i in range(4)]
        b_ccin, b_ccout, b_projtm = Bf("ccin"), Bf("ccout"), Bf("projtm")

        E = sch.emit

        def toks(buf):
            t = list(buf.readers)
            if buf.w:
                t.append(buf.w)
            return t

        E("sync", lambda e: e.dma_start(out=identf[:, :], in_=ident[:, :]), writes=[b_identf], dma="auto")
        E("sync", lambda e: e.dma_start(out=binf[:, :], in_=b_in_fm[:, :]), writes=[b_binf], dma="auto")
        E("sync", lambda e: e.dma_start(out=boutf[:, :], in_=b_out_fm[:, :]), writes=[b_boutf], dma="auto")
        E("gpsimd", lambda e: e.dma_start(out=gpool[:, 0:2048], in_=gam_rep[:, :]), writes=[b_gpool], dma="auto")
        E("gpsimd", lambda e: e.dma_start(out=winslab[:, 0:2048], in_=bet_rep[:, :]), writes=[b_winslab], dma="auto")
        E("vector", lambda e: e.memset(onesb[:, :], 1.0), writes=[b_onesb])
        E("vector", lambda e: e.memset(onesbf[:, :], 1.0), writes=[b_onesbf])
        E("vector", lambda e: e.memset(ones1[:, :], 1.0), writes=[b_ones1])

        mS = lambda tt: stats[:, tt:tt + 1]
        rS = lambda tt: stats[:, 8 + tt:9 + tt]
        t0 = stats[:, 16:17]; t1 = stats[:, 17:18]; t2 = stats[:, 18:19]
        t3 = stats[:, 19:20]; t4 = stats[:, 20:21]; t5 = stats[:, 21:22]
        pA_ap = stats[:, 32:48]
        A_ap = stats[:, 48:64]
        b_pA, b_A = Bf("pA"), Bf("A")

        def rsqrt_into(src_ap, dst_ap, rb, wb):
            E("scalar", lambda e: e.activation(t3, src_ap, AF.Sqrt), reads=[rb], writes=[b_stats])
            E("vector", lambda e: e.reciprocal(t4, t3), reads=[b_stats], writes=[b_stats])
            E("vector", lambda e: e.tensor_mul(t5, t4, t4), reads=[b_stats], writes=[b_stats])
            E("vector", lambda e: e.tensor_mul(t5, src_ap, t5), reads=[rb, b_stats], writes=[b_stats])
            E("vector", lambda e: e.tensor_scalar(t5, t5, -0.5, 1.5, OP.mult, OP.add),
              reads=[b_stats], writes=[b_stats])
            E("vector", lambda e: e.tensor_mul(dst_ap, t4, t5), reads=[b_stats], writes=[wb])

        sS = stats[:, 16:17]; qS = stats[:, 17:18]; uS = stats[:, 18:19]
        m2S = stats[:, 19:20]; u2S = stats[:, 20:21]; w0 = stats[:, 21:22]
        w1s = stats[:, 22:23]; w2 = stats[:, 23:24]

        def rsq(src_ap, dst_ap):
            E("scalar", lambda e: e.activation(w0, src_ap, AF.Sqrt), reads=[b_stats], writes=[b_stats])
            E("vector", lambda e: e.reciprocal(w1s, w0), reads=[b_stats], writes=[b_stats])
            E("vector", lambda e: e.tensor_mul(w2, w1s, w1s), reads=[b_stats], writes=[b_stats])
            E("vector", lambda e: e.tensor_mul(w2, src_ap, w2), reads=[b_stats], writes=[b_stats])
            E("vector", lambda e: e.tensor_scalar(w2, w2, -0.5, 1.5, OP.mult, OP.add),
              reads=[b_stats], writes=[b_stats])
            E("vector", lambda e: e.tensor_mul(dst_ap, w1s, w2), reads=[b_stats], writes=[b_stats])

        for tt in range(NTT):
            E("sync", lambda e, tt=tt: e.dma_start(out=xslot[:, :], in_=x[tt * 128:(tt + 1) * 128, :]),
              writes=[b_x], dma="auto")
            E("vector", lambda e: e.reduce_sum(sS, xslot[:, :], axis=AX), reads=[b_x], writes=[b_stats])
            E("vector", lambda e: e.scalar_tensor_tensor(ring[:, :], xslot[:, :], 1.0, xslot[:, :],
                                                         OP.mult, OP.mult, accum_out=qS),
              reads=[b_x], writes=[b_ring, b_stats])
            E("scalar", lambda e, tt=tt: e.activation(mS(tt), sS, AF.Identity, scale=INV_D),
              reads=[b_stats], writes=[b_stats])
            E("vector", lambda e: e.tensor_scalar(uS, qS, INV_D, float(EPS), OP.mult, OP.add),
              reads=[b_stats], writes=[b_stats])
            E("vector", lambda e, tt=tt: e.tensor_mul(w0, mS(tt), mS(tt)), reads=[b_stats], writes=[b_stats])
            E("vector", lambda e: e.tensor_sub(uS, uS, w0), reads=[b_stats], writes=[b_stats])
            rsq(uS, rS(tt))
            E("vector", lambda e, tt=tt: e.tensor_scalar(xnt[:, :], xslot[:, :], mS(tt), rS(tt),
                                                         OP.subtract, OP.mult),
              reads=[b_x, b_stats], writes=[b_xn])
            E("scalar", lambda e, tt=tt: e.copy(QP1[:, tt * 2048:(tt + 1) * 2048], xnt[:, :]),
              reads=[b_xn], writes=[b_QP1])
            E("vector", lambda e: e.tensor_scalar(w0, sS, INV_D, None, OP.mult),
              reads=[b_stats], writes=[b_stats])
            E("vector", lambda e, tt=tt: e.tensor_sub(w0, w0, mS(tt)), reads=[b_stats], writes=[b_stats])
            E("vector", lambda e, tt=tt: e.tensor_mul(m2S, w0, rS(tt)), reads=[b_stats], writes=[b_stats])
            E("vector", lambda e: e.tensor_scalar(w0, uS, 1.0, -float(EPS), OP.mult, OP.add),
              reads=[b_stats], writes=[b_stats])
            E("vector", lambda e, tt=tt: e.tensor_mul(w1s, rS(tt), rS(tt)), reads=[b_stats], writes=[b_stats])
            E("vector", lambda e: e.tensor_mul(w0, w0, w1s), reads=[b_stats], writes=[b_stats])
            E("vector", lambda e: e.tensor_mul(w1s, m2S, m2S), reads=[b_stats], writes=[b_stats])
            E("vector", lambda e: e.tensor_sub(w0, w0, w1s), reads=[b_stats], writes=[b_stats])
            E("vector", lambda e: e.tensor_scalar(u2S, w0, 1.0, float(EPS), OP.mult, OP.add),
              reads=[b_stats], writes=[b_stats])
            rsq(u2S, w2)
            E("vector", lambda e: e.scalar_tensor_tensor(ring[:, :], xnt[:, :], m2S, gpool[:, 0:2048],
                                                         OP.subtract, OP.mult),
              reads=[b_xn, b_stats, b_gpool], writes=[b_ring])
            E("vector", lambda e: e.tensor_scalar(ring[:, :], ring[:, :], w2, None, OP.mult),
              reads=[b_ring, b_stats], writes=[b_ring])
            E("vector", lambda e: e.tensor_tensor(sc1[:, :], ring[:, :], winslab[:, 0:2048], OP.add),
              reads=[b_ring, b_winslab], writes=[b_sc1])
            for half in range(2):
                ps, bp = PS[2 + half], b_ps[2 + half]
                for k in range(8):
                    kc = half * 8 + k
                    E("tensor", lambda e, kc=kc, k=k, ps=ps: e.transpose(
                        ps[:, k * 128:(k + 1) * 128], sc1[:, kc * 128:(kc + 1) * 128], identf[:, :]),
                      reads=[b_sc1, b_identf], writes=[bp])
                for k in range(8):
                    kc = half * 8 + k
                    E("scalar", lambda e, kc=kc, k=k, tt=tt, ps=ps: e.copy(
                        XNT[:, kc * 1024 + tt * 128: kc * 1024 + (tt + 1) * 128],
                        ps[:, k * 128:(k + 1) * 128]),
                      reads=[bp], writes=[b_XNTh[tt // 4]])

        def load_winslab(g):
            E("gpsimd", lambda e, g=g: e.dma_start(out=winslab[:, :], in_=w_in_st[g, :, :]),
              writes=[b_winslab], dma="auto")

        def qkv_mms(oc01, ps, bp):
            for nh in range(2):
                for kc in range(16):
                    last = (kc == 15 and nh == 1)
                    E("tensor", lambda e, oc=oc01, kc=kc, nh=nh, ps=ps: e.matmul(
                        ps[:, nh * 512:(nh + 1) * 512],
                        winslab[:, kc * 256 + oc * 128: kc * 256 + (oc + 1) * 128],
                        XNT[:, kc * 1024 + nh * 512: kc * 1024 + (nh + 1) * 512],
                        start=(kc == 0), stop=last),
                      reads=[b_winslab, b_XNTh[nh]],
                      writes=[bp] if (kc == 0 and nh == 0) or last else [])

        nv2 = xslot[0:1, 0:1024]
        nk2 = xslot[0:1, 1024:2048]
        rv_ = xslot[0:1, 0:1024]
        rk_ = xslot[0:1, 1024:2048]
        s3_ = tab[0:1, 0:1024]
        s4_ = ring[0:1, 0:1024]
        w1_ = ring[0:1, 1024:2048]
        E("vector", lambda e: e.memset(xslot[0:1, 0:2048], 0.0), writes=[b_x])

        r0ap = xnt[:, 0:1024]
        r1ap = xnt[:, 1024:2048]
        m1ap = ring[:, 0:1024]
        m2ap = ring[:, 1024:2048]
        k1ap = sc1[:, 0:1024]

        def norm_accum(dst_row, src_ap, b_src):
            for nh in range(2):
                E("tensor", lambda e, nh=nh, src_ap=src_ap: e.matmul(
                    PS[2][0:1, nh * 512:(nh + 1) * 512], onesbf[:, :],
                    src_ap[:, nh * 512:(nh + 1) * 512], start=True, stop=True),
                  reads=[b_src, b_onesbf], writes=[b_ps[2]])
            E("vector", lambda e, dst_row=dst_row: e.tensor_tensor(dst_row, dst_row, PS[2][0:1, :], OP.add),
              reads=[b_ps[2], b_x], writes=[b_x])

        pending_norms = []

        def flush_norms():
            if pending_norms:
                pending_norms.pop()
                norm_accum(nv2, woutsl[:, 0:1024], b_wos)
                norm_accum(nk2, woutsl[:, 1024:2048], b_wos)

        for g in range(24):
            load_winslab(g)
            qkv_mms(0, PS[0], b_ps[0])
            qkv_mms(1, PS[1], b_ps[1])
            flush_norms()
            if g < 16:
                kc = g
                if kc < 8:
                    E("sync", lambda e, kc=kc: e.dma_start(out=tab[:, :], in_=tabs_st[kc, :, :]),
                      writes=[b_tab], dma="auto")
                    E("scalar", lambda e, kc=kc: e.activation(r0ap, PS[0][:, :], AF.Identity,
                                                              bias=binf[:, 32 + kc:33 + kc]),
                      reads=[b_ps[0], b_binf], writes=[b_xn])
                    E("vector", lambda e: e.stream_shuffle(r1ap, r0ap, SWAP32),
                      reads=[b_xn], writes=[b_xn])
                    E("vector", lambda e: e.tensor_tensor(m1ap, r0ap, tab[:, 0:1024], OP.mult),
                      reads=[b_xn, b_tab], writes=[b_ring])
                    E("vector", lambda e: e.tensor_tensor(m2ap, r1ap, tab[:, 1024:2048], OP.mult),
                      reads=[b_xn, b_tab], writes=[b_ring])
                    E("vector", lambda e: e.tensor_tensor(vpb[:, :], m1ap, m2ap, OP.add),
                      reads=[b_ring], writes=[b_vpb])
                    E("scalar", lambda e, kc=kc: e.activation(r0ap, PS[1][:, :], AF.Identity,
                                                              bias=binf[:, 16 + kc:17 + kc]),
                      reads=[b_ps[1], b_binf], writes=[b_xn])
                    E("vector", lambda e: e.stream_shuffle(r1ap, r0ap, SWAP32),
                      reads=[b_xn], writes=[b_xn])
                    E("vector", lambda e: e.tensor_tensor(k1ap, r0ap, tab[:, 0:1024], OP.mult),
                      reads=[b_xn, b_tab], writes=[b_sc1])
                    E("vector", lambda e: e.tensor_tensor(m2ap, r1ap, tab[:, 1024:2048], OP.mult),
                      reads=[b_xn, b_tab], writes=[b_ring])
                    E("vector", lambda e: e.tensor_tensor(m1ap, k1ap, m2ap, OP.add),
                      reads=[b_sc1, b_ring], writes=[b_ring])
                else:
                    E("scalar", lambda e, kc=kc: e.activation(vpb[:, :], PS[0][:, :], AF.Identity,
                                                              bias=binf[:, 32 + kc:33 + kc]),
                      reads=[b_ps[0], b_binf], writes=[b_vpb])
                    E("scalar", lambda e, kc=kc: e.activation(m1ap, PS[1][:, :], AF.Identity,
                                                              bias=binf[:, 16 + kc:17 + kc]),
                      reads=[b_ps[1], b_binf], writes=[b_ring])
                E("vector", lambda e, kc=kc: e.tensor_tensor(QP2[:, kc * 1024:(kc + 1) * 1024],
                                                             m1ap, vpb[:, :], OP.mult),
                  reads=[b_ring, b_vpb], writes=[b_QP2])
                E("vector", lambda e: e.tensor_tensor(woutsl[:, 0:1024], vpb[:, :], vpb[:, :], OP.mult),
                  reads=[b_vpb], writes=[b_wos])
                E("vector", lambda e: e.tensor_tensor(woutsl[:, 1024:2048], m1ap, m1ap, OP.mult),
                  reads=[b_ring], writes=[b_wos])
                pending_norms.append(True)
            else:
                q = g - 16
                for j in range(2):
                    oc = 2 * q + j
                    ps, bp = PS[j], b_ps[j]
                    E("scalar", lambda e, oc=oc, ps=ps: e.activation(
                        QTB[:, oc * 1024:(oc + 1) * 1024], ps[:, :], AF.Identity,
                        bias=binf[:, oc:oc + 1]),
                      reads=[bp, b_binf], writes=[b_QTB])

        flush_norms()
        for tt in range(NTT):
            E("vector", lambda e, tt=tt: e.tensor_copy(sc1[:, :], QP1[:, tt * 2048:(tt + 1) * 2048]),
              reads=[b_QP1], writes=[b_sc1])
            for half in range(2):
                ps, bp = PS[2 + half], b_ps[2 + half]
                for k in range(8):
                    kc = half * 8 + k
                    E("tensor", lambda e, kc=kc, k=k, ps=ps: e.transpose(
                        ps[:, k * 128:(k + 1) * 128], sc1[:, kc * 128:(kc + 1) * 128], identf[:, :]),
                      reads=[b_sc1, b_identf], writes=[bp])
                for k in range(8):
                    kc = half * 8 + k
                    E("scalar", lambda e, kc=kc, k=k, tt=tt, ps=ps: e.copy(
                        xnqT[:, kc * 1024 + tt * 128: kc * 1024 + (tt + 1) * 128],
                        ps[:, k * 128:(k + 1) * 128]),
                      reads=[bp], writes=[b_xnqT])

        def rsqrt_row(src, dst):
            E("scalar", lambda e: e.activation(s3_, src, AF.Sqrt), reads=[b_x], writes=[b_tab])
            E("vector", lambda e: e.reciprocal(s4_, s3_), reads=[b_tab], writes=[b_ring])
            E("vector", lambda e: e.tensor_mul(s3_, s4_, s4_), reads=[b_ring], writes=[b_tab])
            E("vector", lambda e: e.tensor_mul(s3_, src, s3_), reads=[b_x, b_tab], writes=[b_tab])
            E("vector", lambda e: e.tensor_scalar(s3_, s3_, -0.5, 1.5, OP.mult, OP.add),
              reads=[b_tab], writes=[b_tab])
            E("vector", lambda e: e.tensor_mul(dst, s4_, s3_), reads=[b_ring, b_tab], writes=[b_x])

        rsqrt_row(nv2, rv_)
        rsqrt_row(nk2, rk_)
        E("vector", lambda e: e.tensor_mul(w1_, rk_, rv_), reads=[b_x], writes=[b_ring])
        for nh in range(2):
            E("tensor", lambda e, nh=nh: e.matmul(PS[2][:, nh * 512:(nh + 1) * 512], ones1[:, :],
                                                  w1_[:, nh * 512:(nh + 1) * 512], start=True, stop=True),
              reads=[b_ring, b_ones1], writes=[b_ps[2]])
        wrep = tab[:, 0:1024]
        E("vector", lambda e: e.tensor_copy(wrep, PS[2][:, :]), reads=[b_ps[2]], writes=[b_tab])
        for kc in range(16):
            E("vector", lambda e, kc=kc: e.scalar_tensor_tensor(
                ring[:, 0:1024], QP2[:, kc * 1024:(kc + 1) * 1024], 1.0, wrep,
                OP.mult, OP.mult, accum_out=pA_ap[:, kc:kc + 1]),
              reads=[b_QP2, b_tab], writes=[b_ring, b_pA])
        E("gpsimd", lambda e: e.dma_start(out=ccin.ap()[:, :], in_=pA_ap),
          reads=[b_pA], writes=[b_ccin], dma="auto")
        cc_sem = sch.new_sem("ccdone")
        E("gpsimd", lambda e: e.collective_compute(
            "AllReduce", OP.add, replica_groups=[[0, 1, 2, 3], [4, 5, 6, 7]],
            ins=[ccin.ap().opt()], outs=[ccout.ap().opt()]),
          reads=[b_ccin], writes=[b_ccout], sem=cc_sem)

        E("gpsimd", lambda e: e.dma_start(out=A_ap, in_=ccout.ap()[:, :]),
          reads=[b_ccout], writes=[b_A], dma="auto")
        E("vector", lambda e: e.tensor_scalar(A_ap, A_ap, 0.5, 0.5, OP.mult, OP.add),
          reads=[b_A], writes=[b_A])

        gBs = (gpool[:, 0:512], gpool[:, 2048:2560])
        fBs = (gpool[:, 512:1024], gpool[:, 2560:3072])
        tAs = (gpool[:, 1024:1536], vpb[:, 0:512])
        tBs = (gpool[:, 1536:2048], vpb[:, 512:1024])
        fbB = gpool[:, 2048:3072]

        pend_tp = []
        blkpar = [0]
        ring_old = toks(b_ring)
        b_rh[0].w = ring_old[-1] if ring_old else None
        b_rh[1].w = ring_old[-1] if ring_old else None

        def flush_tp():
            while pend_tp:
                jb0, tl0, par0 = pend_tp.pop(0)
                pst, bpt = PS[2 + par0], b_ps[2 + par0]
                for i in range(4):
                    E("tensor", lambda e, i=i, par0=par0, pst=pst: e.transpose(
                        pst[:, i * 128:(i + 1) * 128],
                        ring[:, par0 * 512 + i * 128: par0 * 512 + (i + 1) * 128], identf[:, :]),
                      reads=[b_rh[par0], b_identf], writes=[bpt])
                for i in range(4):
                    E("scalar", lambda e, i=i, jb0=jb0, tl0=tl0, pst=pst: e.copy(
                        actT[:, (4 * jb0 + i) * 512 + tl0 * 128: (4 * jb0 + i) * 512 + (tl0 + 1) * 128],
                        pst[:, i * 128:(i + 1) * 128]),
                      reads=[bpt], writes=[b_actT])

        for h in range(2):
            wl_toks = toks(b_wls[0]) + toks(b_wls[1])
            for jb in range(16):
                slab = (QP1, QP2)[jb % 2]
                b_slab = (b_QP1, b_QP2)[jb % 2]
                ew = tuple(wl_toks) if (jb < 2 and slab is QP1) else ()
                E("gpsimd", lambda e, jb=jb, slab=slab: e.dma_start(
                    out=slab[:, :], in_=wg_st[jb, :, :]),
                  writes=[b_slab], dma="auto", waits=ew)
                for tl in range(4):
                    tt = 4 * h + tl
                    ps, bp = PS[tl % 2], b_ps[tl % 2]
                    for kc in range(16):
                        for nh in range(2):
                            last = (kc == 15 and nh == 1)
                            E("tensor", lambda e, kc=kc, nh=nh, tt=tt, ps=ps, slab=slab: e.matmul(
                                ps[:, nh * 512:(nh + 1) * 512],
                                xnqT[:, kc * 1024 + tt * 128: kc * 1024 + (tt + 1) * 128],
                                slab[:, kc * 1024 + nh * 512: kc * 1024 + (nh + 1) * 512],
                                start=(kc == 0), stop=last),
                              reads=[b_xnqT, b_slab],
                              writes=[bp] if (kc == 0 and nh == 0) or last else [])
                    par = blkpar[0]
                    blkpar[0] ^= 1
                    gB, fB, tA, tB = gBs[par], fBs[par], tAs[par], tBs[par]
                    bg = b_gel[par]
                    E("scalar", lambda e, ps=ps, gB=gB: e.copy(gB, ps[:, 0:512]), reads=[bp], writes=[bg])
                    E("scalar", lambda e, ps=ps, fB=fB: e.copy(fB, ps[:, 512:1024]), reads=[bp], writes=[bg])
                    E("vector", lambda e, tA=tA, gB=gB: e.tensor_mul(tA, gB, gB), reads=[bg], writes=[bg])
                    E("vector", lambda e, tA=tA, tB=tB, gB=gB: e.tensor_mul(tB, tA, gB), reads=[bg], writes=[bg])
                    E("scalar", lambda e, tA=tA, tB=tB: e.activation(tA, tB, AF.Copy, scale=GC1),
                      reads=[bg], writes=[bg])
                    E("vector", lambda e, tA=tA, tB=tB, gB=gB: e.tensor_tensor(tB, gB, tA, OP.add),
                      reads=[bg], writes=[bg])
                    E("scalar", lambda e, tA=tA, tB=tB: e.activation(tA, tB, AF.Copy, scale=GC2),
                      reads=[bg], writes=[bg])
                    E("scalar", lambda e, tA=tA, tB=tB: e.activation(tB, tA, AF.Tanh),
                      reads=[bg], writes=[bg])
                    E("vector", lambda e, tA=tA, tB=tB: e.tensor_scalar_add(tA, tB, 1.0),
                      reads=[bg], writes=[bg])
                    E("scalar", lambda e, tA=tA, tB=tB: e.activation(tB, tA, AF.Copy, scale=0.5),
                      reads=[bg], writes=[bg])
                    E("vector", lambda e, tA=tA, tB=tB, gB=gB: e.tensor_mul(tA, gB, tB), reads=[bg], writes=[bg])
                    E("vector", lambda e, par=par, tA=tA, fB=fB: e.tensor_mul(
                        winslab[:, par * 512:(par + 1) * 512], tA, fB),
                      reads=[bg], writes=[b_winslab])
                    E("vector", lambda e, par=par: e.tensor_copy(ring[:, par * 512:(par + 1) * 512],
                                                                 winslab[:, par * 512:(par + 1) * 512]),
                      reads=[b_winslab], writes=[b_rh[par]])
                    flush_tp()
                    pend_tp.append((jb, tl, par))
            flush_tp()
            if h == 1:
                gel1_toks = tuple(toks(b_gel[1]))
                for kc in range(16):
                    E("scalar", lambda e, kc=kc: e.copy(vpb[:, :],
                                                        QTB[:, kc * 1024: kc * 1024 + 1024]),
                      reads=[b_QTB], writes=[b_vpb], waits=(gel1_toks if kc == 0 else ()))
                    E("scalar", lambda e, kc=kc: e.activation(QTB[:, kc * 1024:(kc + 1) * 1024],
                                                              vpb[:, :], AF.Copy,
                                                              scale=A_ap[:, kc:kc + 1]),
                      reads=[b_vpb, b_A], writes=[b_QTB])
            qp1_toks = toks(b_QP1)
            first_wl = [True, True]
            for dh in range(2):
                for grp in range(8):
                    sl = grp % 2
                    ew = tuple(qp1_toks) if first_wl[sl] else ()
                    first_wl[sl] = False
                    E("gpsimd", lambda e, dh=dh, grp=grp, sl=sl: e.dma_start(
                        out=QP1[:, sl * 8192:(sl + 1) * 8192], in_=wl_st[dh, grp, :, :]),
                      writes=[b_wls[sl]], dma="auto", waits=ew)
                    for k8 in range(8):
                        kc = grp * 8 + k8
                        for tl in range(4):
                            last = (kc == 63)
                            E("tensor", lambda e, kc=kc, k8=k8, tl=tl, sl=sl: e.matmul(
                                PS[tl][:, 0:512],
                                actT[:, kc * 512 + tl * 128: kc * 512 + (tl + 1) * 128],
                                QP1[:, sl * 8192 + k8 * 1024: sl * 8192 + k8 * 1024 + 512],
                                start=(kc == 0), stop=last),
                              reads=[b_actT, b_wls[sl]],
                              writes=[b_ps[tl]] if kc == 0 else [])
                            E("tensor", lambda e, kc=kc, k8=k8, tl=tl, sl=sl: e.matmul(
                                PS[tl][:, 512:1024],
                                actT[:, kc * 512 + tl * 128: kc * 512 + (tl + 1) * 128],
                                QP1[:, sl * 8192 + k8 * 1024 + 512: sl * 8192 + (k8 + 1) * 1024],
                                start=(kc == 0), stop=last),
                              reads=[b_actT, b_wls[sl]],
                              writes=[b_ps[tl]] if last else [])
                for tl in range(4):
                    tt = 4 * h + tl
                    E("scalar", lambda e, tl=tl: e.copy(fbB, PS[tl][:, :]),
                      reads=[b_ps[tl]], writes=[b_gpool], waits=tuple(toks(b_gel[1]) + toks(b_gel[0])))
                    E("scalar", lambda e, tt=tt, dh=dh: e.dma_start(
                        out=XNT[:, tt * 2048 + dh * 1024: tt * 2048 + (dh + 1) * 1024].bitcast(U16),
                        in_=fbB.bitcast(U16)),
                      reads=[b_gpool], writes=[b_XNTh[tt // 4]], dma="auto")

        rh_toks = tuple(toks(b_rh[0]) + toks(b_rh[1]))
        first_p4_ring = [True]
        for db in range(16):
            E("gpsimd", lambda e, db=db: e.dma_start(out=woutsl[:, :], in_=wout_st[db, :, :]),
              writes=[b_wos], dma="auto")
            ps, bp = PS[db % 2], b_ps[db % 2]
            for kc in range(16):
                for nh in range(2):
                    last = (kc == 15 and nh == 1)
                    E("tensor", lambda e, kc=kc, nh=nh, ps=ps: e.matmul(
                        ps[:, nh * 512:(nh + 1) * 512],
                        woutsl[:, kc * 128:(kc + 1) * 128],
                        QTB[:, kc * 1024 + nh * 512: kc * 1024 + (nh + 1) * 512],
                        start=(kc == 0), stop=last),
                      reads=[b_wos, b_QTB],
                      writes=[bp] if (kc == 0 and nh == 0) or last else [])
            E("scalar", lambda e, db=db, ps=ps: e.activation(fbB, ps[:, :], AF.Identity,
                                                             bias=boutf[:, db:db + 1]),
              reads=[bp, b_boutf], writes=[b_gpool])
            E("vector", lambda e: e.tensor_copy(ring[:, 0:1024], fbB), reads=[b_gpool], writes=[b_ring],
              waits=(rh_toks if first_p4_ring[0] else ()))
            first_p4_ring[0] = False
            pst, bpt = PS[2 + (db % 2)], b_ps[2 + (db % 2)]
            for i in range(8):
                E("tensor", lambda e, i=i, pst=pst: e.transpose(
                    pst[:, i * 128:(i + 1) * 128], ring[:, i * 128:(i + 1) * 128], identf[:, :]),
                  reads=[b_ring, b_identf], writes=[bpt])
            E("scalar", lambda e, pst=pst: e.copy(stg[:, :], pst[:, :]),
              reads=[bpt], writes=[b_stg])
            E("scalar", lambda e, db=db: e.dma_start(
                out=proj_tm.reshape([8, 128, 2048]).ap()[:, :, db * 128:(db + 1) * 128].transpose([1, 0, 2]),
                in_=stg.reshape([128, 8, 128]).ap()[:, :, :]),
              reads=[b_stg], writes=[b_projtm], dma="auto")

        projslot = gpool[:, 0:2048]
        gel0_toks = tuple(toks(b_gel[0]) + toks(b_gel[1]))
        for tt in range(NTT):
            E("sync", lambda e, tt=tt: e.dma_start(out=xslot[:, :], in_=x[tt * 128:(tt + 1) * 128, :]),
              writes=[b_x], dma="auto")
            E("sync", lambda e, tt=tt: e.dma_start(out=projslot,
                                                   in_=proj_tm.ap()[tt * 128:(tt + 1) * 128, :]),
              reads=[b_projtm], writes=[b_gpool], dma="auto",
              waits=(gel0_toks if tt == 0 else ()))
            E("vector", lambda e, tt=tt: e.tensor_scalar(sc1[:, :], xslot[:, :], mS(tt), rS(tt),
                                                         OP.subtract, OP.mult),
              reads=[b_x, b_stats], writes=[b_sc1])
            E("vector", lambda e: e.tensor_tensor(ring[:, :], sc1[:, :], xslot[:, :], OP.add),
              reads=[b_sc1, b_x], writes=[b_ring])
            E("vector", lambda e: e.tensor_tensor(sc1[:, :], ring[:, :], projslot, OP.add),
              reads=[b_ring, b_gpool], writes=[b_sc1])
            E("vector", lambda e, tt=tt: e.tensor_tensor(ring[:, :], sc1[:, :],
                                                         XNT[:, tt * 2048:(tt + 1) * 2048], OP.add),
              reads=[b_sc1, b_XNTh[tt // 4]], writes=[b_ring])
            E("sync", lambda e, tt=tt: e.dma_start(out=y[tt * 128:(tt + 1) * 128, :], in_=ring[:, :]),
              reads=[b_ring], dma="o")
        so = sch.dma_sem("o")
        sch.wait_only("gpsimd", [(so, so.count)])

        sems = sch.all_sems()
        with contextlib.ExitStack() as es2:
            for s in sems:
                s.h = es2.enter_context(nc.semaphore(s.name))
            with nc.Block() as block:
                @block.sync
                def _(e):
                    sch.run("sync", e)

                @block.vector
                def _(e):
                    sch.run("vector", e)

                @block.scalar
                def _(e):
                    sch.run("scalar", e)

                @block.tensor
                def _(e):
                    sch.run("tensor", e)

                @block.gpsimd
                def _(e):
                    sch.run("gpsimd", e)
    return nc


_NC_CACHE = [None]


def kernel(L0_attn_X, L0_attn_W_in, L0_attn_b_in, L0_attn_W_out, L0_attn_b_out,
           L0_attn_gamma, L0_attn_beta, L0_ffn_w_gating, L0_ffn_w_linear):
    X = np.ascontiguousarray(np.asarray(L0_attn_X, dtype=np.float32))
    W_in = np.asarray(L0_attn_W_in, dtype=np.float32)
    b_in = np.asarray(L0_attn_b_in, dtype=np.float32)
    W_out = np.asarray(L0_attn_W_out, dtype=np.float32)
    b_out = np.asarray(L0_attn_b_out, dtype=np.float32)
    gamma = np.asarray(L0_attn_gamma, dtype=np.float32)
    beta = np.asarray(L0_attn_beta, dtype=np.float32)
    w_g = np.asarray(L0_ffn_w_gating, dtype=np.float32)
    w_l = np.asarray(L0_ffn_w_linear, dtype=np.float32)

    cols = []
    for kc in range(16):
        cols += list(range(4096 + kc * 128, 4096 + (kc + 1) * 128))
        cols += list(range(2048 + kc * 128, 2048 + (kc + 1) * 128))
    cols += list(range(0, 2048))
    w_in_st = np.ascontiguousarray(
        W_in[:, cols].reshape(16, 128, 24, 256).transpose(2, 1, 0, 3).reshape(24, 128, 4096))
    a0 = w_g[0].reshape(16, 128, 16, 512)
    a1 = w_g[1].reshape(16, 128, 16, 512)
    wg_st = np.ascontiguousarray(
        np.concatenate([a0, a1], axis=3).transpose(2, 1, 0, 3).reshape(16, 128, 16384))
    wl_st = np.ascontiguousarray(
        w_l.reshape(8, 8, 128, 2, 1024).transpose(3, 0, 2, 1, 4).reshape(2, 8, 128, 8192))
    wout_st = np.ascontiguousarray(
        W_out.reshape(16, 128, 16, 128).transpose(2, 1, 0, 3).reshape(16, 128, 2048))
    b_in_fm = np.ascontiguousarray(b_in.reshape(48, 128).T)
    b_out_fm = np.ascontiguousarray(b_out.reshape(16, 128).T)
    gam_rep = np.ascontiguousarray(np.broadcast_to(gamma, (128, D)))
    bet_rep = np.ascontiguousarray(np.broadcast_to(beta, (128, D)))
    ident = np.eye(128, dtype=np.float32)

    shared = dict(w_in_st=w_in_st, wg_st=wg_st, wl_st=wl_st, wout_st=wout_st,
                  b_in_fm=b_in_fm, b_out_fm=b_out_fm, gam_rep=gam_rep,
                  bet_rep=bet_rep, ident=ident)
    in_maps = []
    for c in range(NCORE):
        b = c // 4
        r0 = (c % 4) * TOK
        cos_fm, sin_fm = _rotary_tables(c)
        m = dict(shared)
        m["x"] = np.ascontiguousarray(X[b, r0:r0 + TOK, :])
        m["tabs_st"] = np.ascontiguousarray(np.concatenate([cos_fm, sin_fm], axis=2))
        in_maps.append(m)

    if _NC_CACHE[0] is None:
        _NC_CACHE[0] = build_nc()
    nc = _NC_CACHE[0]
    res = run_bass_kernel_spmd(nc, in_maps, core_ids=list(range(NCORE)))
    out = np.empty((B, S, D), dtype=np.float32)
    for c in range(NCORE):
        b = c // 4
        r0 = (c % 4) * TOK
        out[b, r0:r0 + TOK, :] = res.results[c]["y"]
    return out


# version 08
# speedup vs baseline: 2.0507x; 1.0171x over previous
import contextlib
import numpy as np
import concourse.bass as bass
import concourse.mybir as mybir
from concourse.bass_utils import run_bass_kernel_spmd

F32 = mybir.dt.float32
BF16 = mybir.dt.bfloat16
FP8 = mybir.dt.float8e4
U16 = mybir.dt.uint16
AF = mybir.ActivationFunctionType
OP = mybir.AluOpType
AX = mybir.AxisListType.X

B, S, D = 2, 4096, 2048
NCORE = 8
TOK = 1024
NTT = 8
EPS = 1e-5
INV_D = float(1.0 / 2048.0)
GC1 = 0.0446777344
GC2 = 0.796875
SWAP32 = [i ^ 1 for i in range(32)]

ENGS = ("sync", "vector", "scalar", "tensor", "gpsimd")


class Sem:
    __slots__ = ("name", "count", "h")

    def __init__(self, name):
        self.name, self.count, self.h = name, 0, None


class Buf:
    __slots__ = ("name", "w", "readers")

    def __init__(self, name):
        self.name, self.w, self.readers = name, None, []


class Sched:
    def __init__(self):
        self.q = {e: [] for e in ENGS}
        self.esem = {e: Sem("E_" + e) for e in ENGS}
        self.dsem = {}
        self.xsem = []

    def all_sems(self):
        return list(self.esem.values()) + list(self.dsem.values()) + self.xsem

    def dma_sem(self, name):
        if name not in self.dsem:
            self.dsem[name] = Sem("D_" + name)
        return self.dsem[name]

    def new_sem(self, name):
        s = Sem(name)
        self.xsem.append(s)
        return s

    def emit(self, eng, build, reads=(), writes=(), dma=None, sem=None, waits=()):
        wmap = {}

        def addw(tok):
            if tok is None:
                return
            s, v = tok
            if wmap.get(s, -1) < v:
                wmap[s] = v

        for b in reads:
            addw(b.w)
        for b in writes:
            addw(b.w)
            for t in b.readers:
                addw(t)
        for t in waits:
            addw(t)
        if dma is not None:
            if dma == "auto":
                assert writes, "auto dma sem needs a write buffer"
                dma = "b_" + writes[0].name
            s = self.dma_sem(dma)
            s.count += 16
            tok = (s, s.count)
            amt = 16
        elif sem is not None:
            s = sem
            s.count += 1
            tok = (s, s.count)
            amt = 1
        else:
            s = self.esem[eng]
            s.count += 1
            tok = (s, s.count)
            amt = 1
        self.q[eng].append((build, tuple(wmap.items()), (s, amt)))
        for b in reads:
            b.readers.append(tok)
        for b in writes:
            b.w = tok
            b.readers = []
        return tok

    def wait_only(self, eng, toks):
        self.q[eng].append((None, tuple((s, v) for s, v in toks), None))

    def run(self, eng_name, engobj):
        waited = {}
        for build, waits, inc in self.q[eng_name]:
            for s, v in waits:
                if waited.get(s, 0) < v:
                    engobj.wait_ge(s.h, v)
                    waited[s] = v
            if build is None:
                continue
            ins = build(engobj)
            s, amt = inc
            ins.then_inc(s.h, amt)
            if waited.get(s, 0) < s.count:
                pass


def _rotary_tables(core):
    rot = D // 2
    inv_freq = (1.0 / (np.float32(10000.0) **
                       (np.arange(0, rot, 2, dtype=np.float32) / np.float32(rot)))).astype(np.float32)
    p0 = (core % 4) * TOK
    pos = np.arange(p0, p0 + TOK, dtype=np.float32)
    fr = (pos[:, None] * inv_freq[None, :]).astype(np.float32)
    fr = np.repeat(fr, 2, axis=1)
    cos = np.cos(fr).astype(np.float32)
    sin = np.sin(fr).astype(np.float32)
    sinS = sin.copy()
    sinS[:, 0::2] = -sin[:, 0::2]
    cos_fm = np.ascontiguousarray(cos.T.reshape(8, 128, TOK))
    sin_fm = np.ascontiguousarray(sinS.T.reshape(8, 128, TOK))
    return cos_fm, sin_fm


def build_nc():
    nc = bass.Bass("TRN2", target_bir_lowering=False, debug=False)

    def din(name, shape):
        return nc.dram_tensor(name, list(shape), F32, kind="ExternalInput").ap()

    x = din("x", (TOK, D))
    y = nc.dram_tensor("y", [TOK, D], F32, kind="ExternalOutput").ap()
    w_in_st = din("w_in_st", (24, 128, 4096))
    wg_st = din("wg_st", (16, 128, 16384))
    wl_st = din("wl_st", (2, 8, 128, 8192))
    wout_st = din("wout_st", (16, 128, 2048))
    b_in_fm = din("b_in_fm", (128, 48))
    b_out_fm = din("b_out_fm", (128, 16))
    gam_rep = din("gam_rep", (128, D))
    bet_rep = din("bet_rep", (128, D))
    tabs_st = din("tabs_st", (8, 128, 2048))
    ident = din("ident", (128, 128))
    ccin = nc.dram_tensor("ccin", [128, 16], F32)
    ccout = nc.dram_tensor("ccout", [128, 16], F32)
    proj_tm = nc.dram_tensor("proj_tm", [TOK, D], BF16)

    es = contextlib.ExitStack()
    sch = Sched()
    with es:
        T = lambda n, shp, dt: es.enter_context(nc.sbuf_tensor(n, list(shp), dt))
        xslot = T("xslot", (128, D), F32)
        xnt = T("xnt", (128, D), F32)
        sc1 = T("sc1", (128, D), F32)
        ring = T("ring", (128, D), F32)
        tab = T("tab", (128, D), F32)
        stats = T("stats", (128, 64), F32)
        winslab = T("winslab", (128, 4096), BF16)
        XNT = T("XNT", (128, 16384), BF16)
        QTB = T("QTB", (128, 16384), BF16)
        gpool = T("gpool", (128, 3072), BF16)
        vpb = T("vpb", (128, 1024), BF16)
        stg = T("stg", (128, 1024), BF16)
        identf = T("identf", (128, 128), F32)
        onesb = T("onesb", (128, 1), F32)
        onesbf = T("onesbf", (128, 1), BF16)
        ones1 = T("ones1", (1, 128), F32)
        binf = T("binf", (128, 48), F32)
        boutf = T("boutf", (128, 16), F32)
        QP1 = T("QP1", (128, 16384), FP8)
        QP2 = T("QP2", (128, 16384), FP8)
        xnqT = T("xnqT", (128, 16384), FP8)
        actT = T("actT", (128, 32768), FP8)
        woutsl = T("woutsl", (128, 2048), BF16)
        PS = [es.enter_context(nc.psum_tensor(f"ps{i}", [128, 1024], F32)) for i in range(4)]

        bufs = {}

        def Bf(n):
            bufs[n] = Buf(n)
            return bufs[n]

        b_x, b_xn, b_sc1, b_ring, b_tab = Bf("x"), Bf("xn"), Bf("sc1"), Bf("ring"), Bf("tab")
        b_small, b_stats = Bf("small"), Bf("stats")
        b_winslab, b_QTB, b_gpool = Bf("winslab"), Bf("QTB"), Bf("gpool")
        b_XNTh = [Bf("XNTh0"), Bf("XNTh1")]
        b_vpb, b_stg = Bf("vpb"), Bf("stg")
        b_rh = [Bf("rh0"), Bf("rh1")]
        b_gel = [Bf("gel0"), Bf("gel1")]
        b_identf, b_onesb, b_ones1, b_binf, b_boutf = Bf("identf"), Bf("onesb"), Bf("ones1"), Bf("binf"), Bf("boutf")
        b_onesbf = Bf("onesbf")
        b_QP1, b_QP2, b_xnqT, b_actT = Bf("QP1"), Bf("QP2"), Bf("xnqT"), Bf("actT")
        b_wls = [Bf("wls0"), Bf("wls1")]
        b_wos = Bf("wos")
        b_ps = [Bf(f"ps{i}") for i in range(4)]
        b_ccin, b_ccout, b_projtm = Bf("ccin"), Bf("ccout"), Bf("projtm")

        E = sch.emit

        def toks(buf):
            t = list(buf.readers)
            if buf.w:
                t.append(buf.w)
            return t

        E("sync", lambda e: e.dma_start(out=identf[:, :], in_=ident[:, :]), writes=[b_identf], dma="auto")
        E("sync", lambda e: e.dma_start(out=binf[:, :], in_=b_in_fm[:, :]), writes=[b_binf], dma="auto")
        E("sync", lambda e: e.dma_start(out=boutf[:, :], in_=b_out_fm[:, :]), writes=[b_boutf], dma="auto")
        E("gpsimd", lambda e: e.dma_start(out=gpool[:, 0:2048], in_=gam_rep[:, :]), writes=[b_gpool], dma="auto")
        E("gpsimd", lambda e: e.dma_start(out=winslab[:, 0:2048], in_=bet_rep[:, :]), writes=[b_winslab], dma="auto")
        E("vector", lambda e: e.memset(onesb[:, :], 1.0), writes=[b_onesb])
        E("vector", lambda e: e.memset(onesbf[:, :], 1.0), writes=[b_onesbf])
        E("vector", lambda e: e.memset(ones1[:, :], 1.0), writes=[b_ones1])

        mS = lambda tt: stats[:, tt:tt + 1]
        rS = lambda tt: stats[:, 8 + tt:9 + tt]
        t0 = stats[:, 16:17]; t1 = stats[:, 17:18]; t2 = stats[:, 18:19]
        t3 = stats[:, 19:20]; t4 = stats[:, 20:21]; t5 = stats[:, 21:22]
        pA_ap = stats[:, 32:48]
        A_ap = stats[:, 48:64]
        b_pA, b_A = Bf("pA"), Bf("A")

        def rsqrt_into(src_ap, dst_ap, rb, wb):
            E("scalar", lambda e: e.activation(t3, src_ap, AF.Sqrt), reads=[rb], writes=[b_stats])
            E("vector", lambda e: e.reciprocal(t4, t3), reads=[b_stats], writes=[b_stats])
            E("vector", lambda e: e.tensor_mul(t5, t4, t4), reads=[b_stats], writes=[b_stats])
            E("vector", lambda e: e.tensor_mul(t5, src_ap, t5), reads=[rb, b_stats], writes=[b_stats])
            E("vector", lambda e: e.tensor_scalar(t5, t5, -0.5, 1.5, OP.mult, OP.add),
              reads=[b_stats], writes=[b_stats])
            E("vector", lambda e: e.tensor_mul(dst_ap, t4, t5), reads=[b_stats], writes=[wb])

        sAll = stats[:, 16:24]; qAll = stats[:, 24:32]
        mAll = stats[:, 0:8]; rAll = stats[:, 8:16]
        uA = stats[:, 48:56]; m2A = stats[:, 56:64]
        w0a = stats[:, 32:40]; w1a = stats[:, 40:48]
        for tt in range(NTT):
            E("sync", lambda e, tt=tt: e.dma_start(out=xslot[:, :], in_=x[tt * 128:(tt + 1) * 128, :]),
              writes=[b_x], dma="auto")
            E("vector", lambda e, tt=tt: e.reduce_sum(stats[:, 16 + tt:17 + tt], xslot[:, :], axis=AX),
              reads=[b_x], writes=[b_stats])
            E("vector", lambda e, tt=tt: e.scalar_tensor_tensor(ring[:, :], xslot[:, :], 1.0, xslot[:, :],
                                                                OP.mult, OP.mult,
                                                                accum_out=stats[:, 24 + tt:25 + tt]),
              reads=[b_x], writes=[b_ring, b_stats])
        E("scalar", lambda e: e.activation(mAll, sAll, AF.Identity, scale=INV_D),
          reads=[b_stats], writes=[b_stats])
        E("vector", lambda e: e.tensor_scalar(uA, qAll, INV_D, float(EPS), OP.mult, OP.add),
          reads=[b_stats], writes=[b_stats])
        E("vector", lambda e: e.tensor_mul(w0a, mAll, mAll), reads=[b_stats], writes=[b_stats])
        E("vector", lambda e: e.tensor_sub(uA, uA, w0a), reads=[b_stats], writes=[b_stats])
        E("scalar", lambda e: e.activation(w0a, uA, AF.Sqrt), reads=[b_stats], writes=[b_stats])
        E("vector", lambda e: e.reciprocal(w1a, w0a), reads=[b_stats], writes=[b_stats])
        E("vector", lambda e: e.tensor_mul(w0a, w1a, w1a), reads=[b_stats], writes=[b_stats])
        E("vector", lambda e: e.tensor_mul(w0a, uA, w0a), reads=[b_stats], writes=[b_stats])
        E("vector", lambda e: e.tensor_scalar(w0a, w0a, -0.5, 1.5, OP.mult, OP.add),
          reads=[b_stats], writes=[b_stats])
        E("vector", lambda e: e.tensor_mul(rAll, w1a, w0a), reads=[b_stats], writes=[b_stats])
        E("vector", lambda e: e.tensor_scalar(w0a, sAll, INV_D, None, OP.mult),
          reads=[b_stats], writes=[b_stats])
        E("vector", lambda e: e.tensor_sub(w0a, w0a, mAll), reads=[b_stats], writes=[b_stats])
        E("vector", lambda e: e.tensor_mul(m2A, w0a, rAll), reads=[b_stats], writes=[b_stats])
        E("vector", lambda e: e.tensor_scalar(w0a, uA, 1.0, -float(EPS), OP.mult, OP.add),
          reads=[b_stats], writes=[b_stats])
        E("vector", lambda e: e.tensor_mul(w1a, rAll, rAll), reads=[b_stats], writes=[b_stats])
        E("vector", lambda e: e.tensor_mul(w0a, w0a, w1a), reads=[b_stats], writes=[b_stats])
        E("vector", lambda e: e.tensor_mul(w1a, m2A, m2A), reads=[b_stats], writes=[b_stats])
        E("vector", lambda e: e.tensor_sub(w0a, w0a, w1a), reads=[b_stats], writes=[b_stats])
        E("vector", lambda e: e.tensor_scalar(w0a, w0a, 1.0, float(EPS), OP.mult, OP.add),
          reads=[b_stats], writes=[b_stats])
        E("scalar", lambda e: e.activation(w1a, w0a, AF.Sqrt), reads=[b_stats], writes=[b_stats])
        E("vector", lambda e: e.reciprocal(uA, w1a), reads=[b_stats], writes=[b_stats])
        E("vector", lambda e: e.tensor_mul(w1a, uA, uA), reads=[b_stats], writes=[b_stats])
        E("vector", lambda e: e.tensor_mul(w1a, w0a, w1a), reads=[b_stats], writes=[b_stats])
        E("vector", lambda e: e.tensor_scalar(w1a, w1a, -0.5, 1.5, OP.mult, OP.add),
          reads=[b_stats], writes=[b_stats])
        E("vector", lambda e: e.tensor_mul(uA, uA, w1a), reads=[b_stats], writes=[b_stats])
        for tt in range(NTT):
            E("sync", lambda e, tt=tt: e.dma_start(out=xslot[:, :], in_=x[tt * 128:(tt + 1) * 128, :]),
              writes=[b_x], dma="auto")
            E("vector", lambda e, tt=tt: e.tensor_scalar(xnt[:, :], xslot[:, :], mS(tt), rS(tt),
                                                         OP.subtract, OP.mult),
              reads=[b_x, b_stats], writes=[b_xn])
            E("scalar", lambda e, tt=tt: e.copy(QP1[:, tt * 2048:(tt + 1) * 2048], xnt[:, :]),
              reads=[b_xn], writes=[b_QP1])
            E("vector", lambda e, tt=tt: e.scalar_tensor_tensor(ring[:, :], xnt[:, :],
                                                                stats[:, 56 + tt:57 + tt], gpool[:, 0:2048],
                                                                OP.subtract, OP.mult),
              reads=[b_xn, b_stats, b_gpool], writes=[b_ring])
            E("vector", lambda e, tt=tt: e.tensor_scalar(ring[:, :], ring[:, :],
                                                         stats[:, 48 + tt:49 + tt], None, OP.mult),
              reads=[b_ring, b_stats], writes=[b_ring])
            E("vector", lambda e: e.tensor_tensor(sc1[:, :], ring[:, :], winslab[:, 0:2048], OP.add),
              reads=[b_ring, b_winslab], writes=[b_sc1])
            for half in range(2):
                ps, bp = PS[2 + half], b_ps[2 + half]
                for k in range(8):
                    kc = half * 8 + k
                    E("tensor", lambda e, kc=kc, k=k, ps=ps: e.transpose(
                        ps[:, k * 128:(k + 1) * 128], sc1[:, kc * 128:(kc + 1) * 128], identf[:, :]),
                      reads=[b_sc1, b_identf], writes=[bp])
                for k in range(8):
                    kc = half * 8 + k
                    E("scalar", lambda e, kc=kc, k=k, tt=tt, ps=ps: e.copy(
                        XNT[:, kc * 1024 + tt * 128: kc * 1024 + (tt + 1) * 128],
                        ps[:, k * 128:(k + 1) * 128]),
                      reads=[bp], writes=[b_XNTh[tt // 4]])

        def load_winslab(g):
            E("gpsimd", lambda e, g=g: e.dma_start(out=winslab[:, :], in_=w_in_st[g, :, :]),
              writes=[b_winslab], dma="auto")

        def qkv_mms(oc01, ps, bp):
            for nh in range(2):
                for kc in range(16):
                    last = (kc == 15 and nh == 1)
                    E("tensor", lambda e, oc=oc01, kc=kc, nh=nh, ps=ps: e.matmul(
                        ps[:, nh * 512:(nh + 1) * 512],
                        winslab[:, kc * 256 + oc * 128: kc * 256 + (oc + 1) * 128],
                        XNT[:, kc * 1024 + nh * 512: kc * 1024 + (nh + 1) * 512],
                        start=(kc == 0), stop=last),
                      reads=[b_winslab, b_XNTh[nh]],
                      writes=[bp] if (kc == 0 and nh == 0) or last else [])

        nv2 = xslot[0:1, 0:1024]
        nk2 = xslot[0:1, 1024:2048]
        rv_ = xslot[0:1, 0:1024]
        rk_ = xslot[0:1, 1024:2048]
        s3_ = tab[0:1, 0:1024]
        s4_ = ring[0:1, 0:1024]
        w1_ = ring[0:1, 1024:2048]
        E("vector", lambda e: e.memset(xslot[0:1, 0:2048], 0.0), writes=[b_x])

        r0ap = xnt[:, 0:1024]
        r1ap = xnt[:, 1024:2048]
        m1ap = ring[:, 0:1024]
        m2ap = ring[:, 1024:2048]
        k1ap = sc1[:, 0:1024]

        def norm_accum(dst_row, src_ap, b_src):
            for nh in range(2):
                E("tensor", lambda e, nh=nh, src_ap=src_ap: e.matmul(
                    PS[2][0:1, nh * 512:(nh + 1) * 512], onesbf[:, :],
                    src_ap[:, nh * 512:(nh + 1) * 512], start=True, stop=True),
                  reads=[b_src, b_onesbf], writes=[b_ps[2]])
            E("vector", lambda e, dst_row=dst_row: e.tensor_tensor(dst_row, dst_row, PS[2][0:1, :], OP.add),
              reads=[b_ps[2], b_x], writes=[b_x])

        pending_norms = []

        def flush_norms():
            if pending_norms:
                pending_norms.pop()
                norm_accum(nv2, woutsl[:, 0:1024], b_wos)
                norm_accum(nk2, woutsl[:, 1024:2048], b_wos)

        def emit_p2():
            def rsqrt_row(src, dst):
                E("scalar", lambda e: e.activation(s3_, src, AF.Sqrt), reads=[b_x], writes=[b_tab])
                E("vector", lambda e: e.reciprocal(s4_, s3_), reads=[b_tab], writes=[b_ring])
                E("vector", lambda e: e.tensor_mul(s3_, s4_, s4_), reads=[b_ring], writes=[b_tab])
                E("vector", lambda e: e.tensor_mul(s3_, src, s3_), reads=[b_x, b_tab], writes=[b_tab])
                E("vector", lambda e: e.tensor_scalar(s3_, s3_, -0.5, 1.5, OP.mult, OP.add),
                  reads=[b_tab], writes=[b_tab])
                E("vector", lambda e: e.tensor_mul(dst, s4_, s3_), reads=[b_ring, b_tab], writes=[b_x])

            rsqrt_row(nv2, rv_)
            rsqrt_row(nk2, rk_)
            E("vector", lambda e: e.tensor_mul(w1_, rk_, rv_), reads=[b_x], writes=[b_ring])
            for nh in range(2):
                E("tensor", lambda e, nh=nh: e.matmul(PS[2][:, nh * 512:(nh + 1) * 512], ones1[:, :],
                                                      w1_[:, nh * 512:(nh + 1) * 512], start=True, stop=True),
                  reads=[b_ring, b_ones1], writes=[b_ps[2]])
            wrep = tab[:, 0:1024]
            E("vector", lambda e: e.tensor_copy(wrep, PS[2][:, :]), reads=[b_ps[2]], writes=[b_tab])
            for kc in range(16):
                E("vector", lambda e, kc=kc: e.scalar_tensor_tensor(
                    ring[:, 0:1024], QP2[:, kc * 1024:(kc + 1) * 1024], 1.0, wrep,
                    OP.mult, OP.mult, accum_out=pA_ap[:, kc:kc + 1]),
                  reads=[b_QP2, b_tab], writes=[b_ring, b_pA])
            E("gpsimd", lambda e: e.dma_start(out=ccin.ap()[:, :], in_=pA_ap),
              reads=[b_pA], writes=[b_ccin], dma="auto")
            cc_sem = sch.new_sem("ccdone")
            E("gpsimd", lambda e: e.collective_compute(
                "AllReduce", OP.add, replica_groups=[[0, 1, 2, 3], [4, 5, 6, 7]],
                ins=[ccin.ap().opt()], outs=[ccout.ap().opt()]),
              reads=[b_ccin], writes=[b_ccout], sem=cc_sem)

        for g in range(24):
            load_winslab(g)
            qkv_mms(0, PS[0], b_ps[0])
            qkv_mms(1, PS[1], b_ps[1])
            flush_norms()
            if g == 16:
                emit_p2()
            if g < 16:
                kc = g
                if kc < 8:
                    E("sync", lambda e, kc=kc: e.dma_start(out=tab[:, :], in_=tabs_st[kc, :, :]),
                      writes=[b_tab], dma="auto")
                    E("scalar", lambda e, kc=kc: e.activation(r0ap, PS[0][:, :], AF.Identity,
                                                              bias=binf[:, 32 + kc:33 + kc]),
                      reads=[b_ps[0], b_binf], writes=[b_xn])
                    E("vector", lambda e: e.stream_shuffle(r1ap, r0ap, SWAP32),
                      reads=[b_xn], writes=[b_xn])
                    E("vector", lambda e: e.tensor_tensor(m1ap, r0ap, tab[:, 0:1024], OP.mult),
                      reads=[b_xn, b_tab], writes=[b_ring])
                    E("vector", lambda e: e.tensor_tensor(m2ap, r1ap, tab[:, 1024:2048], OP.mult),
                      reads=[b_xn, b_tab], writes=[b_ring])
                    E("vector", lambda e: e.tensor_tensor(vpb[:, :], m1ap, m2ap, OP.add),
                      reads=[b_ring], writes=[b_vpb])
                    E("scalar", lambda e, kc=kc: e.activation(r0ap, PS[1][:, :], AF.Identity,
                                                              bias=binf[:, 16 + kc:17 + kc]),
                      reads=[b_ps[1], b_binf], writes=[b_xn])
                    E("vector", lambda e: e.stream_shuffle(r1ap, r0ap, SWAP32),
                      reads=[b_xn], writes=[b_xn])
                    E("vector", lambda e: e.tensor_tensor(k1ap, r0ap, tab[:, 0:1024], OP.mult),
                      reads=[b_xn, b_tab], writes=[b_sc1])
                    E("vector", lambda e: e.tensor_tensor(m2ap, r1ap, tab[:, 1024:2048], OP.mult),
                      reads=[b_xn, b_tab], writes=[b_ring])
                    E("vector", lambda e: e.tensor_tensor(m1ap, k1ap, m2ap, OP.add),
                      reads=[b_sc1, b_ring], writes=[b_ring])
                else:
                    E("scalar", lambda e, kc=kc: e.activation(vpb[:, :], PS[0][:, :], AF.Identity,
                                                              bias=binf[:, 32 + kc:33 + kc]),
                      reads=[b_ps[0], b_binf], writes=[b_vpb])
                    E("scalar", lambda e, kc=kc: e.activation(m1ap, PS[1][:, :], AF.Identity,
                                                              bias=binf[:, 16 + kc:17 + kc]),
                      reads=[b_ps[1], b_binf], writes=[b_ring])
                E("vector", lambda e, kc=kc: e.tensor_tensor(QP2[:, kc * 1024:(kc + 1) * 1024],
                                                             m1ap, vpb[:, :], OP.mult),
                  reads=[b_ring, b_vpb], writes=[b_QP2])
                E("vector", lambda e: e.tensor_tensor(woutsl[:, 0:1024], vpb[:, :], vpb[:, :], OP.mult),
                  reads=[b_vpb], writes=[b_wos])
                E("vector", lambda e: e.tensor_tensor(woutsl[:, 1024:2048], m1ap, m1ap, OP.mult),
                  reads=[b_ring], writes=[b_wos])
                pending_norms.append(True)
            else:
                q = g - 16
                for j in range(2):
                    oc = 2 * q + j
                    ps, bp = PS[j], b_ps[j]
                    E("scalar", lambda e, oc=oc, ps=ps: e.activation(
                        QTB[:, oc * 1024:(oc + 1) * 1024], ps[:, :], AF.Identity,
                        bias=binf[:, oc:oc + 1]),
                      reads=[bp, b_binf], writes=[b_QTB])

        for tt in range(NTT):
            E("vector", lambda e, tt=tt: e.tensor_copy(sc1[:, :], QP1[:, tt * 2048:(tt + 1) * 2048]),
              reads=[b_QP1], writes=[b_sc1])
            for half in range(2):
                ps, bp = PS[2 + half], b_ps[2 + half]
                for k in range(8):
                    kc = half * 8 + k
                    E("tensor", lambda e, kc=kc, k=k, ps=ps: e.transpose(
                        ps[:, k * 128:(k + 1) * 128], sc1[:, kc * 128:(kc + 1) * 128], identf[:, :]),
                      reads=[b_sc1, b_identf], writes=[bp])
                for k in range(8):
                    kc = half * 8 + k
                    E("scalar", lambda e, kc=kc, k=k, tt=tt, ps=ps: e.copy(
                        xnqT[:, kc * 1024 + tt * 128: kc * 1024 + (tt + 1) * 128],
                        ps[:, k * 128:(k + 1) * 128]),
                      reads=[bp], writes=[b_xnqT])


        E("sync", lambda e: e.dma_start(out=A_ap, in_=ccout.ap()[:, :]),
          reads=[b_ccout], writes=[b_A], dma="auto")
        E("vector", lambda e: e.tensor_scalar(A_ap, A_ap, 0.5, 0.5, OP.mult, OP.add),
          reads=[b_A], writes=[b_A])

        gBs = (gpool[:, 0:512], gpool[:, 2048:2560])
        fBs = (gpool[:, 512:1024], gpool[:, 2560:3072])
        tAs = (gpool[:, 1024:1536], vpb[:, 0:512])
        tBs = (gpool[:, 1536:2048], vpb[:, 512:1024])
        fbB = gpool[:, 2048:3072]

        pend_tp = []
        blkpar = [0]
        ring_old = toks(b_ring)
        b_rh[0].w = ring_old[-1] if ring_old else None
        b_rh[1].w = ring_old[-1] if ring_old else None

        def flush_tp():
            while pend_tp:
                jb0, tl0, par0 = pend_tp.pop(0)
                pst, bpt = PS[2 + par0], b_ps[2 + par0]
                for i in range(4):
                    E("tensor", lambda e, i=i, par0=par0, pst=pst: e.transpose(
                        pst[:, i * 128:(i + 1) * 128],
                        ring[:, par0 * 512 + i * 128: par0 * 512 + (i + 1) * 128], identf[:, :]),
                      reads=[b_rh[par0], b_identf], writes=[bpt])
                for i in range(4):
                    E("vector", lambda e, i=i, jb0=jb0, tl0=tl0, pst=pst: e.tensor_copy(
                        actT[:, (4 * jb0 + i) * 512 + tl0 * 128: (4 * jb0 + i) * 512 + (tl0 + 1) * 128],
                        pst[:, i * 128:(i + 1) * 128]),
                      reads=[bpt], writes=[b_actT])

        for h in range(2):
            wl_toks = toks(b_wls[0]) + toks(b_wls[1])
            for jb in range(16):
                slab = (QP1, QP2)[jb % 2]
                b_slab = (b_QP1, b_QP2)[jb % 2]
                ew = tuple(wl_toks) if (jb < 2 and slab is QP1) else ()
                E("gpsimd", lambda e, jb=jb, slab=slab: e.dma_start(
                    out=slab[:, :], in_=wg_st[jb, :, :]),
                  writes=[b_slab], dma="auto", waits=ew)
                for tl in range(4):
                    tt = 4 * h + tl
                    ps, bp = PS[tl % 2], b_ps[tl % 2]
                    for kc in range(16):
                        for nh in range(2):
                            last = (kc == 15 and nh == 1)
                            E("tensor", lambda e, kc=kc, nh=nh, tt=tt, ps=ps, slab=slab: e.matmul(
                                ps[:, nh * 512:(nh + 1) * 512],
                                xnqT[:, kc * 1024 + tt * 128: kc * 1024 + (tt + 1) * 128],
                                slab[:, kc * 1024 + nh * 512: kc * 1024 + (nh + 1) * 512],
                                start=(kc == 0), stop=last),
                              reads=[b_xnqT, b_slab],
                              writes=[bp] if (kc == 0 and nh == 0) or last else [])
                    par = blkpar[0]
                    blkpar[0] ^= 1
                    gB, fB, tA, tB = gBs[par], fBs[par], tAs[par], tBs[par]
                    bg = b_gel[par]
                    E("scalar", lambda e, ps=ps, gB=gB: e.copy(gB, ps[:, 0:512]), reads=[bp], writes=[bg])
                    E("scalar", lambda e, ps=ps, fB=fB: e.copy(fB, ps[:, 512:1024]), reads=[bp], writes=[bg])
                    E("vector", lambda e, tA=tA, gB=gB: e.tensor_mul(tA, gB, gB), reads=[bg], writes=[bg])
                    E("vector", lambda e, tA=tA, tB=tB, gB=gB: e.tensor_mul(tB, tA, gB), reads=[bg], writes=[bg])
                    E("vector", lambda e, tA=tA, tB=tB: e.tensor_scalar_mul(tA, tB, GC1),
                      reads=[bg], writes=[bg])
                    E("vector", lambda e, tA=tA, tB=tB, gB=gB: e.tensor_tensor(tB, gB, tA, OP.add),
                      reads=[bg], writes=[bg])
                    E("scalar", lambda e, tA=tA, tB=tB: e.activation(tA, tB, AF.Copy, scale=GC2),
                      reads=[bg], writes=[bg])
                    E("scalar", lambda e, tA=tA, tB=tB: e.activation(tB, tA, AF.Tanh),
                      reads=[bg], writes=[bg])
                    E("vector", lambda e, tA=tA, tB=tB: e.tensor_scalar_add(tA, tB, 1.0),
                      reads=[bg], writes=[bg])
                    E("scalar", lambda e, tA=tA, tB=tB: e.activation(tB, tA, AF.Copy, scale=0.5),
                      reads=[bg], writes=[bg])
                    E("vector", lambda e, tA=tA, tB=tB, gB=gB: e.tensor_mul(tA, gB, tB), reads=[bg], writes=[bg])
                    E("vector", lambda e, par=par, tA=tA, fB=fB: e.tensor_mul(
                        winslab[:, par * 512:(par + 1) * 512], tA, fB),
                      reads=[bg], writes=[b_winslab])
                    E("vector", lambda e, par=par: e.tensor_copy(ring[:, par * 512:(par + 1) * 512],
                                                                 winslab[:, par * 512:(par + 1) * 512]),
                      reads=[b_winslab], writes=[b_rh[par]])
                    flush_tp()
                    pend_tp.append((jb, tl, par))
            flush_tp()
            if h == 1:
                gel1_toks = tuple(toks(b_gel[1]))
                for kc in range(16):
                    E("scalar", lambda e, kc=kc: e.copy(vpb[:, :],
                                                        QTB[:, kc * 1024: kc * 1024 + 1024]),
                      reads=[b_QTB], writes=[b_vpb], waits=(gel1_toks if kc == 0 else ()))
                    E("scalar", lambda e, kc=kc: e.activation(QTB[:, kc * 1024:(kc + 1) * 1024],
                                                              vpb[:, :], AF.Copy,
                                                              scale=A_ap[:, kc:kc + 1]),
                      reads=[b_vpb, b_A], writes=[b_QTB])
            qp1_toks = toks(b_QP1)
            first_wl = [True, True]
            for dh in range(2):
                for grp in range(8):
                    sl = grp % 2
                    ew = tuple(qp1_toks) if first_wl[sl] else ()
                    first_wl[sl] = False
                    E("gpsimd", lambda e, dh=dh, grp=grp, sl=sl: e.dma_start(
                        out=QP1[:, sl * 8192:(sl + 1) * 8192], in_=wl_st[dh, grp, :, :]),
                      writes=[b_wls[sl]], dma="auto", waits=ew)
                    for k8 in range(8):
                        kc = grp * 8 + k8
                        for tl in range(4):
                            last = (kc == 63)
                            E("tensor", lambda e, kc=kc, k8=k8, tl=tl, sl=sl: e.matmul(
                                PS[tl][:, 0:512],
                                actT[:, kc * 512 + tl * 128: kc * 512 + (tl + 1) * 128],
                                QP1[:, sl * 8192 + k8 * 1024: sl * 8192 + k8 * 1024 + 512],
                                start=(kc == 0), stop=last),
                              reads=[b_actT, b_wls[sl]],
                              writes=[b_ps[tl]] if kc == 0 else [])
                            E("tensor", lambda e, kc=kc, k8=k8, tl=tl, sl=sl: e.matmul(
                                PS[tl][:, 512:1024],
                                actT[:, kc * 512 + tl * 128: kc * 512 + (tl + 1) * 128],
                                QP1[:, sl * 8192 + k8 * 1024 + 512: sl * 8192 + (k8 + 1) * 1024],
                                start=(kc == 0), stop=last),
                              reads=[b_actT, b_wls[sl]],
                              writes=[b_ps[tl]] if last else [])
                for tl in range(4):
                    tt = 4 * h + tl
                    E("scalar", lambda e, tl=tl: e.copy(fbB, PS[tl][:, :]),
                      reads=[b_ps[tl]], writes=[b_gpool], waits=tuple(toks(b_gel[1]) + toks(b_gel[0])))
                    E("scalar", lambda e, tt=tt, dh=dh: e.dma_start(
                        out=XNT[:, tt * 2048 + dh * 1024: tt * 2048 + (dh + 1) * 1024].bitcast(U16),
                        in_=fbB.bitcast(U16)),
                      reads=[b_gpool], writes=[b_XNTh[tt // 4]], dma="auto")

        rh_toks = tuple(toks(b_rh[0]) + toks(b_rh[1]))
        first_p4_ring = [True]
        for db in range(16):
            E("gpsimd", lambda e, db=db: e.dma_start(out=woutsl[:, :], in_=wout_st[db, :, :]),
              writes=[b_wos], dma="auto")
            ps, bp = PS[db % 2], b_ps[db % 2]
            for kc in range(16):
                for nh in range(2):
                    last = (kc == 15 and nh == 1)
                    E("tensor", lambda e, kc=kc, nh=nh, ps=ps: e.matmul(
                        ps[:, nh * 512:(nh + 1) * 512],
                        woutsl[:, kc * 128:(kc + 1) * 128],
                        QTB[:, kc * 1024 + nh * 512: kc * 1024 + (nh + 1) * 512],
                        start=(kc == 0), stop=last),
                      reads=[b_wos, b_QTB],
                      writes=[bp] if (kc == 0 and nh == 0) or last else [])
            E("scalar", lambda e, db=db, ps=ps: e.activation(fbB, ps[:, :], AF.Identity,
                                                             bias=boutf[:, db:db + 1]),
              reads=[bp, b_boutf], writes=[b_gpool])
            E("vector", lambda e: e.tensor_copy(ring[:, 0:1024], fbB), reads=[b_gpool], writes=[b_ring],
              waits=(rh_toks if first_p4_ring[0] else ()))
            first_p4_ring[0] = False
            pst, bpt = PS[2 + (db % 2)], b_ps[2 + (db % 2)]
            for i in range(8):
                E("tensor", lambda e, i=i, pst=pst: e.transpose(
                    pst[:, i * 128:(i + 1) * 128], ring[:, i * 128:(i + 1) * 128], identf[:, :]),
                  reads=[b_ring, b_identf], writes=[bpt])
            E("scalar", lambda e, pst=pst: e.copy(stg[:, :], pst[:, :]),
              reads=[bpt], writes=[b_stg])
            E("scalar", lambda e, db=db: e.dma_start(
                out=proj_tm.reshape([8, 128, 2048]).ap()[:, :, db * 128:(db + 1) * 128].transpose([1, 0, 2]),
                in_=stg.reshape([128, 8, 128]).ap()[:, :, :]),
              reads=[b_stg], writes=[b_projtm], dma="auto")

        projslot = gpool[:, 0:2048]
        gel0_toks = tuple(toks(b_gel[0]) + toks(b_gel[1]))
        for tt in range(NTT):
            E("sync", lambda e, tt=tt: e.dma_start(out=xslot[:, :], in_=x[tt * 128:(tt + 1) * 128, :]),
              writes=[b_x], dma="auto")
            E("sync", lambda e, tt=tt: e.dma_start(out=projslot,
                                                   in_=proj_tm.ap()[tt * 128:(tt + 1) * 128, :]),
              reads=[b_projtm], writes=[b_gpool], dma="auto",
              waits=(gel0_toks if tt == 0 else ()))
            E("vector", lambda e, tt=tt: e.tensor_scalar(sc1[:, :], xslot[:, :], mS(tt), rS(tt),
                                                         OP.subtract, OP.mult),
              reads=[b_x, b_stats], writes=[b_sc1])
            E("vector", lambda e: e.tensor_tensor(ring[:, :], sc1[:, :], xslot[:, :], OP.add),
              reads=[b_sc1, b_x], writes=[b_ring])
            E("vector", lambda e: e.tensor_tensor(sc1[:, :], ring[:, :], projslot, OP.add),
              reads=[b_ring, b_gpool], writes=[b_sc1])
            E("vector", lambda e, tt=tt: e.tensor_tensor(ring[:, :], sc1[:, :],
                                                         XNT[:, tt * 2048:(tt + 1) * 2048], OP.add),
              reads=[b_sc1, b_XNTh[tt // 4]], writes=[b_ring])
            E("sync", lambda e, tt=tt: e.dma_start(out=y[tt * 128:(tt + 1) * 128, :], in_=ring[:, :]),
              reads=[b_ring], dma="o")
        so = sch.dma_sem("o")
        sch.wait_only("gpsimd", [(so, so.count)])

        sems = sch.all_sems()
        with contextlib.ExitStack() as es2:
            for s in sems:
                s.h = es2.enter_context(nc.semaphore(s.name))
            with nc.Block() as block:
                @block.sync
                def _(e):
                    sch.run("sync", e)

                @block.vector
                def _(e):
                    sch.run("vector", e)

                @block.scalar
                def _(e):
                    sch.run("scalar", e)

                @block.tensor
                def _(e):
                    sch.run("tensor", e)

                @block.gpsimd
                def _(e):
                    sch.run("gpsimd", e)
    return nc


_NC_CACHE = [None]


def kernel(L0_attn_X, L0_attn_W_in, L0_attn_b_in, L0_attn_W_out, L0_attn_b_out,
           L0_attn_gamma, L0_attn_beta, L0_ffn_w_gating, L0_ffn_w_linear):
    X = np.ascontiguousarray(np.asarray(L0_attn_X, dtype=np.float32))
    W_in = np.asarray(L0_attn_W_in, dtype=np.float32)
    b_in = np.asarray(L0_attn_b_in, dtype=np.float32)
    W_out = np.asarray(L0_attn_W_out, dtype=np.float32)
    b_out = np.asarray(L0_attn_b_out, dtype=np.float32)
    gamma = np.asarray(L0_attn_gamma, dtype=np.float32)
    beta = np.asarray(L0_attn_beta, dtype=np.float32)
    w_g = np.asarray(L0_ffn_w_gating, dtype=np.float32)
    w_l = np.asarray(L0_ffn_w_linear, dtype=np.float32)

    cols = []
    for kc in range(16):
        cols += list(range(4096 + kc * 128, 4096 + (kc + 1) * 128))
        cols += list(range(2048 + kc * 128, 2048 + (kc + 1) * 128))
    cols += list(range(0, 2048))
    w_in_st = np.ascontiguousarray(
        W_in[:, cols].reshape(16, 128, 24, 256).transpose(2, 1, 0, 3).reshape(24, 128, 4096))
    a0 = w_g[0].reshape(16, 128, 16, 512)
    a1 = w_g[1].reshape(16, 128, 16, 512)
    wg_st = np.ascontiguousarray(
        np.concatenate([a0, a1], axis=3).transpose(2, 1, 0, 3).reshape(16, 128, 16384))
    wl_st = np.ascontiguousarray(
        w_l.reshape(8, 8, 128, 2, 1024).transpose(3, 0, 2, 1, 4).reshape(2, 8, 128, 8192))
    wout_st = np.ascontiguousarray(
        W_out.reshape(16, 128, 16, 128).transpose(2, 1, 0, 3).reshape(16, 128, 2048))
    b_in_fm = np.ascontiguousarray(b_in.reshape(48, 128).T)
    b_out_fm = np.ascontiguousarray(b_out.reshape(16, 128).T)
    gam_rep = np.ascontiguousarray(np.broadcast_to(gamma, (128, D)))
    bet_rep = np.ascontiguousarray(np.broadcast_to(beta, (128, D)))
    ident = np.eye(128, dtype=np.float32)

    shared = dict(w_in_st=w_in_st, wg_st=wg_st, wl_st=wl_st, wout_st=wout_st,
                  b_in_fm=b_in_fm, b_out_fm=b_out_fm, gam_rep=gam_rep,
                  bet_rep=bet_rep, ident=ident)
    in_maps = []
    for c in range(NCORE):
        b = c // 4
        r0 = (c % 4) * TOK
        cos_fm, sin_fm = _rotary_tables(c)
        m = dict(shared)
        m["x"] = np.ascontiguousarray(X[b, r0:r0 + TOK, :])
        m["tabs_st"] = np.ascontiguousarray(np.concatenate([cos_fm, sin_fm], axis=2))
        in_maps.append(m)

    if _NC_CACHE[0] is None:
        _NC_CACHE[0] = build_nc()
    nc = _NC_CACHE[0]
    res = run_bass_kernel_spmd(nc, in_maps, core_ids=list(range(NCORE)))
    out = np.empty((B, S, D), dtype=np.float32)
    for c in range(NCORE):
        b = c // 4
        r0 = (c % 4) * TOK
        out[b, r0:r0 + TOK, :] = res.results[c]["y"]
    return out


# version 09
# speedup vs baseline: 2.1479x; 1.0474x over previous
import contextlib
import numpy as np
import concourse.bass as bass
import concourse.mybir as mybir
from concourse.bass_utils import run_bass_kernel_spmd

F32 = mybir.dt.float32
BF16 = mybir.dt.bfloat16
FP8 = mybir.dt.float8e4
U16 = mybir.dt.uint16
AF = mybir.ActivationFunctionType
OP = mybir.AluOpType
AX = mybir.AxisListType.X

B, S, D = 2, 4096, 2048
NCORE = 8
TOK = 1024
NTT = 8
EPS = 1e-5
INV_D = float(1.0 / 2048.0)
GC1 = 0.0446777344
GC2 = 0.796875
SWAP32 = [i ^ 1 for i in range(32)]

ENGS = ("sync", "vector", "scalar", "tensor", "gpsimd")


class Sem:
    __slots__ = ("name", "count", "h")

    def __init__(self, name):
        self.name, self.count, self.h = name, 0, None


class Buf:
    __slots__ = ("name", "w", "readers")

    def __init__(self, name):
        self.name, self.w, self.readers = name, None, []


class Sched:
    def __init__(self):
        self.q = {e: [] for e in ENGS}
        self.esem = {e: Sem("E_" + e) for e in ENGS}
        self.dsem = {}
        self.xsem = []

    def all_sems(self):
        return list(self.esem.values()) + list(self.dsem.values()) + self.xsem

    def dma_sem(self, name):
        if name not in self.dsem:
            self.dsem[name] = Sem("D_" + name)
        return self.dsem[name]

    def new_sem(self, name):
        s = Sem(name)
        self.xsem.append(s)
        return s

    def emit(self, eng, build, reads=(), writes=(), dma=None, sem=None, waits=()):
        wmap = {}

        def addw(tok):
            if tok is None:
                return
            s, v = tok
            if wmap.get(s, -1) < v:
                wmap[s] = v

        for b in reads:
            addw(b.w)
        for b in writes:
            addw(b.w)
            for t in b.readers:
                addw(t)
        for t in waits:
            addw(t)
        if dma is not None:
            if dma == "auto":
                assert writes, "auto dma sem needs a write buffer"
                dma = "b_" + writes[0].name
            s = self.dma_sem(dma)
            s.count += 16
            tok = (s, s.count)
            amt = 16
        elif sem is not None:
            s = sem
            s.count += 1
            tok = (s, s.count)
            amt = 1
        else:
            s = self.esem[eng]
            s.count += 1
            tok = (s, s.count)
            amt = 1
        self.q[eng].append((build, tuple(wmap.items()), (s, amt)))
        for b in reads:
            b.readers.append(tok)
        for b in writes:
            b.w = tok
            b.readers = []
        return tok

    def wait_only(self, eng, toks):
        self.q[eng].append((None, tuple((s, v) for s, v in toks), None))

    def run(self, eng_name, engobj):
        waited = {}
        for build, waits, inc in self.q[eng_name]:
            for s, v in waits:
                if waited.get(s, 0) < v:
                    engobj.wait_ge(s.h, v)
                    waited[s] = v
            if build is None:
                continue
            ins = build(engobj)
            s, amt = inc
            ins.then_inc(s.h, amt)
            if waited.get(s, 0) < s.count:
                pass


def _rotary_tables(core):
    rot = D // 2
    inv_freq = (1.0 / (np.float32(10000.0) **
                       (np.arange(0, rot, 2, dtype=np.float32) / np.float32(rot)))).astype(np.float32)
    p0 = (core % 4) * TOK
    pos = np.arange(p0, p0 + TOK, dtype=np.float32)
    fr = (pos[:, None] * inv_freq[None, :]).astype(np.float32)
    fr = np.repeat(fr, 2, axis=1)
    cos = np.cos(fr).astype(np.float32)
    sin = np.sin(fr).astype(np.float32)
    sinS = sin.copy()
    sinS[:, 0::2] = -sin[:, 0::2]
    cos_fm = np.ascontiguousarray(cos.T.reshape(8, 128, TOK))
    sin_fm = np.ascontiguousarray(sinS.T.reshape(8, 128, TOK))
    return cos_fm, sin_fm


def build_nc():
    nc = bass.Bass("TRN2", target_bir_lowering=False, debug=False)

    def din(name, shape):
        return nc.dram_tensor(name, list(shape), F32, kind="ExternalInput").ap()

    x = din("x", (TOK, D))
    y = nc.dram_tensor("y", [TOK, D], F32, kind="ExternalOutput").ap()
    w_in_st = din("w_in_st", (24, 128, 4096))
    wg_st = din("wg_st", (16, 128, 16384))
    wl_st = din("wl_st", (2, 8, 128, 8192))
    wout_st = din("wout_st", (16, 128, 2048))
    b_in_fm = din("b_in_fm", (128, 48))
    b_out_fm = din("b_out_fm", (128, 16))
    gam_rep = din("gam_rep", (128, D))
    bet_rep = din("bet_rep", (128, D))
    tabs_st = din("tabs_st", (8, 128, 2048))
    ident = din("ident", (128, 128))
    ccin = nc.dram_tensor("ccin", [128, 16], F32)
    ccout = nc.dram_tensor("ccout", [128, 16], F32)
    proj_tm = nc.dram_tensor("proj_tm", [TOK, D], BF16)

    es = contextlib.ExitStack()
    sch = Sched()
    with es:
        T = lambda n, shp, dt: es.enter_context(nc.sbuf_tensor(n, list(shp), dt))
        xslot = T("xslot", (128, D), F32)
        xnt = T("xnt", (128, D), F32)
        sc1 = T("sc1", (128, D), F32)
        ring = T("ring", (128, D), F32)
        tab = T("tab", (128, D), F32)
        stats = T("stats", (128, 64), F32)
        winslab = T("winslab", (128, 4096), BF16)
        XNT = T("XNT", (128, 16384), BF16)
        QTB = T("QTB", (128, 16384), BF16)
        gpool = T("gpool", (128, 3072), BF16)
        vpb = T("vpb", (128, 1024), BF16)
        stg = T("stg", (128, 1024), BF16)
        identf = T("identf", (128, 128), F32)
        onesb = T("onesb", (128, 1), F32)
        onesbf = T("onesbf", (128, 1), BF16)
        ones1 = T("ones1", (1, 128), F32)
        binf = T("binf", (128, 48), F32)
        boutf = T("boutf", (128, 16), F32)
        QP1 = T("QP1", (128, 16384), FP8)
        QP2 = T("QP2", (128, 16384), FP8)
        xnqT = T("xnqT", (128, 16384), FP8)
        actT = T("actT", (128, 32768), FP8)
        woutsl = T("woutsl", (128, 2048), BF16)
        PS = [es.enter_context(nc.psum_tensor(f"ps{i}", [128, 1024], F32)) for i in range(4)]

        bufs = {}

        def Bf(n):
            bufs[n] = Buf(n)
            return bufs[n]

        b_x, b_xn, b_sc1, b_ring, b_tab = Bf("x"), Bf("xn"), Bf("sc1"), Bf("ring"), Bf("tab")
        b_small, b_stats = Bf("small"), Bf("stats")
        b_winslab, b_QTB, b_gpool = Bf("winslab"), Bf("QTB"), Bf("gpool")
        b_XNTh = [Bf("XNTh0"), Bf("XNTh1")]
        b_vpb, b_stg = Bf("vpb"), Bf("stg")
        b_slY = Bf("slY")
        b_rh = [Bf("rh0"), Bf("rh1")]
        b_gel = [Bf("gel0"), Bf("gel1")]
        b_identf, b_onesb, b_ones1, b_binf, b_boutf = Bf("identf"), Bf("onesb"), Bf("ones1"), Bf("binf"), Bf("boutf")
        b_onesbf = Bf("onesbf")
        b_QP1, b_QP2, b_xnqT, b_actT = Bf("QP1"), Bf("QP2"), Bf("xnqT"), Bf("actT")
        b_wls = [Bf("wls0"), Bf("wls1")]
        b_wos = Bf("wos")
        b_ps = [Bf(f"ps{i}") for i in range(4)]
        b_ccin, b_ccout, b_projtm = Bf("ccin"), Bf("ccout"), Bf("projtm")

        E = sch.emit

        def toks(buf):
            t = list(buf.readers)
            if buf.w:
                t.append(buf.w)
            return t

        E("sync", lambda e: e.dma_start(out=identf[:, :], in_=ident[:, :]), writes=[b_identf], dma="auto")
        E("sync", lambda e: e.dma_start(out=binf[:, :], in_=b_in_fm[:, :]), writes=[b_binf], dma="auto")
        E("sync", lambda e: e.dma_start(out=boutf[:, :], in_=b_out_fm[:, :]), writes=[b_boutf], dma="auto")
        E("gpsimd", lambda e: e.dma_start(out=gpool[:, 0:2048], in_=gam_rep[:, :]), writes=[b_gpool], dma="auto")
        E("gpsimd", lambda e: e.dma_start(out=winslab[:, 0:2048], in_=bet_rep[:, :]), writes=[b_winslab], dma="auto")
        E("vector", lambda e: e.memset(onesb[:, :], 1.0), writes=[b_onesb])
        E("vector", lambda e: e.memset(onesbf[:, :], 1.0), writes=[b_onesbf])
        E("vector", lambda e: e.memset(ones1[:, :], 1.0), writes=[b_ones1])

        mS = lambda tt: stats[:, tt:tt + 1]
        rS = lambda tt: stats[:, 8 + tt:9 + tt]
        t0 = stats[:, 16:17]; t1 = stats[:, 17:18]; t2 = stats[:, 18:19]
        t3 = stats[:, 19:20]; t4 = stats[:, 20:21]; t5 = stats[:, 21:22]
        pA_ap = stats[:, 32:48]
        A_ap = stats[:, 48:64]
        b_pA, b_A = Bf("pA"), Bf("A")

        def rsqrt_into(src_ap, dst_ap, rb, wb):
            E("scalar", lambda e: e.activation(t3, src_ap, AF.Sqrt), reads=[rb], writes=[b_stats])
            E("vector", lambda e: e.reciprocal(t4, t3), reads=[b_stats], writes=[b_stats])
            E("vector", lambda e: e.tensor_mul(t5, t4, t4), reads=[b_stats], writes=[b_stats])
            E("vector", lambda e: e.tensor_mul(t5, src_ap, t5), reads=[rb, b_stats], writes=[b_stats])
            E("vector", lambda e: e.tensor_scalar(t5, t5, -0.5, 1.5, OP.mult, OP.add),
              reads=[b_stats], writes=[b_stats])
            E("vector", lambda e: e.tensor_mul(dst_ap, t4, t5), reads=[b_stats], writes=[wb])

        sAll = stats[:, 16:24]; qAll = stats[:, 24:32]
        mAll = stats[:, 0:8]; rAll = stats[:, 8:16]
        uA = stats[:, 48:56]; m2A = stats[:, 56:64]
        w0a = stats[:, 32:40]; w1a = stats[:, 40:48]
        for tt in range(NTT):
            E("sync", lambda e, tt=tt: e.dma_start(out=xslot[:, :], in_=x[tt * 128:(tt + 1) * 128, :]),
              writes=[b_x], dma="auto")
            E("vector", lambda e, tt=tt: e.reduce_sum(stats[:, 16 + tt:17 + tt], xslot[:, :], axis=AX),
              reads=[b_x], writes=[b_stats])
            E("vector", lambda e, tt=tt: e.scalar_tensor_tensor(ring[:, :], xslot[:, :], 1.0, xslot[:, :],
                                                                OP.mult, OP.mult,
                                                                accum_out=stats[:, 24 + tt:25 + tt]),
              reads=[b_x], writes=[b_ring, b_stats])
        E("scalar", lambda e: e.activation(mAll, sAll, AF.Identity, scale=INV_D),
          reads=[b_stats], writes=[b_stats])
        E("vector", lambda e: e.tensor_scalar(uA, qAll, INV_D, float(EPS), OP.mult, OP.add),
          reads=[b_stats], writes=[b_stats])
        E("vector", lambda e: e.tensor_mul(w0a, mAll, mAll), reads=[b_stats], writes=[b_stats])
        E("vector", lambda e: e.tensor_sub(uA, uA, w0a), reads=[b_stats], writes=[b_stats])
        E("scalar", lambda e: e.activation(w0a, uA, AF.Sqrt), reads=[b_stats], writes=[b_stats])
        E("vector", lambda e: e.reciprocal(w1a, w0a), reads=[b_stats], writes=[b_stats])
        E("vector", lambda e: e.tensor_mul(w0a, w1a, w1a), reads=[b_stats], writes=[b_stats])
        E("vector", lambda e: e.tensor_mul(w0a, uA, w0a), reads=[b_stats], writes=[b_stats])
        E("vector", lambda e: e.tensor_scalar(w0a, w0a, -0.5, 1.5, OP.mult, OP.add),
          reads=[b_stats], writes=[b_stats])
        E("vector", lambda e: e.tensor_mul(rAll, w1a, w0a), reads=[b_stats], writes=[b_stats])
        E("vector", lambda e: e.tensor_scalar(w0a, sAll, INV_D, None, OP.mult),
          reads=[b_stats], writes=[b_stats])
        E("vector", lambda e: e.tensor_sub(w0a, w0a, mAll), reads=[b_stats], writes=[b_stats])
        E("vector", lambda e: e.tensor_mul(m2A, w0a, rAll), reads=[b_stats], writes=[b_stats])
        E("vector", lambda e: e.tensor_scalar(w0a, uA, 1.0, -float(EPS), OP.mult, OP.add),
          reads=[b_stats], writes=[b_stats])
        E("vector", lambda e: e.tensor_mul(w1a, rAll, rAll), reads=[b_stats], writes=[b_stats])
        E("vector", lambda e: e.tensor_mul(w0a, w0a, w1a), reads=[b_stats], writes=[b_stats])
        E("vector", lambda e: e.tensor_mul(w1a, m2A, m2A), reads=[b_stats], writes=[b_stats])
        E("vector", lambda e: e.tensor_sub(w0a, w0a, w1a), reads=[b_stats], writes=[b_stats])
        E("vector", lambda e: e.tensor_scalar(w0a, w0a, 1.0, float(EPS), OP.mult, OP.add),
          reads=[b_stats], writes=[b_stats])
        E("scalar", lambda e: e.activation(w1a, w0a, AF.Sqrt), reads=[b_stats], writes=[b_stats])
        E("vector", lambda e: e.reciprocal(uA, w1a), reads=[b_stats], writes=[b_stats])
        E("vector", lambda e: e.tensor_mul(w1a, uA, uA), reads=[b_stats], writes=[b_stats])
        E("vector", lambda e: e.tensor_mul(w1a, w0a, w1a), reads=[b_stats], writes=[b_stats])
        E("vector", lambda e: e.tensor_scalar(w1a, w1a, -0.5, 1.5, OP.mult, OP.add),
          reads=[b_stats], writes=[b_stats])
        E("vector", lambda e: e.tensor_mul(uA, uA, w1a), reads=[b_stats], writes=[b_stats])
        for tt in range(NTT):
            E("sync", lambda e, tt=tt: e.dma_start(out=xslot[:, :], in_=x[tt * 128:(tt + 1) * 128, :]),
              writes=[b_x], dma="auto")
            E("vector", lambda e, tt=tt: e.tensor_scalar(xnt[:, :], xslot[:, :], mS(tt), rS(tt),
                                                         OP.subtract, OP.mult),
              reads=[b_x, b_stats], writes=[b_xn])
            E("scalar", lambda e, tt=tt: e.copy(QP1[:, tt * 2048:(tt + 1) * 2048], xnt[:, :]),
              reads=[b_xn], writes=[b_QP1])
            E("vector", lambda e, tt=tt: e.scalar_tensor_tensor(ring[:, :], xnt[:, :],
                                                                stats[:, 56 + tt:57 + tt], gpool[:, 0:2048],
                                                                OP.subtract, OP.mult),
              reads=[b_xn, b_stats, b_gpool], writes=[b_ring])
            E("vector", lambda e, tt=tt: e.tensor_scalar(ring[:, :], ring[:, :],
                                                         stats[:, 48 + tt:49 + tt], None, OP.mult),
              reads=[b_ring, b_stats], writes=[b_ring])
            E("vector", lambda e: e.tensor_tensor(sc1[:, :], ring[:, :], winslab[:, 0:2048], OP.add),
              reads=[b_ring, b_winslab], writes=[b_sc1])
            for half in range(2):
                ps, bp = PS[2 + half], b_ps[2 + half]
                for k in range(8):
                    kc = half * 8 + k
                    E("tensor", lambda e, kc=kc, k=k, ps=ps: e.transpose(
                        ps[:, k * 128:(k + 1) * 128], sc1[:, kc * 128:(kc + 1) * 128], identf[:, :]),
                      reads=[b_sc1, b_identf], writes=[bp])
                for k in range(8):
                    kc = half * 8 + k
                    E("scalar", lambda e, kc=kc, k=k, tt=tt, ps=ps: e.copy(
                        XNT[:, kc * 1024 + tt * 128: kc * 1024 + (tt + 1) * 128],
                        ps[:, k * 128:(k + 1) * 128]),
                      reads=[bp], writes=[b_XNTh[tt // 4]])

        gam_done = [None]

        def load_winslab(g):
            if g % 2 == 0:
                E("gpsimd", lambda e, g=g: e.dma_start(out=winslab[:, :], in_=w_in_st[g, :, :]),
                  writes=[b_winslab], dma="auto")
            else:
                ew = tuple(gam_done[0]) if g == 1 else ()
                E("gpsimd", lambda e, g=g: e.dma_start(out=gpool[:, 0:3072], in_=w_in_st[g, :, 0:3072]),
                  writes=[b_slY], dma="auto", waits=ew)
                E("gpsimd", lambda e, g=g: e.dma_start(out=stg[:, :], in_=w_in_st[g, :, 3072:4096]),
                  writes=[b_slY], dma="auto", waits=ew)

        def slab_ap(g, kc, oc):
            base = kc * 256 + oc * 128
            if g % 2 == 0:
                return winslab[:, base: base + 128], b_winslab
            if base < 3072:
                return gpool[:, base: base + 128], b_slY
            return stg[:, base - 3072: base - 3072 + 128], b_slY

        def qkv_mms(g, oc01, ps, bp):
            for nh in range(2):
                for kc in range(16):
                    last = (kc == 15 and nh == 1)
                    sap, sbuf_b = slab_ap(g, kc, oc01)
                    E("tensor", lambda e, sap=sap, kc=kc, nh=nh, ps=ps: e.matmul(
                        ps[:, nh * 512:(nh + 1) * 512],
                        sap,
                        XNT[:, kc * 1024 + nh * 512: kc * 1024 + (nh + 1) * 512],
                        start=(kc == 0), stop=last),
                      reads=[sbuf_b, b_XNTh[nh]],
                      writes=[bp] if (kc == 0 and nh == 0) or last else [])

        nv2 = xslot[0:1, 0:1024]
        nk2 = xslot[0:1, 1024:2048]
        rv_ = xslot[0:1, 0:1024]
        rk_ = xslot[0:1, 1024:2048]
        s3_ = tab[0:1, 0:1024]
        s4_ = ring[0:1, 0:1024]
        w1_ = ring[0:1, 1024:2048]
        E("vector", lambda e: e.memset(xslot[0:1, 0:2048], 0.0), writes=[b_x])

        r0ap = xnt[:, 0:1024]
        r1ap = xnt[:, 1024:2048]
        m1ap = ring[:, 0:1024]
        m2ap = ring[:, 1024:2048]
        k1ap = sc1[:, 0:1024]

        def norm_accum(dst_row, src_ap, b_src):
            for nh in range(2):
                E("tensor", lambda e, nh=nh, src_ap=src_ap: e.matmul(
                    PS[2][0:1, nh * 512:(nh + 1) * 512], onesbf[:, :],
                    src_ap[:, nh * 512:(nh + 1) * 512], start=True, stop=True),
                  reads=[b_src, b_onesbf], writes=[b_ps[2]])
            E("vector", lambda e, dst_row=dst_row: e.tensor_tensor(dst_row, dst_row, PS[2][0:1, :], OP.add),
              reads=[b_ps[2], b_x], writes=[b_x])

        pending_norms = []
        gam_done[0] = toks(b_gpool) + toks(b_stg)

        def flush_norms():
            if pending_norms:
                pending_norms.pop()
                norm_accum(nv2, woutsl[:, 0:1024], b_wos)
                norm_accum(nk2, woutsl[:, 1024:2048], b_wos)

        def emit_p2():
            def rsqrt_row(src, dst):
                E("scalar", lambda e: e.activation(s3_, src, AF.Sqrt), reads=[b_x], writes=[b_tab])
                E("vector", lambda e: e.reciprocal(s4_, s3_), reads=[b_tab], writes=[b_ring])
                E("vector", lambda e: e.tensor_mul(s3_, s4_, s4_), reads=[b_ring], writes=[b_tab])
                E("vector", lambda e: e.tensor_mul(s3_, src, s3_), reads=[b_x, b_tab], writes=[b_tab])
                E("vector", lambda e: e.tensor_scalar(s3_, s3_, -0.5, 1.5, OP.mult, OP.add),
                  reads=[b_tab], writes=[b_tab])
                E("vector", lambda e: e.tensor_mul(dst, s4_, s3_), reads=[b_ring, b_tab], writes=[b_x])

            rsqrt_row(nv2, rv_)
            rsqrt_row(nk2, rk_)
            E("vector", lambda e: e.tensor_mul(w1_, rk_, rv_), reads=[b_x], writes=[b_ring])
            for nh in range(2):
                E("tensor", lambda e, nh=nh: e.matmul(PS[2][:, nh * 512:(nh + 1) * 512], ones1[:, :],
                                                      w1_[:, nh * 512:(nh + 1) * 512], start=True, stop=True),
                  reads=[b_ring, b_ones1], writes=[b_ps[2]])
            wrep = tab[:, 0:1024]
            E("vector", lambda e: e.tensor_copy(wrep, PS[2][:, :]), reads=[b_ps[2]], writes=[b_tab])
            for kc in range(16):
                E("vector", lambda e, kc=kc: e.scalar_tensor_tensor(
                    ring[:, 0:1024], QP2[:, kc * 1024:(kc + 1) * 1024], 1.0, wrep,
                    OP.mult, OP.mult, accum_out=pA_ap[:, kc:kc + 1]),
                  reads=[b_QP2, b_tab], writes=[b_ring, b_pA])
            E("gpsimd", lambda e: e.dma_start(out=ccin.ap()[:, :], in_=pA_ap),
              reads=[b_pA], writes=[b_ccin], dma="auto")
            cc_sem = sch.new_sem("ccdone")
            E("gpsimd", lambda e: e.collective_compute(
                "AllReduce", OP.add, replica_groups=[[0, 1, 2, 3], [4, 5, 6, 7]],
                ins=[ccin.ap().opt()], outs=[ccout.ap().opt()]),
              reads=[b_ccin], writes=[b_ccout], sem=cc_sem)

        for g in range(24):
            load_winslab(g)
            qkv_mms(g, 0, PS[0], b_ps[0])
            qkv_mms(g, 1, PS[1], b_ps[1])
            flush_norms()
            if g == 16:
                emit_p2()
            if g < 16:
                kc = g
                if kc < 8:
                    E("sync", lambda e, kc=kc: e.dma_start(out=tab[:, :], in_=tabs_st[kc, :, :]),
                      writes=[b_tab], dma="auto")
                    E("scalar", lambda e, kc=kc: e.activation(r0ap, PS[0][:, :], AF.Identity,
                                                              bias=binf[:, 32 + kc:33 + kc]),
                      reads=[b_ps[0], b_binf], writes=[b_xn])
                    E("vector", lambda e: e.stream_shuffle(r1ap, r0ap, SWAP32),
                      reads=[b_xn], writes=[b_xn])
                    E("vector", lambda e: e.tensor_tensor(m1ap, r0ap, tab[:, 0:1024], OP.mult),
                      reads=[b_xn, b_tab], writes=[b_ring])
                    E("vector", lambda e: e.tensor_tensor(m2ap, r1ap, tab[:, 1024:2048], OP.mult),
                      reads=[b_xn, b_tab], writes=[b_ring])
                    E("vector", lambda e: e.tensor_tensor(vpb[:, :], m1ap, m2ap, OP.add),
                      reads=[b_ring], writes=[b_vpb])
                    E("scalar", lambda e, kc=kc: e.activation(r0ap, PS[1][:, :], AF.Identity,
                                                              bias=binf[:, 16 + kc:17 + kc]),
                      reads=[b_ps[1], b_binf], writes=[b_xn])
                    E("vector", lambda e: e.stream_shuffle(r1ap, r0ap, SWAP32),
                      reads=[b_xn], writes=[b_xn])
                    E("vector", lambda e: e.tensor_tensor(k1ap, r0ap, tab[:, 0:1024], OP.mult),
                      reads=[b_xn, b_tab], writes=[b_sc1])
                    E("vector", lambda e: e.tensor_tensor(m2ap, r1ap, tab[:, 1024:2048], OP.mult),
                      reads=[b_xn, b_tab], writes=[b_ring])
                    E("vector", lambda e: e.tensor_tensor(m1ap, k1ap, m2ap, OP.add),
                      reads=[b_sc1, b_ring], writes=[b_ring])
                else:
                    E("scalar", lambda e, kc=kc: e.activation(vpb[:, :], PS[0][:, :], AF.Identity,
                                                              bias=binf[:, 32 + kc:33 + kc]),
                      reads=[b_ps[0], b_binf], writes=[b_vpb])
                    E("scalar", lambda e, kc=kc: e.activation(m1ap, PS[1][:, :], AF.Identity,
                                                              bias=binf[:, 16 + kc:17 + kc]),
                      reads=[b_ps[1], b_binf], writes=[b_ring])
                E("vector", lambda e, kc=kc: e.tensor_tensor(QP2[:, kc * 1024:(kc + 1) * 1024],
                                                             m1ap, vpb[:, :], OP.mult),
                  reads=[b_ring, b_vpb], writes=[b_QP2])
                E("vector", lambda e: e.tensor_tensor(woutsl[:, 0:1024], vpb[:, :], vpb[:, :], OP.mult),
                  reads=[b_vpb], writes=[b_wos])
                E("vector", lambda e: e.tensor_tensor(woutsl[:, 1024:2048], m1ap, m1ap, OP.mult),
                  reads=[b_ring], writes=[b_wos])
                pending_norms.append(True)
            else:
                q = g - 16
                for j in range(2):
                    oc = 2 * q + j
                    ps, bp = PS[j], b_ps[j]
                    E("scalar", lambda e, oc=oc, ps=ps: e.activation(
                        QTB[:, oc * 1024:(oc + 1) * 1024], ps[:, :], AF.Identity,
                        bias=binf[:, oc:oc + 1]),
                      reads=[bp, b_binf], writes=[b_QTB])

        for tt in range(NTT):
            E("vector", lambda e, tt=tt: e.tensor_copy(sc1[:, :], QP1[:, tt * 2048:(tt + 1) * 2048]),
              reads=[b_QP1], writes=[b_sc1])
            for half in range(2):
                ps, bp = PS[2 + half], b_ps[2 + half]
                for k in range(8):
                    kc = half * 8 + k
                    E("tensor", lambda e, kc=kc, k=k, ps=ps: e.transpose(
                        ps[:, k * 128:(k + 1) * 128], sc1[:, kc * 128:(kc + 1) * 128], identf[:, :]),
                      reads=[b_sc1, b_identf], writes=[bp])
                for k in range(8):
                    kc = half * 8 + k
                    E("scalar", lambda e, kc=kc, k=k, tt=tt, ps=ps: e.copy(
                        xnqT[:, kc * 1024 + tt * 128: kc * 1024 + (tt + 1) * 128],
                        ps[:, k * 128:(k + 1) * 128]),
                      reads=[bp], writes=[b_xnqT])


        E("sync", lambda e: e.dma_start(out=A_ap, in_=ccout.ap()[:, :]),
          reads=[b_ccout], writes=[b_A], dma="auto")
        E("vector", lambda e: e.tensor_scalar(A_ap, A_ap, 0.5, 0.5, OP.mult, OP.add),
          reads=[b_A], writes=[b_A])

        gBs = (gpool[:, 0:512], gpool[:, 2048:2560])
        fBs = (gpool[:, 512:1024], gpool[:, 2560:3072])
        tAs = (gpool[:, 1024:1536], vpb[:, 0:512])
        tBs = (gpool[:, 1536:2048], vpb[:, 512:1024])
        fbB = gpool[:, 2048:3072]

        pend_tp = []
        blkpar = [0]
        slabY_need = [2]
        slabY_toks = tuple(toks(b_slY))
        ring_old = toks(b_ring)
        b_rh[0].w = ring_old[-1] if ring_old else None
        b_rh[1].w = ring_old[-1] if ring_old else None

        def flush_tp():
            while pend_tp:
                jb0, tl0, par0 = pend_tp.pop(0)
                pst, bpt = PS[2 + par0], b_ps[2 + par0]
                for i in range(4):
                    E("tensor", lambda e, i=i, par0=par0, pst=pst: e.transpose(
                        pst[:, i * 128:(i + 1) * 128],
                        ring[:, par0 * 512 + i * 128: par0 * 512 + (i + 1) * 128], identf[:, :]),
                      reads=[b_rh[par0], b_identf], writes=[bpt])
                for i in range(4):
                    E("vector", lambda e, i=i, jb0=jb0, tl0=tl0, pst=pst: e.tensor_copy(
                        actT[:, (4 * jb0 + i) * 512 + tl0 * 128: (4 * jb0 + i) * 512 + (tl0 + 1) * 128],
                        pst[:, i * 128:(i + 1) * 128]),
                      reads=[bpt], writes=[b_actT])

        for h in range(2):
            wl_toks = toks(b_wls[0]) + toks(b_wls[1])
            for jb in range(16):
                slab = (QP1, QP2)[jb % 2]
                b_slab = (b_QP1, b_QP2)[jb % 2]
                ew = tuple(wl_toks) if (jb < 2 and slab is QP1) else ()
                E("gpsimd", lambda e, jb=jb, slab=slab: e.dma_start(
                    out=slab[:, :], in_=wg_st[jb, :, :]),
                  writes=[b_slab], dma="auto", waits=ew)
                for tl in range(4):
                    tt = 4 * h + tl
                    ps, bp = PS[tl % 2], b_ps[tl % 2]
                    for kc in range(16):
                        for nh in range(2):
                            last = (kc == 15 and nh == 1)
                            E("tensor", lambda e, kc=kc, nh=nh, tt=tt, ps=ps, slab=slab: e.matmul(
                                ps[:, nh * 512:(nh + 1) * 512],
                                xnqT[:, kc * 1024 + tt * 128: kc * 1024 + (tt + 1) * 128],
                                slab[:, kc * 1024 + nh * 512: kc * 1024 + (nh + 1) * 512],
                                start=(kc == 0), stop=last),
                              reads=[b_xnqT, b_slab],
                              writes=[bp] if (kc == 0 and nh == 0) or last else [])
                    par = blkpar[0]
                    blkpar[0] ^= 1
                    gB, fB, tA, tB = gBs[par], fBs[par], tAs[par], tBs[par]
                    bg = b_gel[par]
                    E("scalar", lambda e, ps=ps, gB=gB: e.copy(gB, ps[:, 0:512]), reads=[bp], writes=[bg],
                      waits=(slabY_toks if slabY_need[0] > 0 else ()))
                    E("scalar", lambda e, ps=ps, fB=fB: e.copy(fB, ps[:, 512:1024]), reads=[bp], writes=[bg])
                    slabY_need[0] -= 1
                    E("vector", lambda e, tA=tA, gB=gB: e.tensor_mul(tA, gB, gB), reads=[bg], writes=[bg])
                    E("vector", lambda e, tA=tA, tB=tB, gB=gB: e.tensor_mul(tB, tA, gB), reads=[bg], writes=[bg])
                    E("vector", lambda e, tA=tA, tB=tB: e.tensor_scalar_mul(tA, tB, GC1),
                      reads=[bg], writes=[bg])
                    E("vector", lambda e, tA=tA, tB=tB, gB=gB: e.tensor_tensor(tB, gB, tA, OP.add),
                      reads=[bg], writes=[bg])
                    E("scalar", lambda e, tA=tA, tB=tB: e.activation(tA, tB, AF.Copy, scale=GC2),
                      reads=[bg], writes=[bg])
                    E("scalar", lambda e, tA=tA, tB=tB: e.activation(tB, tA, AF.Tanh),
                      reads=[bg], writes=[bg])
                    E("vector", lambda e, tA=tA, tB=tB: e.tensor_scalar_add(tA, tB, 1.0),
                      reads=[bg], writes=[bg])
                    E("scalar", lambda e, tA=tA, tB=tB: e.activation(tB, tA, AF.Copy, scale=0.5),
                      reads=[bg], writes=[bg])
                    E("vector", lambda e, tA=tA, tB=tB, gB=gB: e.tensor_mul(tA, gB, tB), reads=[bg], writes=[bg])
                    E("vector", lambda e, par=par, tA=tA, fB=fB: e.tensor_mul(
                        winslab[:, par * 512:(par + 1) * 512], tA, fB),
                      reads=[bg], writes=[b_winslab])
                    E("vector", lambda e, par=par: e.tensor_copy(ring[:, par * 512:(par + 1) * 512],
                                                                 winslab[:, par * 512:(par + 1) * 512]),
                      reads=[b_winslab], writes=[b_rh[par]])
                    flush_tp()
                    pend_tp.append((jb, tl, par))
            flush_tp()
            if h == 1:
                gel1_toks = tuple(toks(b_gel[1]))
                for kc in range(16):
                    E("scalar", lambda e, kc=kc: e.copy(vpb[:, :],
                                                        QTB[:, kc * 1024: kc * 1024 + 1024]),
                      reads=[b_QTB], writes=[b_vpb], waits=(gel1_toks if kc == 0 else ()))
                    E("scalar", lambda e, kc=kc: e.activation(QTB[:, kc * 1024:(kc + 1) * 1024],
                                                              vpb[:, :], AF.Copy,
                                                              scale=A_ap[:, kc:kc + 1]),
                      reads=[b_vpb, b_A], writes=[b_QTB])
            qp1_toks = toks(b_QP1)
            first_wl = [True, True]
            for dh in range(2):
                for grp in range(8):
                    sl = grp % 2
                    ew = tuple(qp1_toks) if first_wl[sl] else ()
                    first_wl[sl] = False
                    E("gpsimd", lambda e, dh=dh, grp=grp, sl=sl: e.dma_start(
                        out=QP1[:, sl * 8192:(sl + 1) * 8192], in_=wl_st[dh, grp, :, :]),
                      writes=[b_wls[sl]], dma="auto", waits=ew)
                    for k8 in range(8):
                        kc = grp * 8 + k8
                        for tl in range(4):
                            last = (kc == 63)
                            E("tensor", lambda e, kc=kc, k8=k8, tl=tl, sl=sl: e.matmul(
                                PS[tl][:, 0:512],
                                actT[:, kc * 512 + tl * 128: kc * 512 + (tl + 1) * 128],
                                QP1[:, sl * 8192 + k8 * 1024: sl * 8192 + k8 * 1024 + 512],
                                start=(kc == 0), stop=last),
                              reads=[b_actT, b_wls[sl]],
                              writes=[b_ps[tl]] if kc == 0 else [])
                            E("tensor", lambda e, kc=kc, k8=k8, tl=tl, sl=sl: e.matmul(
                                PS[tl][:, 512:1024],
                                actT[:, kc * 512 + tl * 128: kc * 512 + (tl + 1) * 128],
                                QP1[:, sl * 8192 + k8 * 1024 + 512: sl * 8192 + (k8 + 1) * 1024],
                                start=(kc == 0), stop=last),
                              reads=[b_actT, b_wls[sl]],
                              writes=[b_ps[tl]] if last else [])
                for tl in range(4):
                    tt = 4 * h + tl
                    E("scalar", lambda e, tl=tl: e.copy(fbB, PS[tl][:, :]),
                      reads=[b_ps[tl]], writes=[b_gpool], waits=tuple(toks(b_gel[1]) + toks(b_gel[0])))
                    E("scalar", lambda e, tt=tt, dh=dh: e.dma_start(
                        out=XNT[:, tt * 2048 + dh * 1024: tt * 2048 + (dh + 1) * 1024].bitcast(U16),
                        in_=fbB.bitcast(U16)),
                      reads=[b_gpool], writes=[b_XNTh[tt // 4]], dma="auto")

        rh_toks = tuple(toks(b_rh[0]) + toks(b_rh[1]))
        first_p4_ring = [True]
        for db in range(16):
            E("gpsimd", lambda e, db=db: e.dma_start(out=woutsl[:, :], in_=wout_st[db, :, :]),
              writes=[b_wos], dma="auto")
            ps, bp = PS[db % 2], b_ps[db % 2]
            for kc in range(16):
                for nh in range(2):
                    last = (kc == 15 and nh == 1)
                    E("tensor", lambda e, kc=kc, nh=nh, ps=ps: e.matmul(
                        ps[:, nh * 512:(nh + 1) * 512],
                        woutsl[:, kc * 128:(kc + 1) * 128],
                        QTB[:, kc * 1024 + nh * 512: kc * 1024 + (nh + 1) * 512],
                        start=(kc == 0), stop=last),
                      reads=[b_wos, b_QTB],
                      writes=[bp] if (kc == 0 and nh == 0) or last else [])
            E("scalar", lambda e, db=db, ps=ps: e.activation(fbB, ps[:, :], AF.Identity,
                                                             bias=boutf[:, db:db + 1]),
              reads=[bp, b_boutf], writes=[b_gpool])
            E("vector", lambda e: e.tensor_copy(ring[:, 0:1024], fbB), reads=[b_gpool], writes=[b_ring],
              waits=(rh_toks if first_p4_ring[0] else ()))
            first_p4_ring[0] = False
            pst, bpt = PS[2 + (db % 2)], b_ps[2 + (db % 2)]
            for i in range(8):
                E("tensor", lambda e, i=i, pst=pst: e.transpose(
                    pst[:, i * 128:(i + 1) * 128], ring[:, i * 128:(i + 1) * 128], identf[:, :]),
                  reads=[b_ring, b_identf], writes=[bpt])
            E("scalar", lambda e, pst=pst: e.copy(stg[:, :], pst[:, :]),
              reads=[bpt, b_slY], writes=[b_stg])
            E("scalar", lambda e, db=db: e.dma_start(
                out=proj_tm.reshape([8, 128, 2048]).ap()[:, :, db * 128:(db + 1) * 128].transpose([1, 0, 2]),
                in_=stg.reshape([128, 8, 128]).ap()[:, :, :]),
              reads=[b_stg], writes=[b_projtm], dma="auto")

        projslot = gpool[:, 0:2048]
        gel0_toks = tuple(toks(b_gel[0]) + toks(b_gel[1]))
        for tt in range(NTT):
            E("sync", lambda e, tt=tt: e.dma_start(out=xslot[:, :], in_=x[tt * 128:(tt + 1) * 128, :]),
              writes=[b_x], dma="auto")
            E("sync", lambda e, tt=tt: e.dma_start(out=projslot,
                                                   in_=proj_tm.ap()[tt * 128:(tt + 1) * 128, :]),
              reads=[b_projtm], writes=[b_gpool], dma="auto",
              waits=(gel0_toks if tt == 0 else ()))
            E("vector", lambda e, tt=tt: e.tensor_scalar(sc1[:, :], xslot[:, :], mS(tt), rS(tt),
                                                         OP.subtract, OP.mult),
              reads=[b_x, b_stats], writes=[b_sc1])
            E("vector", lambda e: e.tensor_tensor(ring[:, :], sc1[:, :], xslot[:, :], OP.add),
              reads=[b_sc1, b_x], writes=[b_ring])
            E("vector", lambda e: e.tensor_tensor(sc1[:, :], ring[:, :], projslot, OP.add),
              reads=[b_ring, b_gpool], writes=[b_sc1])
            E("vector", lambda e, tt=tt: e.tensor_tensor(ring[:, :], sc1[:, :],
                                                         XNT[:, tt * 2048:(tt + 1) * 2048], OP.add),
              reads=[b_sc1, b_XNTh[tt // 4]], writes=[b_ring])
            E("sync", lambda e, tt=tt: e.dma_start(out=y[tt * 128:(tt + 1) * 128, :], in_=ring[:, :]),
              reads=[b_ring], dma="o")
        so = sch.dma_sem("o")
        sch.wait_only("gpsimd", [(so, so.count)])

        sems = sch.all_sems()
        with contextlib.ExitStack() as es2:
            for s in sems:
                s.h = es2.enter_context(nc.semaphore(s.name))
            with nc.Block() as block:
                @block.sync
                def _(e):
                    sch.run("sync", e)

                @block.vector
                def _(e):
                    sch.run("vector", e)

                @block.scalar
                def _(e):
                    sch.run("scalar", e)

                @block.tensor
                def _(e):
                    sch.run("tensor", e)

                @block.gpsimd
                def _(e):
                    sch.run("gpsimd", e)
    return nc


_NC_CACHE = [None]


def kernel(L0_attn_X, L0_attn_W_in, L0_attn_b_in, L0_attn_W_out, L0_attn_b_out,
           L0_attn_gamma, L0_attn_beta, L0_ffn_w_gating, L0_ffn_w_linear):
    X = np.ascontiguousarray(np.asarray(L0_attn_X, dtype=np.float32))
    W_in = np.asarray(L0_attn_W_in, dtype=np.float32)
    b_in = np.asarray(L0_attn_b_in, dtype=np.float32)
    W_out = np.asarray(L0_attn_W_out, dtype=np.float32)
    b_out = np.asarray(L0_attn_b_out, dtype=np.float32)
    gamma = np.asarray(L0_attn_gamma, dtype=np.float32)
    beta = np.asarray(L0_attn_beta, dtype=np.float32)
    w_g = np.asarray(L0_ffn_w_gating, dtype=np.float32)
    w_l = np.asarray(L0_ffn_w_linear, dtype=np.float32)

    cols = []
    for kc in range(16):
        cols += list(range(4096 + kc * 128, 4096 + (kc + 1) * 128))
        cols += list(range(2048 + kc * 128, 2048 + (kc + 1) * 128))
    cols += list(range(0, 2048))
    w_in_st = np.ascontiguousarray(
        W_in[:, cols].reshape(16, 128, 24, 256).transpose(2, 1, 0, 3).reshape(24, 128, 4096))
    a0 = w_g[0].reshape(16, 128, 16, 512)
    a1 = w_g[1].reshape(16, 128, 16, 512)
    wg_st = np.ascontiguousarray(
        np.concatenate([a0, a1], axis=3).transpose(2, 1, 0, 3).reshape(16, 128, 16384))
    wl_st = np.ascontiguousarray(
        w_l.reshape(8, 8, 128, 2, 1024).transpose(3, 0, 2, 1, 4).reshape(2, 8, 128, 8192))
    wout_st = np.ascontiguousarray(
        W_out.reshape(16, 128, 16, 128).transpose(2, 1, 0, 3).reshape(16, 128, 2048))
    b_in_fm = np.ascontiguousarray(b_in.reshape(48, 128).T)
    b_out_fm = np.ascontiguousarray(b_out.reshape(16, 128).T)
    gam_rep = np.ascontiguousarray(np.broadcast_to(gamma, (128, D)))
    bet_rep = np.ascontiguousarray(np.broadcast_to(beta, (128, D)))
    ident = np.eye(128, dtype=np.float32)

    shared = dict(w_in_st=w_in_st, wg_st=wg_st, wl_st=wl_st, wout_st=wout_st,
                  b_in_fm=b_in_fm, b_out_fm=b_out_fm, gam_rep=gam_rep,
                  bet_rep=bet_rep, ident=ident)
    in_maps = []
    for c in range(NCORE):
        b = c // 4
        r0 = (c % 4) * TOK
        cos_fm, sin_fm = _rotary_tables(c)
        m = dict(shared)
        m["x"] = np.ascontiguousarray(X[b, r0:r0 + TOK, :])
        m["tabs_st"] = np.ascontiguousarray(np.concatenate([cos_fm, sin_fm], axis=2))
        in_maps.append(m)

    if _NC_CACHE[0] is None:
        _NC_CACHE[0] = build_nc()
    nc = _NC_CACHE[0]
    res = run_bass_kernel_spmd(nc, in_maps, core_ids=list(range(NCORE)))
    out = np.empty((B, S, D), dtype=np.float32)
    for c in range(NCORE):
        b = c // 4
        r0 = (c % 4) * TOK
        out[b, r0:r0 + TOK, :] = res.results[c]["y"]
    return out


# version 10
# speedup vs baseline: 2.1991x; 1.0238x over previous
import contextlib
import numpy as np
import concourse.bass as bass
import concourse.mybir as mybir
from concourse.bass_utils import run_bass_kernel_spmd

F32 = mybir.dt.float32
BF16 = mybir.dt.bfloat16
FP8 = mybir.dt.float8e4
U16 = mybir.dt.uint16
AF = mybir.ActivationFunctionType
OP = mybir.AluOpType
AX = mybir.AxisListType.X

B, S, D = 2, 4096, 2048
NCORE = 8
TOK = 1024
NTT = 8
EPS = 1e-5
INV_D = float(1.0 / 2048.0)
GC1 = 0.0446777344
GC2 = 0.796875
SWAP32 = [i ^ 1 for i in range(32)]

ENGS = ("sync", "vector", "scalar", "tensor", "gpsimd")


class Sem:
    __slots__ = ("name", "count", "h")

    def __init__(self, name):
        self.name, self.count, self.h = name, 0, None


class Buf:
    __slots__ = ("name", "w", "readers")

    def __init__(self, name):
        self.name, self.w, self.readers = name, None, []


class Sched:
    def __init__(self):
        self.q = {e: [] for e in ENGS}
        self.esem = {e: Sem("E_" + e) for e in ENGS}
        self.dsem = {}
        self.xsem = []

    def all_sems(self):
        return list(self.esem.values()) + list(self.dsem.values()) + self.xsem

    def dma_sem(self, name):
        if name not in self.dsem:
            self.dsem[name] = Sem("D_" + name)
        return self.dsem[name]

    def new_sem(self, name):
        s = Sem(name)
        self.xsem.append(s)
        return s

    def emit(self, eng, build, reads=(), writes=(), dma=None, sem=None, waits=()):
        wmap = {}

        def addw(tok):
            if tok is None:
                return
            s, v = tok
            if wmap.get(s, -1) < v:
                wmap[s] = v

        for b in reads:
            addw(b.w)
        for b in writes:
            addw(b.w)
            for t in b.readers:
                addw(t)
        for t in waits:
            addw(t)
        if dma is not None:
            if dma == "auto":
                assert writes, "auto dma sem needs a write buffer"
                dma = "b_" + writes[0].name
            s = self.dma_sem(dma)
            s.count += 16
            tok = (s, s.count)
            amt = 16
        elif sem is not None:
            s = sem
            s.count += 1
            tok = (s, s.count)
            amt = 1
        else:
            s = self.esem[eng]
            s.count += 1
            tok = (s, s.count)
            amt = 1
        self.q[eng].append((build, tuple(wmap.items()), (s, amt)))
        for b in reads:
            b.readers.append(tok)
        for b in writes:
            b.w = tok
            b.readers = []
        return tok

    def wait_only(self, eng, toks):
        self.q[eng].append((None, tuple((s, v) for s, v in toks), None))

    def run(self, eng_name, engobj):
        waited = {}
        for build, waits, inc in self.q[eng_name]:
            for s, v in waits:
                if waited.get(s, 0) < v:
                    engobj.wait_ge(s.h, v)
                    waited[s] = v
            if build is None:
                continue
            ins = build(engobj)
            s, amt = inc
            ins.then_inc(s.h, amt)
            if waited.get(s, 0) < s.count:
                pass


def _rotary_tables(core):
    rot = D // 2
    inv_freq = (1.0 / (np.float32(10000.0) **
                       (np.arange(0, rot, 2, dtype=np.float32) / np.float32(rot)))).astype(np.float32)
    p0 = (core % 4) * TOK
    pos = np.arange(p0, p0 + TOK, dtype=np.float32)
    fr = (pos[:, None] * inv_freq[None, :]).astype(np.float32)
    fr = np.repeat(fr, 2, axis=1)
    cos = np.cos(fr).astype(np.float32)
    sin = np.sin(fr).astype(np.float32)
    sinS = sin.copy()
    sinS[:, 0::2] = -sin[:, 0::2]
    cos_fm = np.ascontiguousarray(cos.T.reshape(8, 128, TOK))
    sin_fm = np.ascontiguousarray(sinS.T.reshape(8, 128, TOK))
    return cos_fm, sin_fm


def build_nc():
    nc = bass.Bass("TRN2", target_bir_lowering=False, debug=False)

    def din(name, shape):
        return nc.dram_tensor(name, list(shape), F32, kind="ExternalInput").ap()

    x = din("x", (TOK, D))
    y = nc.dram_tensor("y", [TOK, D], F32, kind="ExternalOutput").ap()
    w_in_st = din("w_in_st", (24, 128, 4096))
    wg_st = din("wg_st", (16, 128, 16384))
    wl_st = din("wl_st", (2, 8, 128, 8192))
    wout_st = din("wout_st", (16, 128, 2048))
    b_in_fm = din("b_in_fm", (128, 48))
    b_out_fm = din("b_out_fm", (128, 16))
    gam_rep = din("gam_rep", (128, D))
    bet_rep = din("bet_rep", (128, D))
    tabs_st = din("tabs_st", (8, 128, 2048))
    ident = din("ident", (128, 128))
    ccin = nc.dram_tensor("ccin", [128, 16], F32)
    ccout = nc.dram_tensor("ccout", [128, 16], F32)
    proj_tm = nc.dram_tensor("proj_tm", [TOK, D], BF16)

    es = contextlib.ExitStack()
    sch = Sched()
    with es:
        T = lambda n, shp, dt: es.enter_context(nc.sbuf_tensor(n, list(shp), dt))
        xslot = T("xslot", (128, D), F32)
        xnt = T("xnt", (128, D), F32)
        sc1 = T("sc1", (128, D), F32)
        ring = T("ring", (128, D), F32)
        tab = T("tab", (128, D), F32)
        stats = T("stats", (128, 64), F32)
        winslab = T("winslab", (128, 4096), BF16)
        XNT = T("XNT", (128, 16384), BF16)
        QTB = T("QTB", (128, 16384), BF16)
        gpool = T("gpool", (128, 3072), BF16)
        vpb = T("vpb", (128, 1024), BF16)
        stg = T("stg", (128, 1024), BF16)
        identf = T("identf", (128, 128), F32)
        onesb = T("onesb", (128, 1), F32)
        onesbf = T("onesbf", (128, 1), BF16)
        ones1 = T("ones1", (1, 128), F32)
        binf = T("binf", (128, 48), F32)
        boutf = T("boutf", (128, 16), F32)
        QP1 = T("QP1", (128, 16384), FP8)
        QP2 = T("QP2", (128, 16384), FP8)
        xnqT = T("xnqT", (128, 16384), FP8)
        actT = T("actT", (128, 32768), FP8)
        woutsl = T("woutsl", (128, 2048), BF16)
        PS = [es.enter_context(nc.psum_tensor(f"ps{i}", [128, 1024], F32)) for i in range(4)]

        bufs = {}

        def Bf(n):
            bufs[n] = Buf(n)
            return bufs[n]

        b_x, b_xn, b_sc1, b_ring, b_tab = Bf("x"), Bf("xn"), Bf("sc1"), Bf("ring"), Bf("tab")
        b_small, b_stats = Bf("small"), Bf("stats")
        b_winslab, b_QTB, b_gpool = Bf("winslab"), Bf("QTB"), Bf("gpool")
        b_XNTh = [Bf("XNTh0"), Bf("XNTh1")]
        b_vpb, b_stg = Bf("vpb"), Bf("stg")
        b_slY = Bf("slY")
        b_rh = [Bf("rh0"), Bf("rh1")]
        b_gel = [Bf("gel0"), Bf("gel1")]
        b_identf, b_onesb, b_ones1, b_binf, b_boutf = Bf("identf"), Bf("onesb"), Bf("ones1"), Bf("binf"), Bf("boutf")
        b_onesbf = Bf("onesbf")
        b_QP1, b_QP2, b_xnqT, b_actT = Bf("QP1"), Bf("QP2"), Bf("xnqT"), Bf("actT")
        b_wls = [Bf("wls0"), Bf("wls1")]
        b_wos = Bf("wos")
        b_ps = [Bf(f"ps{i}") for i in range(4)]
        b_ccin, b_ccout, b_projtm = Bf("ccin"), Bf("ccout"), Bf("projtm")

        E = sch.emit

        def toks(buf):
            t = list(buf.readers)
            if buf.w:
                t.append(buf.w)
            return t

        E("sync", lambda e: e.dma_start(out=identf[:, :], in_=ident[:, :]), writes=[b_identf], dma="auto")
        E("sync", lambda e: e.dma_start(out=binf[:, :], in_=b_in_fm[:, :]), writes=[b_binf], dma="auto")
        E("sync", lambda e: e.dma_start(out=boutf[:, :], in_=b_out_fm[:, :]), writes=[b_boutf], dma="auto")
        E("gpsimd", lambda e: e.dma_start(out=gpool[:, 0:2048], in_=gam_rep[:, :]), writes=[b_gpool], dma="auto")
        E("gpsimd", lambda e: e.dma_start(out=winslab[:, 0:2048], in_=bet_rep[:, :]), writes=[b_winslab], dma="auto")
        E("vector", lambda e: e.memset(onesb[:, :], 1.0), writes=[b_onesb])
        E("vector", lambda e: e.memset(onesbf[:, :], 1.0), writes=[b_onesbf])
        E("vector", lambda e: e.memset(ones1[:, :], 1.0), writes=[b_ones1])

        mS = lambda tt: stats[:, tt:tt + 1]
        rS = lambda tt: stats[:, 8 + tt:9 + tt]
        t0 = stats[:, 16:17]; t1 = stats[:, 17:18]; t2 = stats[:, 18:19]
        t3 = stats[:, 19:20]; t4 = stats[:, 20:21]; t5 = stats[:, 21:22]
        pA_ap = stats[:, 32:48]
        A_ap = stats[:, 48:64]
        b_pA, b_A = Bf("pA"), Bf("A")

        def rsqrt_into(src_ap, dst_ap, rb, wb):
            E("scalar", lambda e: e.activation(t3, src_ap, AF.Sqrt), reads=[rb], writes=[b_stats])
            E("vector", lambda e: e.reciprocal(t4, t3), reads=[b_stats], writes=[b_stats])
            E("vector", lambda e: e.tensor_mul(t5, t4, t4), reads=[b_stats], writes=[b_stats])
            E("vector", lambda e: e.tensor_mul(t5, src_ap, t5), reads=[rb, b_stats], writes=[b_stats])
            E("vector", lambda e: e.tensor_scalar(t5, t5, -0.5, 1.5, OP.mult, OP.add),
              reads=[b_stats], writes=[b_stats])
            E("vector", lambda e: e.tensor_mul(dst_ap, t4, t5), reads=[b_stats], writes=[wb])

        sAll = stats[:, 16:24]; qAll = stats[:, 24:32]
        mAll = stats[:, 0:8]; rAll = stats[:, 8:16]
        uA = stats[:, 48:56]; m2A = stats[:, 56:64]
        w0a = stats[:, 32:40]; w1a = stats[:, 40:48]
        for tt in range(NTT):
            E("sync", lambda e, tt=tt: e.dma_start(out=xslot[:, :], in_=x[tt * 128:(tt + 1) * 128, :]),
              writes=[b_x], dma="auto")
            E("vector", lambda e, tt=tt: e.reduce_sum(stats[:, 16 + tt:17 + tt], xslot[:, :], axis=AX),
              reads=[b_x], writes=[b_stats])
            E("vector", lambda e, tt=tt: e.scalar_tensor_tensor(ring[:, :], xslot[:, :], 1.0, xslot[:, :],
                                                                OP.mult, OP.mult,
                                                                accum_out=stats[:, 24 + tt:25 + tt]),
              reads=[b_x], writes=[b_ring, b_stats])
        E("scalar", lambda e: e.activation(mAll, sAll, AF.Identity, scale=INV_D),
          reads=[b_stats], writes=[b_stats])
        E("vector", lambda e: e.tensor_scalar(uA, qAll, INV_D, float(EPS), OP.mult, OP.add),
          reads=[b_stats], writes=[b_stats])
        E("vector", lambda e: e.tensor_mul(w0a, mAll, mAll), reads=[b_stats], writes=[b_stats])
        E("vector", lambda e: e.tensor_sub(uA, uA, w0a), reads=[b_stats], writes=[b_stats])
        E("scalar", lambda e: e.activation(w0a, uA, AF.Sqrt), reads=[b_stats], writes=[b_stats])
        E("vector", lambda e: e.reciprocal(w1a, w0a), reads=[b_stats], writes=[b_stats])
        E("vector", lambda e: e.tensor_mul(w0a, w1a, w1a), reads=[b_stats], writes=[b_stats])
        E("vector", lambda e: e.tensor_mul(w0a, uA, w0a), reads=[b_stats], writes=[b_stats])
        E("vector", lambda e: e.tensor_scalar(w0a, w0a, -0.5, 1.5, OP.mult, OP.add),
          reads=[b_stats], writes=[b_stats])
        E("vector", lambda e: e.tensor_mul(rAll, w1a, w0a), reads=[b_stats], writes=[b_stats])
        E("vector", lambda e: e.tensor_scalar(w0a, sAll, INV_D, None, OP.mult),
          reads=[b_stats], writes=[b_stats])
        E("vector", lambda e: e.tensor_sub(w0a, w0a, mAll), reads=[b_stats], writes=[b_stats])
        E("vector", lambda e: e.tensor_mul(m2A, w0a, rAll), reads=[b_stats], writes=[b_stats])
        E("vector", lambda e: e.tensor_scalar(w0a, uA, 1.0, -float(EPS), OP.mult, OP.add),
          reads=[b_stats], writes=[b_stats])
        E("vector", lambda e: e.tensor_mul(w1a, rAll, rAll), reads=[b_stats], writes=[b_stats])
        E("vector", lambda e: e.tensor_mul(w0a, w0a, w1a), reads=[b_stats], writes=[b_stats])
        E("vector", lambda e: e.tensor_mul(w1a, m2A, m2A), reads=[b_stats], writes=[b_stats])
        E("vector", lambda e: e.tensor_sub(w0a, w0a, w1a), reads=[b_stats], writes=[b_stats])
        E("vector", lambda e: e.tensor_scalar(w0a, w0a, 1.0, float(EPS), OP.mult, OP.add),
          reads=[b_stats], writes=[b_stats])
        E("scalar", lambda e: e.activation(w1a, w0a, AF.Sqrt), reads=[b_stats], writes=[b_stats])
        E("vector", lambda e: e.reciprocal(uA, w1a), reads=[b_stats], writes=[b_stats])
        E("vector", lambda e: e.tensor_mul(w1a, uA, uA), reads=[b_stats], writes=[b_stats])
        E("vector", lambda e: e.tensor_mul(w1a, w0a, w1a), reads=[b_stats], writes=[b_stats])
        E("vector", lambda e: e.tensor_scalar(w1a, w1a, -0.5, 1.5, OP.mult, OP.add),
          reads=[b_stats], writes=[b_stats])
        E("vector", lambda e: e.tensor_mul(uA, uA, w1a), reads=[b_stats], writes=[b_stats])
        for tt in range(NTT):
            E("sync", lambda e, tt=tt: e.dma_start(out=xslot[:, :], in_=x[tt * 128:(tt + 1) * 128, :]),
              writes=[b_x], dma="auto")
            E("vector", lambda e, tt=tt: e.tensor_scalar(xnt[:, :], xslot[:, :], mS(tt), rS(tt),
                                                         OP.subtract, OP.mult),
              reads=[b_x, b_stats], writes=[b_xn])
            E("scalar", lambda e, tt=tt: e.copy(QP1[:, tt * 2048:(tt + 1) * 2048], xnt[:, :]),
              reads=[b_xn], writes=[b_QP1])
            E("vector", lambda e, tt=tt: e.scalar_tensor_tensor(ring[:, :], xnt[:, :],
                                                                stats[:, 56 + tt:57 + tt], gpool[:, 0:2048],
                                                                OP.subtract, OP.mult),
              reads=[b_xn, b_stats, b_gpool], writes=[b_ring])
            E("vector", lambda e, tt=tt: e.tensor_scalar(ring[:, :], ring[:, :],
                                                         stats[:, 48 + tt:49 + tt], None, OP.mult),
              reads=[b_ring, b_stats], writes=[b_ring])
            E("vector", lambda e: e.tensor_tensor(sc1[:, :], ring[:, :], winslab[:, 0:2048], OP.add),
              reads=[b_ring, b_winslab], writes=[b_sc1])
            for half in range(2):
                ps, bp = PS[2 + half], b_ps[2 + half]
                for k in range(8):
                    kc = half * 8 + k
                    E("tensor", lambda e, kc=kc, k=k, ps=ps: e.transpose(
                        ps[:, k * 128:(k + 1) * 128], sc1[:, kc * 128:(kc + 1) * 128], identf[:, :]),
                      reads=[b_sc1, b_identf], writes=[bp])
                for k in range(8):
                    kc = half * 8 + k
                    E("scalar", lambda e, kc=kc, k=k, tt=tt, ps=ps: e.copy(
                        XNT[:, kc * 1024 + tt * 128: kc * 1024 + (tt + 1) * 128],
                        ps[:, k * 128:(k + 1) * 128]),
                      reads=[bp], writes=[b_XNTh[tt // 4]])

        gam_done = [None]

        def load_winslab(g):
            if g % 2 == 0:
                E("gpsimd", lambda e, g=g: e.dma_start(out=winslab[:, :], in_=w_in_st[g, :, :]),
                  writes=[b_winslab], dma="auto")
            else:
                ew = tuple(gam_done[0]) if g == 1 else ()
                E("gpsimd", lambda e, g=g: e.dma_start(out=gpool[:, 0:3072], in_=w_in_st[g, :, 0:3072]),
                  writes=[b_slY], dma="auto", waits=ew)
                E("gpsimd", lambda e, g=g: e.dma_start(out=stg[:, :], in_=w_in_st[g, :, 3072:4096]),
                  writes=[b_slY], dma="auto", waits=ew)

        def slab_ap(g, kc, oc):
            base = kc * 256 + oc * 128
            if g % 2 == 0:
                return winslab[:, base: base + 128], b_winslab
            if base < 3072:
                return gpool[:, base: base + 128], b_slY
            return stg[:, base - 3072: base - 3072 + 128], b_slY

        def qkv_mms(g, oc01, ps, bp):
            for nh in range(2):
                for kc in range(16):
                    fin = (kc == 15 and nh == 1)
                    sap, sbuf_b = slab_ap(g, kc, oc01)
                    E("tensor", lambda e, sap=sap, kc=kc, nh=nh, ps=ps: e.matmul(
                        ps[:, nh * 512:(nh + 1) * 512],
                        sap,
                        XNT[:, kc * 1024 + nh * 512: kc * 1024 + (nh + 1) * 512],
                        start=(kc == 0), stop=(kc == 15)),
                      reads=[sbuf_b, b_XNTh[nh]],
                      writes=[bp] if (kc == 0 and nh == 0) or fin else [])

        nv2 = xslot[0:1, 0:1024]
        nk2 = xslot[0:1, 1024:2048]
        rv_ = xslot[0:1, 0:1024]
        rk_ = xslot[0:1, 1024:2048]
        s3_ = tab[0:1, 0:1024]
        s4_ = ring[0:1, 0:1024]
        w1_ = ring[0:1, 1024:2048]
        E("vector", lambda e: e.memset(xslot[0:1, 0:2048], 0.0), writes=[b_x])

        r0ap = xnt[:, 0:1024]
        r1ap = xnt[:, 1024:2048]
        m1ap = ring[:, 0:1024]
        m2ap = ring[:, 1024:2048]
        k1ap = sc1[:, 0:1024]

        def norm_accum(dst_row, src_ap, b_src):
            for nh in range(2):
                E("tensor", lambda e, nh=nh, src_ap=src_ap: e.matmul(
                    PS[2][0:1, nh * 512:(nh + 1) * 512], onesbf[:, :],
                    src_ap[:, nh * 512:(nh + 1) * 512], start=True, stop=True),
                  reads=[b_src, b_onesbf], writes=[b_ps[2]])
            E("vector", lambda e, dst_row=dst_row: e.tensor_tensor(dst_row, dst_row, PS[2][0:1, :], OP.add),
              reads=[b_ps[2], b_x], writes=[b_x])

        pending_norms = []
        gam_done[0] = toks(b_gpool) + toks(b_stg)

        def flush_norms():
            if pending_norms:
                pending_norms.pop()
                norm_accum(nv2, woutsl[:, 0:1024], b_wos)
                norm_accum(nk2, woutsl[:, 1024:2048], b_wos)

        def emit_p2():
            def rsqrt_row(src, dst):
                E("scalar", lambda e: e.activation(s3_, src, AF.Sqrt), reads=[b_x], writes=[b_tab])
                E("vector", lambda e: e.reciprocal(s4_, s3_), reads=[b_tab], writes=[b_ring])
                E("vector", lambda e: e.tensor_mul(s3_, s4_, s4_), reads=[b_ring], writes=[b_tab])
                E("vector", lambda e: e.tensor_mul(s3_, src, s3_), reads=[b_x, b_tab], writes=[b_tab])
                E("vector", lambda e: e.tensor_scalar(s3_, s3_, -0.5, 1.5, OP.mult, OP.add),
                  reads=[b_tab], writes=[b_tab])
                E("vector", lambda e: e.tensor_mul(dst, s4_, s3_), reads=[b_ring, b_tab], writes=[b_x])

            rsqrt_row(nv2, rv_)
            rsqrt_row(nk2, rk_)
            E("vector", lambda e: e.tensor_mul(w1_, rk_, rv_), reads=[b_x], writes=[b_ring])
            for nh in range(2):
                E("tensor", lambda e, nh=nh: e.matmul(PS[2][:, nh * 512:(nh + 1) * 512], ones1[:, :],
                                                      w1_[:, nh * 512:(nh + 1) * 512], start=True, stop=True),
                  reads=[b_ring, b_ones1], writes=[b_ps[2]])
            wrep = tab[:, 0:1024]
            E("vector", lambda e: e.tensor_copy(wrep, PS[2][:, :]), reads=[b_ps[2]], writes=[b_tab])
            for kc in range(16):
                E("vector", lambda e, kc=kc: e.scalar_tensor_tensor(
                    ring[:, 0:1024], QP2[:, kc * 1024:(kc + 1) * 1024], 1.0, wrep,
                    OP.mult, OP.mult, accum_out=pA_ap[:, kc:kc + 1]),
                  reads=[b_QP2, b_tab], writes=[b_ring, b_pA])
            E("gpsimd", lambda e: e.dma_start(out=ccin.ap()[:, :], in_=pA_ap),
              reads=[b_pA], writes=[b_ccin], dma="auto")
            cc_sem = sch.new_sem("ccdone")
            E("gpsimd", lambda e: e.collective_compute(
                "AllReduce", OP.add, replica_groups=[[0, 1, 2, 3], [4, 5, 6, 7]],
                ins=[ccin.ap().opt()], outs=[ccout.ap().opt()]),
              reads=[b_ccin], writes=[b_ccout], sem=cc_sem)

        for g in range(24):
            load_winslab(g)
            qkv_mms(g, 0, PS[0], b_ps[0])
            qkv_mms(g, 1, PS[1], b_ps[1])
            flush_norms()
            if g == 16:
                emit_p2()
            if g < 16:
                kc = g
                if kc < 8:
                    E("sync", lambda e, kc=kc: e.dma_start(out=tab[:, :], in_=tabs_st[kc, :, :]),
                      writes=[b_tab], dma="auto")
                    E("scalar", lambda e, kc=kc: e.activation(r0ap, PS[0][:, :], AF.Identity,
                                                              bias=binf[:, 32 + kc:33 + kc]),
                      reads=[b_ps[0], b_binf], writes=[b_xn])
                    E("vector", lambda e: e.stream_shuffle(r1ap, r0ap, SWAP32),
                      reads=[b_xn], writes=[b_xn])
                    E("vector", lambda e: e.tensor_tensor(m1ap, r0ap, tab[:, 0:1024], OP.mult),
                      reads=[b_xn, b_tab], writes=[b_ring])
                    E("vector", lambda e: e.tensor_tensor(m2ap, r1ap, tab[:, 1024:2048], OP.mult),
                      reads=[b_xn, b_tab], writes=[b_ring])
                    E("vector", lambda e: e.tensor_tensor(vpb[:, :], m1ap, m2ap, OP.add),
                      reads=[b_ring], writes=[b_vpb])
                    E("scalar", lambda e, kc=kc: e.activation(r0ap, PS[1][:, :], AF.Identity,
                                                              bias=binf[:, 16 + kc:17 + kc]),
                      reads=[b_ps[1], b_binf], writes=[b_xn])
                    E("vector", lambda e: e.stream_shuffle(r1ap, r0ap, SWAP32),
                      reads=[b_xn], writes=[b_xn])
                    E("vector", lambda e: e.tensor_tensor(k1ap, r0ap, tab[:, 0:1024], OP.mult),
                      reads=[b_xn, b_tab], writes=[b_sc1])
                    E("vector", lambda e: e.tensor_tensor(m2ap, r1ap, tab[:, 1024:2048], OP.mult),
                      reads=[b_xn, b_tab], writes=[b_ring])
                    E("vector", lambda e: e.tensor_tensor(m1ap, k1ap, m2ap, OP.add),
                      reads=[b_sc1, b_ring], writes=[b_ring])
                else:
                    E("scalar", lambda e, kc=kc: e.activation(vpb[:, :], PS[0][:, :], AF.Identity,
                                                              bias=binf[:, 32 + kc:33 + kc]),
                      reads=[b_ps[0], b_binf], writes=[b_vpb])
                    E("scalar", lambda e, kc=kc: e.activation(m1ap, PS[1][:, :], AF.Identity,
                                                              bias=binf[:, 16 + kc:17 + kc]),
                      reads=[b_ps[1], b_binf], writes=[b_ring])
                E("vector", lambda e, kc=kc: e.tensor_tensor(QP2[:, kc * 1024:(kc + 1) * 1024],
                                                             m1ap, vpb[:, :], OP.mult),
                  reads=[b_ring, b_vpb], writes=[b_QP2])
                E("vector", lambda e: e.tensor_tensor(woutsl[:, 0:1024], vpb[:, :], vpb[:, :], OP.mult),
                  reads=[b_vpb], writes=[b_wos])
                E("vector", lambda e: e.tensor_tensor(woutsl[:, 1024:2048], m1ap, m1ap, OP.mult),
                  reads=[b_ring], writes=[b_wos])
                pending_norms.append(True)
            else:
                q = g - 16
                for j in range(2):
                    oc = 2 * q + j
                    ps, bp = PS[j], b_ps[j]
                    E("scalar", lambda e, oc=oc, ps=ps: e.activation(
                        QTB[:, oc * 1024:(oc + 1) * 1024], ps[:, :], AF.Identity,
                        bias=binf[:, oc:oc + 1]),
                      reads=[bp, b_binf], writes=[b_QTB])

        for tt in range(NTT):
            E("vector", lambda e, tt=tt: e.tensor_copy(sc1[:, :], QP1[:, tt * 2048:(tt + 1) * 2048]),
              reads=[b_QP1], writes=[b_sc1])
            for half in range(2):
                ps, bp = PS[2 + half], b_ps[2 + half]
                for k in range(8):
                    kc = half * 8 + k
                    E("tensor", lambda e, kc=kc, k=k, ps=ps: e.transpose(
                        ps[:, k * 128:(k + 1) * 128], sc1[:, kc * 128:(kc + 1) * 128], identf[:, :]),
                      reads=[b_sc1, b_identf], writes=[bp])
                for k in range(8):
                    kc = half * 8 + k
                    E("scalar", lambda e, kc=kc, k=k, tt=tt, ps=ps: e.copy(
                        xnqT[:, kc * 1024 + tt * 128: kc * 1024 + (tt + 1) * 128],
                        ps[:, k * 128:(k + 1) * 128]),
                      reads=[bp], writes=[b_xnqT])


        E("sync", lambda e: e.dma_start(out=A_ap, in_=ccout.ap()[:, :]),
          reads=[b_ccout], writes=[b_A], dma="auto")
        E("vector", lambda e: e.tensor_scalar(A_ap, A_ap, 0.5, 0.5, OP.mult, OP.add),
          reads=[b_A], writes=[b_A])

        gBs = (gpool[:, 0:512], gpool[:, 2048:2560])
        fBs = (gpool[:, 512:1024], gpool[:, 2560:3072])
        tAs = (gpool[:, 1024:1536], vpb[:, 0:512])
        tBs = (gpool[:, 1536:2048], vpb[:, 512:1024])
        fbB = gpool[:, 2048:3072]

        pend_tp = []
        blkpar = [0]
        slabY_need = [2]
        slabY_toks = tuple(toks(b_slY))
        ring_old = toks(b_ring)
        b_rh[0].w = ring_old[-1] if ring_old else None
        b_rh[1].w = ring_old[-1] if ring_old else None

        def flush_tp():
            while pend_tp:
                jb0, tl0, par0 = pend_tp.pop(0)
                pst, bpt = PS[2 + par0], b_ps[2 + par0]
                for i in range(4):
                    E("tensor", lambda e, i=i, par0=par0, pst=pst: e.transpose(
                        pst[:, i * 128:(i + 1) * 128],
                        ring[:, par0 * 512 + i * 128: par0 * 512 + (i + 1) * 128], identf[:, :]),
                      reads=[b_rh[par0], b_identf], writes=[bpt])
                for i in range(4):
                    E("vector", lambda e, i=i, jb0=jb0, tl0=tl0, pst=pst: e.tensor_copy(
                        actT[:, (4 * jb0 + i) * 512 + tl0 * 128: (4 * jb0 + i) * 512 + (tl0 + 1) * 128],
                        pst[:, i * 128:(i + 1) * 128]),
                      reads=[bpt], writes=[b_actT])

        for h in range(2):
            wl_toks = toks(b_wls[0]) + toks(b_wls[1])
            for jb in range(16):
                slab = (QP1, QP2)[jb % 2]
                b_slab = (b_QP1, b_QP2)[jb % 2]
                ew = tuple(wl_toks) if (jb < 2 and slab is QP1) else ()
                E("gpsimd", lambda e, jb=jb, slab=slab: e.dma_start(
                    out=slab[:, :], in_=wg_st[jb, :, :]),
                  writes=[b_slab], dma="auto", waits=ew)
                for tl in range(4):
                    tt = 4 * h + tl
                    ps, bp = PS[tl % 2], b_ps[tl % 2]
                    for kc in range(16):
                        for nh in range(2):
                            fin = (kc == 15 and nh == 1)
                            E("tensor", lambda e, kc=kc, nh=nh, tt=tt, ps=ps, slab=slab: e.matmul(
                                ps[:, nh * 512:(nh + 1) * 512],
                                xnqT[:, kc * 1024 + tt * 128: kc * 1024 + (tt + 1) * 128],
                                slab[:, kc * 1024 + nh * 512: kc * 1024 + (nh + 1) * 512],
                                start=(kc == 0), stop=(kc == 15)),
                              reads=[b_xnqT, b_slab],
                              writes=[bp] if (kc == 0 and nh == 0) or fin else [])
                    par = blkpar[0]
                    blkpar[0] ^= 1
                    gB, fB, tA, tB = gBs[par], fBs[par], tAs[par], tBs[par]
                    bg = b_gel[par]
                    E("scalar", lambda e, ps=ps, gB=gB: e.copy(gB, ps[:, 0:512]), reads=[bp], writes=[bg],
                      waits=(slabY_toks if slabY_need[0] > 0 else ()))
                    E("scalar", lambda e, ps=ps, fB=fB: e.copy(fB, ps[:, 512:1024]), reads=[bp], writes=[bg])
                    slabY_need[0] -= 1
                    E("vector", lambda e, tA=tA, gB=gB: e.tensor_mul(tA, gB, gB), reads=[bg], writes=[bg])
                    E("vector", lambda e, tA=tA, tB=tB, gB=gB: e.tensor_mul(tB, tA, gB), reads=[bg], writes=[bg])
                    E("vector", lambda e, tA=tA, tB=tB: e.tensor_scalar_mul(tA, tB, GC1),
                      reads=[bg], writes=[bg])
                    E("vector", lambda e, tA=tA, tB=tB, gB=gB: e.tensor_tensor(tB, gB, tA, OP.add),
                      reads=[bg], writes=[bg])
                    E("scalar", lambda e, tA=tA, tB=tB: e.activation(tA, tB, AF.Copy, scale=GC2),
                      reads=[bg], writes=[bg])
                    E("scalar", lambda e, tA=tA, tB=tB: e.activation(tB, tA, AF.Tanh),
                      reads=[bg], writes=[bg])
                    E("vector", lambda e, tA=tA, tB=tB: e.tensor_scalar_add(tA, tB, 1.0),
                      reads=[bg], writes=[bg])
                    E("scalar", lambda e, tA=tA, tB=tB: e.activation(tB, tA, AF.Copy, scale=0.5),
                      reads=[bg], writes=[bg])
                    E("vector", lambda e, tA=tA, tB=tB, gB=gB: e.tensor_mul(tA, gB, tB), reads=[bg], writes=[bg])
                    E("vector", lambda e, par=par, tA=tA, fB=fB: e.tensor_mul(
                        winslab[:, par * 512:(par + 1) * 512], tA, fB),
                      reads=[bg], writes=[b_winslab])
                    E("vector", lambda e, par=par: e.tensor_copy(ring[:, par * 512:(par + 1) * 512],
                                                                 winslab[:, par * 512:(par + 1) * 512]),
                      reads=[b_winslab], writes=[b_rh[par]])
                    flush_tp()
                    pend_tp.append((jb, tl, par))
            flush_tp()
            if h == 1:
                gel1_toks = tuple(toks(b_gel[1]))
                for kc in range(16):
                    E("scalar", lambda e, kc=kc: e.copy(vpb[:, :],
                                                        QTB[:, kc * 1024: kc * 1024 + 1024]),
                      reads=[b_QTB], writes=[b_vpb], waits=(gel1_toks if kc == 0 else ()))
                    E("scalar", lambda e, kc=kc: e.activation(QTB[:, kc * 1024:(kc + 1) * 1024],
                                                              vpb[:, :], AF.Copy,
                                                              scale=A_ap[:, kc:kc + 1]),
                      reads=[b_vpb, b_A], writes=[b_QTB])
            qp1_toks = toks(b_QP1)
            first_wl = [True, True]
            for dh in range(2):
                for grp in range(8):
                    sl = grp % 2
                    ew = tuple(qp1_toks) if first_wl[sl] else ()
                    first_wl[sl] = False
                    E("gpsimd", lambda e, dh=dh, grp=grp, sl=sl: e.dma_start(
                        out=QP1[:, sl * 8192:(sl + 1) * 8192], in_=wl_st[dh, grp, :, :]),
                      writes=[b_wls[sl]], dma="auto", waits=ew)
                    for k8 in range(8):
                        kc = grp * 8 + k8
                        for tl in range(4):
                            last = (kc == 63)
                            E("tensor", lambda e, kc=kc, k8=k8, tl=tl, sl=sl: e.matmul(
                                PS[tl][:, 0:512],
                                actT[:, kc * 512 + tl * 128: kc * 512 + (tl + 1) * 128],
                                QP1[:, sl * 8192 + k8 * 1024: sl * 8192 + k8 * 1024 + 512],
                                start=(kc == 0), stop=last),
                              reads=[b_actT, b_wls[sl]],
                              writes=[b_ps[tl]] if kc == 0 else [])
                            E("tensor", lambda e, kc=kc, k8=k8, tl=tl, sl=sl: e.matmul(
                                PS[tl][:, 512:1024],
                                actT[:, kc * 512 + tl * 128: kc * 512 + (tl + 1) * 128],
                                QP1[:, sl * 8192 + k8 * 1024 + 512: sl * 8192 + (k8 + 1) * 1024],
                                start=(kc == 0), stop=last),
                              reads=[b_actT, b_wls[sl]],
                              writes=[b_ps[tl]] if last else [])
                for tl in range(4):
                    tt = 4 * h + tl
                    E("scalar", lambda e, tl=tl: e.copy(fbB, PS[tl][:, :]),
                      reads=[b_ps[tl]], writes=[b_gpool], waits=tuple(toks(b_gel[1]) + toks(b_gel[0])))
                    E("scalar", lambda e, tt=tt, dh=dh: e.dma_start(
                        out=XNT[:, tt * 2048 + dh * 1024: tt * 2048 + (dh + 1) * 1024].bitcast(U16),
                        in_=fbB.bitcast(U16)),
                      reads=[b_gpool], writes=[b_XNTh[tt // 4]], dma="auto")

        rh_toks = tuple(toks(b_rh[0]) + toks(b_rh[1]))
        first_p4_ring = [True]
        for db in range(16):
            E("gpsimd", lambda e, db=db: e.dma_start(out=woutsl[:, :], in_=wout_st[db, :, :]),
              writes=[b_wos], dma="auto")
            ps, bp = PS[db % 2], b_ps[db % 2]
            for kc in range(16):
                for nh in range(2):
                    fin = (kc == 15 and nh == 1)
                    E("tensor", lambda e, kc=kc, nh=nh, ps=ps: e.matmul(
                        ps[:, nh * 512:(nh + 1) * 512],
                        woutsl[:, kc * 128:(kc + 1) * 128],
                        QTB[:, kc * 1024 + nh * 512: kc * 1024 + (nh + 1) * 512],
                        start=(kc == 0), stop=(kc == 15)),
                      reads=[b_wos, b_QTB],
                      writes=[bp] if (kc == 0 and nh == 0) or fin else [])
            E("scalar", lambda e, db=db, ps=ps: e.activation(fbB, ps[:, :], AF.Identity,
                                                             bias=boutf[:, db:db + 1]),
              reads=[bp, b_boutf], writes=[b_gpool])
            E("vector", lambda e: e.tensor_copy(ring[:, 0:1024], fbB), reads=[b_gpool], writes=[b_ring],
              waits=(rh_toks if first_p4_ring[0] else ()))
            first_p4_ring[0] = False
            pst, bpt = PS[2 + (db % 2)], b_ps[2 + (db % 2)]
            for i in range(8):
                E("tensor", lambda e, i=i, pst=pst: e.transpose(
                    pst[:, i * 128:(i + 1) * 128], ring[:, i * 128:(i + 1) * 128], identf[:, :]),
                  reads=[b_ring, b_identf], writes=[bpt])
            E("scalar", lambda e, pst=pst: e.copy(stg[:, :], pst[:, :]),
              reads=[bpt, b_slY], writes=[b_stg])
            E("scalar", lambda e, db=db: e.dma_start(
                out=proj_tm.reshape([8, 128, 2048]).ap()[:, :, db * 128:(db + 1) * 128].transpose([1, 0, 2]),
                in_=stg.reshape([128, 8, 128]).ap()[:, :, :]),
              reads=[b_stg], writes=[b_projtm], dma="auto")

        projslot = gpool[:, 0:2048]
        gel0_toks = tuple(toks(b_gel[0]) + toks(b_gel[1]))
        for tt in range(NTT):
            E("sync", lambda e, tt=tt: e.dma_start(out=xslot[:, :], in_=x[tt * 128:(tt + 1) * 128, :]),
              writes=[b_x], dma="auto")
            E("sync", lambda e, tt=tt: e.dma_start(out=projslot,
                                                   in_=proj_tm.ap()[tt * 128:(tt + 1) * 128, :]),
              reads=[b_projtm], writes=[b_gpool], dma="auto",
              waits=(gel0_toks if tt == 0 else ()))
            E("vector", lambda e, tt=tt: e.tensor_scalar(sc1[:, :], xslot[:, :], mS(tt), rS(tt),
                                                         OP.subtract, OP.mult),
              reads=[b_x, b_stats], writes=[b_sc1])
            E("vector", lambda e: e.tensor_tensor(ring[:, :], sc1[:, :], xslot[:, :], OP.add),
              reads=[b_sc1, b_x], writes=[b_ring])
            E("vector", lambda e: e.tensor_tensor(sc1[:, :], ring[:, :], projslot, OP.add),
              reads=[b_ring, b_gpool], writes=[b_sc1])
            E("vector", lambda e, tt=tt: e.tensor_tensor(ring[:, :], sc1[:, :],
                                                         XNT[:, tt * 2048:(tt + 1) * 2048], OP.add),
              reads=[b_sc1, b_XNTh[tt // 4]], writes=[b_ring])
            E("sync", lambda e, tt=tt: e.dma_start(out=y[tt * 128:(tt + 1) * 128, :], in_=ring[:, :]),
              reads=[b_ring], dma="o")
        so = sch.dma_sem("o")
        sch.wait_only("gpsimd", [(so, so.count)])

        sems = sch.all_sems()
        with contextlib.ExitStack() as es2:
            for s in sems:
                s.h = es2.enter_context(nc.semaphore(s.name))
            with nc.Block() as block:
                @block.sync
                def _(e):
                    sch.run("sync", e)

                @block.vector
                def _(e):
                    sch.run("vector", e)

                @block.scalar
                def _(e):
                    sch.run("scalar", e)

                @block.tensor
                def _(e):
                    sch.run("tensor", e)

                @block.gpsimd
                def _(e):
                    sch.run("gpsimd", e)
    return nc


_NC_CACHE = [None]


def kernel(L0_attn_X, L0_attn_W_in, L0_attn_b_in, L0_attn_W_out, L0_attn_b_out,
           L0_attn_gamma, L0_attn_beta, L0_ffn_w_gating, L0_ffn_w_linear):
    X = np.ascontiguousarray(np.asarray(L0_attn_X, dtype=np.float32))
    W_in = np.asarray(L0_attn_W_in, dtype=np.float32)
    b_in = np.asarray(L0_attn_b_in, dtype=np.float32)
    W_out = np.asarray(L0_attn_W_out, dtype=np.float32)
    b_out = np.asarray(L0_attn_b_out, dtype=np.float32)
    gamma = np.asarray(L0_attn_gamma, dtype=np.float32)
    beta = np.asarray(L0_attn_beta, dtype=np.float32)
    w_g = np.asarray(L0_ffn_w_gating, dtype=np.float32)
    w_l = np.asarray(L0_ffn_w_linear, dtype=np.float32)

    cols = []
    for kc in range(16):
        cols += list(range(4096 + kc * 128, 4096 + (kc + 1) * 128))
        cols += list(range(2048 + kc * 128, 2048 + (kc + 1) * 128))
    cols += list(range(0, 2048))
    w_in_st = np.ascontiguousarray(
        W_in[:, cols].reshape(16, 128, 24, 256).transpose(2, 1, 0, 3).reshape(24, 128, 4096))
    a0 = w_g[0].reshape(16, 128, 16, 512)
    a1 = w_g[1].reshape(16, 128, 16, 512)
    wg_st = np.ascontiguousarray(
        np.concatenate([a0, a1], axis=3).transpose(2, 1, 0, 3).reshape(16, 128, 16384))
    wl_st = np.ascontiguousarray(
        w_l.reshape(8, 8, 128, 2, 1024).transpose(3, 0, 2, 1, 4).reshape(2, 8, 128, 8192))
    wout_st = np.ascontiguousarray(
        W_out.reshape(16, 128, 16, 128).transpose(2, 1, 0, 3).reshape(16, 128, 2048))
    b_in_fm = np.ascontiguousarray(b_in.reshape(48, 128).T)
    b_out_fm = np.ascontiguousarray(b_out.reshape(16, 128).T)
    gam_rep = np.ascontiguousarray(np.broadcast_to(gamma, (128, D)))
    bet_rep = np.ascontiguousarray(np.broadcast_to(beta, (128, D)))
    ident = np.eye(128, dtype=np.float32)

    shared = dict(w_in_st=w_in_st, wg_st=wg_st, wl_st=wl_st, wout_st=wout_st,
                  b_in_fm=b_in_fm, b_out_fm=b_out_fm, gam_rep=gam_rep,
                  bet_rep=bet_rep, ident=ident)
    in_maps = []
    for c in range(NCORE):
        b = c // 4
        r0 = (c % 4) * TOK
        cos_fm, sin_fm = _rotary_tables(c)
        m = dict(shared)
        m["x"] = np.ascontiguousarray(X[b, r0:r0 + TOK, :])
        m["tabs_st"] = np.ascontiguousarray(np.concatenate([cos_fm, sin_fm], axis=2))
        in_maps.append(m)

    if _NC_CACHE[0] is None:
        _NC_CACHE[0] = build_nc()
    nc = _NC_CACHE[0]
    res = run_bass_kernel_spmd(nc, in_maps, core_ids=list(range(NCORE)))
    out = np.empty((B, S, D), dtype=np.float32)
    for c in range(NCORE):
        b = c // 4
        r0 = (c % 4) * TOK
        out[b, r0:r0 + TOK, :] = res.results[c]["y"]
    return out
